# version 14
# baseline (speedup 1.0000x reference)
import os
import math
import numpy as np

import concourse.bass as bass
import concourse.mybir as mybir
from concourse.bass_utils import run_bass_kernel_spmd

F32 = mybir.dt.float32
BF16 = mybir.dt.bfloat16
AF = mybir.ActivationFunctionType
ALU = mybir.AluOpType
AX = mybir.AxisListType

NCORE = 8
S = 8192
D = 2048
T = 1024
HD = 128
NH = 16
KVH = 4
NE = 8
NI = 1408
NIC = 11
DC = 16
TCH = 16
SCALE = 1.0 / math.sqrt(HD)
LN_EPS = 1e-5
RMS_EPS = 1e-6
ROPE_THETA = 1.0e6

TWO_PI = float(2.0 * np.pi)
INV_2PI = float(1.0 / TWO_PI)
MAGIC = float(1.5 * 2.0 ** 23)
CW1 = float(np.float32(6.28125))
CW2 = float(np.float32(1.9353628e-03))
CW3 = float(TWO_PI - np.float64(np.float32(CW1)) - np.float64(np.float32(CW2)))

DBG = os.environ.get("KDBG", "")


class Pl:
    ENGS = ("sync", "gpsimd", "vector", "scalar", "tensor")

    def __init__(self, nc):
        self.nc = nc
        self.ops = {e: [] for e in self.ENGS}
        self.meta = {e: [] for e in self.ENGS}
        self.sems = {}
        self.cnt = {}
        self.lastw = {e: {} for e in self.ENGS}

    def s(self, name):
        if name not in self.sems:
            self.sems[name] = self.nc.alloc_semaphore(name)
            self.cnt[name] = 0
        return self.sems[name]

    def wait(self, eng, ev):
        if ev is None:
            return
        if isinstance(ev, (list, tuple)) and (len(ev) != 2 or not isinstance(ev[0], str)):
            for e2 in ev:
                self.wait(eng, e2)
            return
        name, val = ev
        if self.lastw[eng].get(name, -1) >= val:
            return
        self.lastw[eng][name] = val
        h = self.s(name)
        self.meta[eng].append(("w", name, val))
        self.ops[eng].append(lambda e, h=h, v=val: e.wait_ge(h, v))

    def emit(self, eng, fn, waits=(), inc=None, n=1):
        for w in waits:
            self.wait(eng, w)
        if inc is None:
            self.ops[eng].append(fn)
            self.meta[eng].append(("i", None, 0))
            return None
        h = self.s(inc)
        self.cnt[inc] += n
        self.ops[eng].append(lambda e, fn=fn, h=h, n=n: fn(e).then_inc(h, n))
        self.meta[eng].append(("i", inc, n))
        return (inc, self.cnt[inc])

    def pe(self, fn, waits=(), inc="pe"):
        return self.emit("tensor", fn, waits, inc)

    def ve(self, fn, waits=(), inc="dve"):
        return self.emit("vector", fn, waits, inc)

    def sc(self, fn, waits=(), inc="act"):
        return self.emit("scalar", fn, waits, inc)

    def gp(self, fn, waits=(), inc="pool"):
        return self.emit("gpsimd", fn, waits, inc)

    def dma(self, eng, fn, waits=(), inc="ld"):
        return self.emit(eng, fn, waits, inc, n=16)

    def check_deadlock(self):
        pos = {e: 0 for e in self.ENGS}
        val = {name: 0 for name in self.cnt}
        progress = True
        while progress:
            progress = False
            for e in self.ENGS:
                while pos[e] < len(self.meta[e]):
                    kind, name, v = self.meta[e][pos[e]]
                    if kind == "w":
                        if val.get(name, 0) < v:
                            break
                    else:
                        if name is not None:
                            val[name] = val.get(name, 0) + v
                    pos[e] += 1
                    progress = True
        stuck = {e: (pos[e], len(self.meta[e])) for e in self.ENGS
                 if pos[e] < len(self.meta[e])}
        if stuck:
            detail = {}
            for e in stuck:
                kind, name, v = self.meta[e][pos[e]]
                detail[e] = (pos[e], "waits", name, v, "cur", val.get(name, 0))
            raise AssertionError(f"DEADLOCK in plan: {detail}")

    def coll(self, fn, waits=()):
        for w in waits:
            self.wait("gpsimd", w)
        h = self.s("cc")
        self.cnt["cc"] += 1
        self.ops["gpsimd"].append(lambda e, fn=fn, h=h: fn(e).then_inc(h))
        self.meta["gpsimd"].append(("i", "cc", 1))
        return ("cc", self.cnt["cc"])


def _build():
    nc = bass.Bass("TRN2", target_bir_lowering=False)

    def din(name, shape):
        return nc.dram_tensor(name, shape, F32, kind="ExternalInput").ap()

    x_sh = din("x_sh", [128, 2 * DC * 512])
    wq_my = din("wq_my", [2, D, HD])
    wk_my = din("wk_my", [2, D, HD])
    wv_my = din("wv_my", [2, D, HD])
    b3_my = din("b3_my", [HD, 8])
    wo_my = din("wo_my", [2, DC, HD, HD])
    bo_t = din("bo_t", [HD, DC])
    q2_my = din("q2_my", [2, D, HD])
    k2_my = din("k2_my", [1, D, HD])
    v2_my = din("v2_my", [1, D, HD])
    o2_my = din("o2_my", [2, DC, HD, HD])
    qknw = din("qknw", [HD, 2])
    anw_t = din("anw_t", [HD, DC])
    fnw_t = din("fnw_t", [HD, DC])
    gw_t = din("gw_t", [D, NE])
    wg_p = din("wg_p", [NE, NIC, D, HD])
    wu_p = din("wu_p", [NE, NIC, D, HD])
    wd_p = din("wd_p", [NE, DC, NI, HD])

    out_t = nc.dram_tensor("out_t", [D, T], F32, kind="ExternalOutput").ap()
    dbg_out = {}
    if DBG:
        for nm in ("d_x1", "d_h1", "d_F"):
            dbg_out[nm] = nc.dram_tensor(nm, [D, T], F32, kind="ExternalOutput").ap()
        dbg_out["d_comb"] = nc.dram_tensor("d_comb", [128, 64], F32,
                                           kind="ExternalOutput").ap()
    if DBG == "2":
        dbg_out["d_xg1"] = nc.dram_tensor("d_xg1", [NCORE * 128, 16384],
                                          BF16, kind="ExternalOutput").ap()
        for nm in ("d_q", "d_k", "d_v"):
            dbg_out[nm] = nc.dram_tensor(nm, [128, 8192], BF16,
                                         kind="ExternalOutput").ap()
        dbg_out["d_ao"] = nc.dram_tensor("d_ao", [128, 16384], BF16,
                                         kind="ExternalOutput").ap()
        dbg_out["d_rs1o"] = nc.dram_tensor("d_rs1o", [D, T], BF16,
                                           kind="ExternalOutput").ap()
        dbg_out["d_m1"] = nc.dram_tensor("d_m1", [128, 16384], F32,
                                         kind="ExternalOutput").ap()
        dbg_out["d_x1b"] = nc.dram_tensor("d_x1b", [128, 16384], F32,
                                          kind="ExternalOutput").ap()
        for nm in ("d_q2", "d_k2", "d_v2"):
            dbg_out[nm] = nc.dram_tensor(nm, [128, 8192], BF16,
                                         kind="ExternalOutput").ap()
        dbg_out["d_ao2"] = nc.dram_tensor("d_ao2", [128, 16384], BF16,
                                          kind="ExternalOutput").ap()
        dbg_out["d_rs2o"] = nc.dram_tensor("d_rs2o", [D, T], BF16,
                                           kind="ExternalOutput").ap()
        dbg_out["d_fb"] = nc.dram_tensor("d_fb", [128, 16384], F32,
                                         kind="ExternalOutput").ap()
        dbg_out["d_hb"] = nc.dram_tensor("d_hb", [128, 16384], BF16,
                                         kind="ExternalOutput").ap()
        dbg_out["d_lnbc"] = nc.dram_tensor("d_lnbc", [128, 2048], F32,
                                           kind="ExternalOutput").ap()
        dbg_out["d_rows"] = nc.dram_tensor("d_rows", [128, 1024], F32,
                                           kind="ExternalOutput").ap()
        dbg_out["d_ps"] = nc.dram_tensor("d_ps", [4, 2048], F32,
                                         kind="ExternalOutput").ap()
        dbg_out["d_trig"] = nc.dram_tensor("d_trig", [128, 1024], F32,
                                           kind="ExternalOutput").ap()

    xg1_loc = nc.dram_tensor("xg1_loc", [128, 16384], BF16, kind="Internal")
    xg1 = nc.dram_tensor("xg1", [NCORE * 128, 16384], BF16, kind="Internal",
                         addr_space="Shared")
    xg2_loc = nc.dram_tensor("xg2_loc", [128, 16384], BF16, kind="Internal")
    xg2 = nc.dram_tensor("xg2", [NCORE * 128, 16384], BF16, kind="Internal",
                         addr_space="Shared")
    rs1_loc = nc.dram_tensor("rs1_loc", [NCORE * D, T], BF16, kind="Internal")
    rs1_out = nc.dram_tensor("rs1_out", [D, T], BF16, kind="Internal")
    rs2_loc = nc.dram_tensor("rs2_loc", [NCORE * D, T], BF16, kind="Internal")
    rs2_out = nc.dram_tensor("rs2_out", [D, T], BF16, kind="Internal")
    x1_dram = nc.dram_tensor("x1_dram", [D, T], F32, kind="Internal")
    h1_dram = nc.dram_tensor("h1_dram", [D, T], F32, kind="Internal")

    def sb(name, shape, dt):
        return nc.alloc_sbuf_tensor(name, shape, dt)

    rbuf = sb("rbuf", [128, 16 * 1024], F32)
    hbuf = sb("hbuf", [128, 16 * 1024], BF16)
    qbuf = sb("qbuf", [128, 8192], BF16)
    kbuf = sb("kbuf", [128, 8192], BF16)
    vbuf = sb("vbuf", [128, 8192], BF16)
    aobuf = sb("aobuf", [128, 16 * 1024], BF16)
    pbuf = sb("pbuf", [128, 8 * 512], BF16)
    oslab = sb("oslab", [128, 32 * 128], BF16)
    ones_f = sb("ones_f", [128, 128], F32)
    ident_f = sb("ident_f", [128, 128], F32)
    ones_b = sb("ones_b", [128, 8], BF16)
    freqc = sb("freqc", [128, 4], F32)
    cols = sb("cols", [128, 64], F32)
    bias3 = sb("bias3", [128, 8], F32)
    bo_sb = sb("bo_sb", [128, DC], F32)
    anw_sb = sb("anw_sb", [128, DC], F32)
    fnw_sb = sb("fnw_sb", [128, DC], F32)
    qknw_sb = sb("qknw_sb", [128, 2], F32)
    gwt_sb = sb("gwt_sb", [128, 128], F32)
    whi_sb = sb("whi_sb", [128, 128], BF16)
    wlo_sb = sb("wlo_sb", [128, 128], BF16)
    comb_all = sb("comb_all", [128, 64], F32)
    rows = sb("rows", [128, 1024], F32)
    wgsl = sb("wgsl", [128, 2 * 2048], BF16)

    rem = nc.sbuf_bytes_remaining
    assert rem >= 0, f"sbuf over budget by {-rem}"

    def addr(h):
        return nc.lookup_mloc(h).addr

    def at(name, base, off, shape, dt):
        return nc.alloc_sbuf_tensor_at(name, shape, dt, offset=base + off)

    hb0 = addr(hbuf)
    pn = at("pn", hb0, 0, [128, 512], F32)
    qn = at("qn", hb0, 2048, [128, 512], F32)
    rb = at("rb", hb0, 4096, [128, 512], F32)
    sqt = at("sqt", hb0, 6144, [128, 512], BF16)
    tbv = at("tbv", hb0, 7168, [128, 512], F32)
    fv = at("fv", hb0, 9216, [128, 512], F32)
    auxv = at("auxv", hb0, 11264, [128, 512], F32)
    sinv = at("sinv", hb0, 13312, [128, 512], F32)
    cosv = at("cosv", hb0, 15360, [128, 512], F32)
    stv = at("stv", hb0, 17408, [128, 3 * 512], BF16)
    swp = at("swp", hb0, 17408, [128, 512], F32)
    lnbc = at("lnbc", hb0, 20480, [128, 2 * 1024], F32)
    hscr = at("hscr", hb0, 28672, [128, 2 * 1024], BF16)
    pnB = at("pnB", hb0, 20480, [128, 512], F32)
    qnB = at("qnB", hb0, 22528, [128, 512], F32)
    auxvB = at("auxvB", hb0, 24576, [128, 512], F32)
    tbvB = at("tbvB", hb0, 26624, [128, 512], F32)
    sqtB = at("sqtB", hb0, 19456, [128, 512], BF16)

    qb0, kb0, vb0, pb0, ob0, rb0 = (addr(qbuf), addr(kbuf), addr(vbuf),
                                    addr(pbuf), addr(oslab), addr(rbuf))
    xst = at("xst", qb0, 0, [128, 2 * 1024], F32)
    rsst = at("rsst", kb0, 0, [128, 2 * 1024], BF16)
    hilo = at("hilo", kb0, 4096, [128, 2 * 1024], BF16)
    h1ev = at("h1ev", vb0, 0, [128, 2 * 1024], F32)
    hsqt = at("hsqt", vb0, 8192, [128, 1024], BF16)
    rstbc = at("rstbc", vb0, 10240, [128, 1024], F32)
    mescr = at("mescr", kb0, 0, [128, 512], F32)
    silscr = at("silscr", kb0, 2048, [128, 512], BF16)
    cbc = at("cbc", vb0, 0, [128, 1024], F32)
    wusl = at("wusl", pb0, 0, [128, 2 * 2048], BF16)
    wdsl = at("wdsl", ob0, 0, [128, 2 * 1408], BF16)
    xsb = at("xsb", rb0, 0, [128, 2 * 8192], BF16)
    combT = at("combT", addr(aobuf), 24576, [8, 1024], F32)
    aslab = at("aslab", rb0, 32768, [128, 6 * 2048], BF16)
    sinvB = at("sinvB", rb0, 57344, [128, 512], F32)
    cosvB = at("cosvB", rb0, 59392, [128, 512], F32)
    swpB = at("swpB", rb0, 61440, [128, 512], F32)

    ps = [nc.alloc_psum_tensor(f"ps{i}", [128, 512], F32) for i in range(8)]

    P = Pl(nc)
    st = {}

    def dbg_snap(pairs):
        allw = [(name, P.cnt[name]) for name in list(P.cnt)]
        ev = None
        for (o, i_) in pairs:
            ev = P.dma("sync", lambda e, o=o, i_=i_: e.dma_start(out=o, in_=i_),
                       waits=allw, inc="dbg")
        for eng in Pl.ENGS:
            P.wait(eng, ev)

    def dump_banks(row, bounce):
        allw = [(name, P.cnt[name]) for name in list(P.cnt)]
        prev = None
        for b in range(8):
            prev = P.ve(lambda e, b=b: e.tensor_copy(
                bounce[0:1, 512 * (b % 4):512 * (b % 4 + 1)],
                ps[b][0:1, 0:512]), waits=allw + [prev])
            if b % 4 == 3:
                prev = P.dma("sync", lambda e, b=b, row=row: e.dma_start(
                    out=dbg_out["d_ps"][row + b // 4:row + b // 4 + 1, :],
                    in_=bounce[0:1, 0:2048]), waits=[prev], inc="dbg")
        for eng in Pl.ENGS:
            P.wait(eng, prev)

    ev_b3 = P.dma("sync", lambda e: e.dma_start(out=bias3[:, 0:6], in_=b3_my[:, 0:6]))
    ev_bo = P.dma("sync", lambda e: e.dma_start(out=bo_sb[:, :], in_=bo_t[:, :]),
                  waits=[ev_b3])
    ev_anw = P.dma("sync", lambda e: e.dma_start(out=anw_sb[:, :], in_=anw_t[:, :]),
                   waits=[ev_bo])
    ev_fnw = P.dma("sync", lambda e: e.dma_start(out=fnw_sb[:, :], in_=fnw_t[:, :]),
                   waits=[ev_anw])
    ev_qknw = P.dma("sync", lambda e: e.dma_start(out=qknw_sb[:, :], in_=qknw[:, :]),
                    waits=[ev_fnw])
    ev_gw = P.dma("sync", lambda e: e.dma_start(
        out=gwt_sb[:, :].rearrange("p (c f) -> p c f", c=DC),
        in_=gw_t[:, :].rearrange("(c p) f -> p c f", p=128)), waits=[ev_qknw])
    ev_one = P.gp(lambda e: e.memset(ones_f[:, :], 1.0))
    ev_oneb = P.gp(lambda e: e.memset(ones_b[:, :], 1.0))
    ev_id0 = P.gp(lambda e: e.memset(ident_f[:, :], 1.0), waits=[ev_one])
    ev_idf = P.gp(lambda e: e.affine_select(
        out=ident_f[:, :], in_=ident_f[:, :], pattern=[[-1, 128]],
        compare_op=ALU.is_equal, fill=0.0, base=0, channel_multiplier=1),
        waits=[ev_id0])
    ev_j = P.gp(lambda e: e.iota(freqc[:, 1:2], pattern=[[1, 1]], base=0,
                                 channel_multiplier=1,
                                 allow_small_or_imprecise_dtypes=True))
    ev_j = P.ve(lambda e: e.tensor_scalar(
        freqc[:, 2:3], freqc[:, 1:2], 64.0, None, op0=ALU.is_ge), waits=[ev_j])
    ev_j = P.ve(lambda e: e.scalar_tensor_tensor(
        freqc[:, 1:2], freqc[:, 2:3], -64.0, freqc[:, 1:2],
        op0=ALU.mult, op1=ALU.add), waits=[ev_j])
    ev = P.ve(lambda e: e.memset(freqc[:, 0:1], 1.0), waits=[ev_j])
    for i in range(5, -1, -1):
        r_i = float(np.float32(1.0 / (ROPE_THETA ** ((2 ** i) / 64.0))))
        ev = P.ve(lambda e, i=i: e.tensor_scalar(
            freqc[:, 2:3], freqc[:, 1:2], float(2 ** i), None, op0=ALU.is_ge),
            waits=[ev])
        ev = P.ve(lambda e, i=i: e.scalar_tensor_tensor(
            freqc[:, 1:2], freqc[:, 2:3], float(-(2 ** i)), freqc[:, 1:2],
            op0=ALU.mult, op1=ALU.add), waits=[ev])
        ev = P.ve(lambda e, r_i=r_i: e.tensor_scalar(
            freqc[:, 2:3], freqc[:, 2:3], float(r_i - 1.0), 1.0,
            op0=ALU.mult, op1=ALU.add), waits=[ev])
        ev = P.ve(lambda e: e.tensor_tensor(
            freqc[:, 0:1], freqc[:, 0:1], freqc[:, 2:3], op=ALU.mult), waits=[ev])
    ev = P.ve(lambda e: e.tensor_scalar(
        freqc[0:64, 0:1], freqc[0:64, 0:1], -1.0, None, op0=ALU.mult), waits=[ev])
    ev_freq = ev

    ev_x0st = P.dma("gpsimd", lambda e: e.dma_start(
        out=xg1_loc.ap(), in_=x_sh[:, :]), inc="xld")
    ev_cc1 = P.coll(lambda e: e.collective_compute(
        "AllGather", ALU.bypass, replica_groups=[list(range(NCORE))],
        ins=[xg1_loc.ap().opt()], outs=[xg1.ap().opt()]), waits=[ev_x0st])

    slab_src1 = [wq_my, wq_my, wk_my, wk_my, wv_my, wv_my]
    ev_wa = None
    for si in range(6):
        ev_wa = P.dma("gpsimd", lambda e, si=si, src=slab_src1[si], hs=si % 2:
                      e.dma_start(
            out=aslab[:, 2048 * si:2048 * (si + 1)].rearrange("p (d f) -> p d f", d=DC),
            in_=src[hs].rearrange("(d p) f -> p d f", p=128)), inc="wld")

    st["xs_war"] = [None, None]
    st["pslot_war"] = [None] * 8

    def xstream(gx, c, cc_ev):
        par = c % 2
        r, h = c // 2, c % 2
        src = gx.ap()[128 * r:128 * (r + 1), 8192 * h:8192 * (h + 1)]
        eng = "sync" if par == 0 else "scalar"
        ev = P.dma(eng, lambda e, par=par, src=src: e.dma_start(
            out=xsb[:, 8192 * par:8192 * (par + 1)], in_=src),
            waits=[st["xs_war"][par], cc_ev, st.get("ln_done")],
            inc=f"xs{par}")
        return ev, par

    def xslice(par, d):
        return xsb[:, 8192 * par + 512 * d: 8192 * par + 512 * (d + 1)]

    def rope_tables(c):
        pp = c % 2
        sv = sinv if pp == 0 else sinvB
        cv = cosv if pp == 0 else cosvB
        av_ = auxv if pp == 0 else auxvB
        tb = tbv if pp == 0 else tbvB
        ev = P.gp(lambda e, c=c, tb=tb: e.iota(
            tb[:, :], pattern=[[1, 512]], base=512 * c,
            channel_multiplier=0,
            allow_small_or_imprecise_dtypes=True),
            waits=[st.get(f"rope_war{pp}"), st.get(f"tbv_war{pp}"),
                   st.get("ln_done")])
        ev = P.ve(lambda e, tb=tb: e.tensor_scalar(
            fv[:, :], tb[:, :], freqc[:, 0:1], None, op0=ALU.mult),
            waits=[ev, ev_freq, st.get("fv_war")])
        ev = P.ve(lambda e, av_=av_: e.tensor_scalar(
            av_[:, :], fv[:, :], INV_2PI, MAGIC, op0=ALU.mult, op1=ALU.add),
            waits=[ev, st.get(f"aux_war{pp}")])
        ev = P.ve(lambda e, av_=av_: e.tensor_scalar(
            av_[:, :], av_[:, :], MAGIC, None, op0=ALU.subtract), waits=[ev])
        first = True
        for cw in (CW1, CW2, CW3):
            src = fv if first else sv
            ev = P.ve(lambda e, cw=cw, src=src, sv=sv, av_=av_:
                      e.scalar_tensor_tensor(
                sv[:, :], av_[:, :], -cw, src[:, :], op0=ALU.mult, op1=ALU.add),
                waits=[ev, st.get(f"sin_war{pp}") if first else None])
            if first:
                st["fv_war"] = ev
            first = False
        ev = P.ve(lambda e, sv=sv: e.tensor_scalar(
            sv[:, :], sv[:, :], float(np.pi), float(-np.pi),
            op0=ALU.min, op1=ALU.max), waits=[ev])
        ev_r = ev
        ev = P.ve(lambda e, sv=sv, cv=cv: e.tensor_scalar(
            cv[:, :], sv[:, :], float(np.pi / 2), float(np.pi),
            op0=ALU.add, op1=ALU.is_gt),
            waits=[ev, st.get(f"cos_war{pp}")])
        ev = P.ve(lambda e, sv=sv, cv=cv: e.scalar_tensor_tensor(
            cv[:, :], cv[:, :], -TWO_PI, sv[:, :], op0=ALU.mult, op1=ALU.add),
            waits=[ev])
        ev = P.ve(lambda e, cv=cv: e.tensor_scalar(
            cv[:, :], cv[:, :], float(np.pi / 2), None, op0=ALU.add),
            waits=[ev])
        ev = P.ve(lambda e, cv=cv: e.tensor_scalar(
            cv[:, :], cv[:, :], float(np.pi), float(-np.pi),
            op0=ALU.min, op1=ALU.max), waits=[ev])
        ev_cos = P.sc(lambda e, cv=cv: e.activation(cv[:, :], cv[:, :], AF.Sin),
                      waits=[ev])
        ev_sin = P.sc(lambda e, sv=sv: e.activation(sv[:, :], sv[:, :], AF.Sin),
                      waits=[ev_r, ev])
        return ev_sin, ev_cos

    def proj_pass(att, hs, gx, cc_ev, prev_done):
        if att == 1:
            parts = [("q", hs, qbuf), ("k", 2 + hs, kbuf), ("v", 4 + hs, vbuf)]
            slab_wait = ev_wa
        elif hs == 0:
            parts = [("q", 0, qbuf), ("k", 2, kbuf), ("v", 3, vbuf)]
            slab_wait = st["ev_w2"]
        else:
            parts = [("q", 1, qbuf)]
            slab_wait = st["ev_w2"]
        rows_off = {"q": 0, "k": 16}
        ssq_red = {"q": [None] * TCH, "k": [None] * TCH}
        pn_war = [st.get("pn_war0"), st.get("pn_war1")]
        qn_war = [st.get("qn_war0"), st.get("qn_war1")]
        sq_war = [st.get("sq_war0"), st.get("sq_war1")]
        for c in range(TCH):
            pp = c % 2
            pn_p = pn if pp == 0 else pnB
            qn_p = qn if pp == 0 else qnB
            sqt_p = sqt if pp == 0 else sqtB
            swp_p = swp if pp == 0 else swpB
            sv_p = sinv if pp == 0 else sinvB
            cv_p = cosv if pp == 0 else cosvB
            av_p = auxv if pp == 0 else auxvB
            tb_p = tbv if pp == 0 else tbvB
            rrow = rows[32:33, 0:512] if pp == 0 else rows[64:65, 0:512]
            orow = ones_f[32:33, 0:128] if pp == 0 else ones_f[64:65, 0:128]
            p6row = ps[6][0:1, 0:512] if pp == 0 else ps[6][32:33, 0:512]
            ev_x, par = xstream(gx, c, cc_ev)
            tab_evs = rope_tables(c) if att == 2 else None
            last_x_read = None
            for (pname, slab_i, buf) in parts:
                bank = ps[{"q": 0, "k": 2, "v": 4}[pname] + pp]
                bank_war = st.get(f"bk_{pname}{pp}")
                mm = None
                for d in range(DC):
                    mm = P.pe(lambda e, bank=bank, slab_i=slab_i, par=par, d=d:
                              e.matmul(bank[0:128, 0:512],
                                       aslab[:, 2048 * slab_i + 128 * d:
                                             2048 * slab_i + 128 * (d + 1)],
                                       xslice(par, d),
                                       start=(d == 0), stop=(d == DC - 1)),
                              waits=[ev_x, slab_wait, bank_war, prev_done]
                              if d == 0 else (),
                              inc="pe" if d == DC - 1 else None)
                last_x_read = mm
                if att == 1:
                    bofs = {"q": 0, "k": 2, "v": 4}[pname] + hs
                    if pname in ("q", "k"):
                        evv = P.ve(lambda e, bank=bank, buf=buf, c=c, bofs=bofs:
                                   e.tensor_scalar(buf[:, 512 * c:512 * (c + 1)],
                                                   bank[0:128, 0:512],
                                                   bias3[:, bofs:bofs + 1], None,
                                                   op0=ALU.add),
                                   waits=[mm, ev_b3])
                        st[f"bk_{pname}{pp}"] = evv
                        buf_ev = evv
                    else:
                        evv = P.ve(lambda e, bank=bank, pn_p=pn_p, bofs=bofs:
                                   e.tensor_scalar(pn_p[:, :], bank[0:128, 0:512],
                                                   bias3[:, bofs:bofs + 1], None,
                                                   op0=ALU.add),
                                   waits=[mm, ev_b3, pn_war[pp]])
                        trs = None
                        for sub in range(4):
                            trs = P.pe(lambda e, bank=bank, pn_p=pn_p, sub=sub:
                                       e.transpose(
                                bank[0:128, 128 * sub:128 * (sub + 1)],
                                pn_p[:, 128 * sub:128 * (sub + 1)], ident_f[:, :]),
                                waits=[evv, ev_idf] if sub == 0 else (),
                                inc="pe" if sub == 3 else None)
                        vcp = P.ve(lambda e, bank=bank, c=c: e.tensor_copy(
                            vbuf[:, 512 * c:512 * (c + 1)], bank[0:128, 0:512]),
                            waits=[trs])
                        st[f"bk_v{pp}"] = vcp
                        pn_war[pp] = trs
                        buf_ev = vcp
                else:
                    if pname in ("q", "k"):
                        dst = pn_p if pname == "q" else qn_p
                        dst2 = qn_p if pname == "q" else pn_p
                        dwar = pn_war[pp] if pname == "q" else qn_war[pp]
                        evv = P.ve(lambda e, bank=bank, dst=dst: e.tensor_copy(
                            dst[:, :], bank[0:128, 0:512]),
                            waits=[mm, dwar, st.get("ln_done")])
                        evs = P.ve(lambda e, dst=dst, sqt_p=sqt_p: e.tensor_mul(
                            sqt_p[:, :], dst[:, :], dst[:, :]),
                            waits=[evv, sq_war[pp], st.get("ln_done")])
                        ssq_mm = P.pe(lambda e, p6row=p6row, sqt_p=sqt_p:
                                      e.matmul(
                            p6row, ones_b[:, 0:1], sqt_p[:, :],
                            start=True, stop=True),
                            waits=[evs, ev_oneb, st.get(f"ps6r{pp}")])
                        sq_war[pp] = ssq_mm
                        evr = P.ve(lambda e, rrow=rrow, p6row=p6row:
                                   e.tensor_scalar(
                            rrow, p6row, 1.0 / HD,
                            RMS_EPS, op0=ALU.mult, op1=ALU.add),
                            waits=[ssq_mm, st.get(f"rows3264_war{pp}")])
                        st[f"ps6r{pp}"] = evr
                        evl = P.sc(lambda e, rrow=rrow: e.activation(
                            rrow, rrow, AF.Ln), waits=[evr])
                        evx = P.sc(lambda e, rrow=rrow: e.activation(
                            rrow, rrow, AF.Exp, scale=-0.5), waits=[evl])
                        evb = P.pe(lambda e, bank=bank, orow=orow, rrow=rrow:
                                   e.matmul(
                            bank[0:128, 0:512], orow, rrow,
                            start=True, stop=True),
                            waits=[evx, ev_one, evv])
                        st[f"rows3264_war{pp}"] = evb
                        evrb = P.ve(lambda e, bank=bank: e.tensor_copy(
                            rb[:, :], bank[0:128, 0:512]),
                            waits=[evb, st.get("rb_war")])
                        st[f"bk_{pname}{pp}"] = evrb
                        nwcol = qknw_sb[:, 0:1] if pname == "q" else qknw_sb[:, 1:2]
                        d2war = qn_war[pp] if pname == "q" else pn_war[pp]
                        evn = P.ve(lambda e, dst=dst, dst2=dst2, nwcol=nwcol:
                                   e.scalar_tensor_tensor(
                                       dst2[:, :], dst[:, :], nwcol, rb[:, :],
                                       op0=ALU.mult, op1=ALU.mult),
                                   waits=[evrb, ev_qknw, d2war])
                        st["rb_war"] = evn
                        evs_t, evc_t = tab_evs
                        s1 = P.gp(lambda e, dst2=dst2, swp_p=swp_p: e.tensor_copy(
                            swp_p[0:64, :], dst2[64:128, :]),
                            waits=[evn, st.get(f"swp_war{pp}"),
                                   st.get("ln_done")])
                        s2 = P.gp(lambda e, dst2=dst2, swp_p=swp_p: e.tensor_copy(
                            swp_p[64:128, :], dst2[0:64, :]), waits=[s1])
                        m1 = P.gp(lambda e, dst2=dst2, av_p=av_p, cv_p=cv_p:
                                  e.tensor_tensor(
                            av_p[:, :], dst2[:, :], cv_p[:, :], op=ALU.mult),
                            waits=[evn, evc_t, st.get(f"aux_war{pp}")])
                        m2 = P.gp(lambda e, swp_p=swp_p, tb_p=tb_p, sv_p=sv_p:
                                  e.tensor_tensor(
                            tb_p[:, :], swp_p[:, :], sv_p[:, :], op=ALU.mult),
                            waits=[s2, evs_t, st.get(f"tbv_war{pp}")])
                        o2 = P.gp(lambda e, buf=buf, c=c, av_p=av_p, tb_p=tb_p:
                                  e.tensor_tensor(
                            buf[:, 512 * c:512 * (c + 1)], av_p[:, :],
                            tb_p[:, :], op=ALU.add), waits=[m1, m2])
                        st[f"swp_war{pp}"] = m2
                        st[f"aux_war{pp}"] = o2
                        st[f"tbv_war{pp}"] = o2
                        st[f"rope_war{pp}"] = o2
                        st[f"sin_war{pp}"] = o2
                        st[f"cos_war{pp}"] = o2
                        if pname == "q":
                            qn_war[pp] = [qn_war[pp], o2]
                            pn_war[pp] = [pn_war[pp], o2]
                        else:
                            pn_war[pp] = [pn_war[pp], o2]
                            qn_war[pp] = [qn_war[pp], o2]
                        buf_ev = o2
                    else:
                        evv = P.ve(lambda e, bank=bank, pn_p=pn_p: e.tensor_copy(
                            pn_p[:, :], bank[0:128, 0:512]),
                            waits=[mm, pn_war[pp], st.get("ln_done")])
                        trs = None
                        for sub in range(4):
                            trs = P.pe(lambda e, bank=bank, pn_p=pn_p, sub=sub:
                                       e.transpose(
                                bank[0:128, 128 * sub:128 * (sub + 1)],
                                pn_p[:, 128 * sub:128 * (sub + 1)], ident_f[:, :]),
                                waits=[evv, ev_idf] if sub == 0 else (),
                                inc="pe" if sub == 3 else None)
                        vcp = P.ve(lambda e, bank=bank, c=c: e.tensor_copy(
                            vbuf[:, 512 * c:512 * (c + 1)], bank[0:128, 0:512]),
                            waits=[trs])
                        st[f"bk_v{pp}"] = vcp
                        pn_war[pp] = trs
                        buf_ev = vcp
                if pname in ("q", "k"):
                    roff = rows_off[pname]
                    evs2 = P.ve(lambda e, buf=buf, c=c, sqt_p=sqt_p: e.tensor_mul(
                        sqt_p[:, :], buf[:, 512 * c:512 * (c + 1)],
                        buf[:, 512 * c:512 * (c + 1)]), waits=[buf_ev, sq_war[pp]])
                    smm = P.pe(lambda e, p6row=p6row, sqt_p=sqt_p: e.matmul(
                        p6row, ones_b[:, 0:1], sqt_p[:, :],
                        start=True, stop=True),
                        waits=[evs2, ev_oneb, st.get(f"ps6r{pp}")])
                    sq_war[pp] = smm
                    red = P.ve(lambda e, roff=roff, c=c, p6row=p6row:
                               e.tensor_reduce(
                        rows[0:1, roff + c:roff + c + 1], p6row,
                        axis=AX.X, op=ALU.max), waits=[smm, st.get("rows0_war")])
                    st[f"ps6r{pp}"] = red
                    ssq_red[pname][c] = red
            st["xs_war"][par] = last_x_read
        st["pn_war0"], st["pn_war1"] = pn_war
        st["qn_war0"], st["qn_war1"] = qn_war
        st["sq_war0"], st["sq_war1"] = sq_war
        return ssq_red

    def proj_pass2(gx, cc_ev, prev_done):
        parts = ["q", "k", "v"]
        slabs = {"q": 0, "k": 2, "v": 3}
        slab_wait = st["ev_w2"]
        rows_off = {"q": 0, "k": 16, "q1": 34}
        ssq_red = {"q": [None] * TCH, "k": [None] * TCH, "q1": [None] * TCH}
        pn_war = [st.get("pn_war0"), st.get("pn_war1")]
        qn_war = [st.get("qn_war0"), st.get("qn_war1")]
        sq_war = [st.get("sq_war0"), st.get("sq_war1")]

        def bufs(c):
            pp = c % 2
            return {
                "pp": pp,
                "pn": pn if pp == 0 else pnB,
                "qn": qn if pp == 0 else qnB,
                "sqt": sqt if pp == 0 else sqtB,
                "swp": swp if pp == 0 else swpB,
                "sv": sinv if pp == 0 else sinvB,
                "cv": cosv if pp == 0 else cosvB,
                "av": auxv if pp == 0 else auxvB,
                "tb": tbv if pp == 0 else tbvB,
                "rrow": rows[32:33, 0:512] if pp == 0 else rows[64:65, 0:512],
                "orow": ones_f[32:33, 0:128] if pp == 0 else ones_f[64:65, 0:128],
                "p6": ps[6][0:1, 0:512] if pp == 0 else ps[6][32:33, 0:512],
            }

        def s1(c):
            B = bufs(c)
            pp = B["pp"]
            ev_x, par = xstream(gx, c, cc_ev)
            tab_evs = rope_tables(c)
            dat = {"B": B, "tab": tab_evs, "par": par, "c": c}
            last = None
            for pname in parts:
                bank = ps[{"q": 0, "k": 2, "v": 4}[pname] + pp]
                bank_war = st.get(f"bk_{pname}{pp}")
                mm = None
                for d in range(DC):
                    mm = P.pe(lambda e, bank=bank, si=slabs[pname], par=par, d=d:
                              e.matmul(bank[0:128, 0:512],
                                       aslab[:, 2048 * si + 128 * d:
                                             2048 * si + 128 * (d + 1)],
                                       xslice(par, d),
                                       start=(d == 0), stop=(d == DC - 1)),
                              waits=[ev_x, slab_wait, bank_war, prev_done]
                              if d == 0 else (),
                              inc="pe" if d == DC - 1 else None)
                dat[f"mm_{pname}"] = mm
                last = mm
            st["xs_war"][par] = last
            evv = P.ve(lambda e, B=B, pp=pp: e.tensor_copy(
                B["pn"][:, :], ps[0 + pp][0:128, 0:512]),
                waits=[dat["mm_q"], pn_war[pp], st.get("ln_done")])
            evs = P.ve(lambda e, B=B: e.tensor_mul(
                B["sqt"][:, :], B["pn"][:, :], B["pn"][:, :]),
                waits=[evv, sq_war[pp], st.get("ln_done")])
            ssq_mm = P.pe(lambda e, B=B: e.matmul(
                B["p6"], ones_b[:, 0:1], B["sqt"][:, :], start=True, stop=True),
                waits=[evs, ev_oneb, st.get(f"ps6r{pp}")])
            sq_war[pp] = ssq_mm
            dat["evv_q"] = evv
            dat["ssq_q"] = ssq_mm
            return dat

        def qk_tail(pname, dat, evv, ssq_mm, bank, relkey, outbuf, outoff):
            B = dat["B"]
            pp = B["pp"]
            c = dat["c"]
            kside = pname == "k"
            dst = B["qn"] if kside else B["pn"]
            dst2 = B["pn"] if kside else B["qn"]
            evr = P.ve(lambda e, B=B: e.tensor_scalar(
                B["rrow"], B["p6"], 1.0 / HD, RMS_EPS,
                op0=ALU.mult, op1=ALU.add),
                waits=[ssq_mm, st.get(f"rows3264_war{pp}")])
            st[f"ps6r{pp}"] = evr
            evl = P.sc(lambda e, B=B: e.activation(B["rrow"], B["rrow"], AF.Ln),
                       waits=[evr])
            evx = P.sc(lambda e, B=B: e.activation(B["rrow"], B["rrow"], AF.Exp,
                                                   scale=-0.5), waits=[evl])
            evb = P.pe(lambda e, bank=bank, B=B: e.matmul(
                bank[0:128, 0:512], B["orow"], B["rrow"],
                start=True, stop=True), waits=[evx, ev_one, evv])
            st[f"rows3264_war{pp}"] = evb
            evrb = P.ve(lambda e, bank=bank: e.tensor_copy(
                rb[:, :], bank[0:128, 0:512]), waits=[evb, st.get("rb_war")])
            st[relkey] = evrb
            nwcol = qknw_sb[:, 1:2] if kside else qknw_sb[:, 0:1]
            d2war = pn_war[pp] if kside else qn_war[pp]
            evn = P.ve(lambda e, dst=dst, dst2=dst2, nwcol=nwcol:
                       e.scalar_tensor_tensor(
                           dst2[:, :], dst[:, :], nwcol, rb[:, :],
                           op0=ALU.mult, op1=ALU.mult),
                       waits=[evrb, ev_qknw, d2war])
            st["rb_war"] = evn
            evs_t, evc_t = dat["tab"]
            s1_ = P.gp(lambda e, dst2=dst2, B=B: e.tensor_copy(
                B["swp"][0:64, :], dst2[64:128, :]),
                waits=[evn, st.get(f"swp_war{pp}"), st.get("ln_done")])
            s2_ = P.gp(lambda e, dst2=dst2, B=B: e.tensor_copy(
                B["swp"][64:128, :], dst2[0:64, :]), waits=[s1_])
            m1 = P.gp(lambda e, dst2=dst2, B=B: e.tensor_tensor(
                B["av"][:, :], dst2[:, :], B["cv"][:, :], op=ALU.mult),
                waits=[evn, evc_t, st.get(f"aux_war{pp}")])
            m2 = P.gp(lambda e, B=B: e.tensor_tensor(
                B["tb"][:, :], B["swp"][:, :], B["sv"][:, :], op=ALU.mult),
                waits=[s2_, evs_t, st.get(f"tbv_war{pp}")])
            o2 = P.gp(lambda e, c=c, B=B, outbuf=outbuf, outoff=outoff:
                      e.tensor_tensor(
                outbuf[:, outoff + 512 * c:outoff + 512 * (c + 1)],
                B["av"][:, :], B["tb"][:, :], op=ALU.add), waits=[m1, m2])
            st[f"swp_war{pp}"] = m2
            st[f"aux_war{pp}"] = o2
            st[f"tbv_war{pp}"] = o2
            st[f"rope_war{pp}"] = o2
            st[f"sin_war{pp}"] = o2
            st[f"cos_war{pp}"] = o2
            pn_war[pp] = [pn_war[pp], o2]
            qn_war[pp] = [qn_war[pp], o2]
            evs2 = P.ve(lambda e, outbuf=outbuf, outoff=outoff, c=c, B=B:
                        e.tensor_mul(
                B["sqt"][:, :], outbuf[:, outoff + 512 * c:outoff + 512 * (c + 1)],
                outbuf[:, outoff + 512 * c:outoff + 512 * (c + 1)]),
                waits=[o2, sq_war[pp]])
            smm = P.pe(lambda e, B=B: e.matmul(
                B["p6"], ones_b[:, 0:1], B["sqt"][:, :], start=True, stop=True),
                waits=[evs2, ev_oneb, st.get(f"ps6r{pp}")])
            sq_war[pp] = smm
            red = P.ve(lambda e, roff=rows_off[pname], c=c, B=B:
                       e.tensor_reduce(
                rows[0:1, roff + c:roff + c + 1], B["p6"],
                axis=AX.X, op=ALU.max), waits=[smm, st.get("rows0_war")])
            st[f"ps6r{pp}"] = red
            ssq_red[pname][c] = red

        def s2(dat):
            B = dat["B"]
            pp = B["pp"]
            c = dat["c"]
            par = dat["par"]
            qk_tail("q", dat, dat["evv_q"], dat["ssq_q"], ps[0 + pp],
                    f"bk_q{pp}", qbuf, 0)
            mmq1 = None
            for d in range(DC):
                mmq1 = P.pe(lambda e, par=par, d=d: e.matmul(
                    ps[7][0:128, 0:512],
                    aslab[:, 2048 * 1 + 128 * d:2048 * 1 + 128 * (d + 1)],
                    xslice(par, d),
                    start=(d == 0), stop=(d == DC - 1)),
                    waits=[st.get("ps7_war")] if d == 0 else (),
                    inc="pe" if d == DC - 1 else None)
            st["xs_war"][par] = mmq1
            evv1 = P.ve(lambda e, B=B: e.tensor_copy(
                B["pn"][:, :], ps[7][0:128, 0:512]),
                waits=[mmq1, pn_war[pp]])
            evs1 = P.ve(lambda e, B=B: e.tensor_mul(
                B["sqt"][:, :], B["pn"][:, :], B["pn"][:, :]),
                waits=[evv1, sq_war[pp]])
            ssq1 = P.pe(lambda e, B=B: e.matmul(
                B["p6"], ones_b[:, 0:1], B["sqt"][:, :], start=True, stop=True),
                waits=[evs1, ev_oneb, st.get(f"ps6r{pp}")])
            sq_war[pp] = ssq1
            qk_tail("q1", dat, evv1, ssq1, ps[7], "ps7_war", aobuf, 8192)
            evv = P.ve(lambda e, B=B, pp=pp: e.tensor_copy(
                B["qn"][:, :], ps[2 + pp][0:128, 0:512]),
                waits=[dat["mm_k"], qn_war[pp]])
            evs = P.ve(lambda e, B=B: e.tensor_mul(
                B["sqt"][:, :], B["qn"][:, :], B["qn"][:, :]),
                waits=[evv, sq_war[pp]])
            ssq_mm = P.pe(lambda e, B=B: e.matmul(
                B["p6"], ones_b[:, 0:1], B["sqt"][:, :],
                start=True, stop=True),
                waits=[evs, ev_oneb, st.get(f"ps6r{pp}")])
            sq_war[pp] = ssq_mm
            qk_tail("k", dat, evv, ssq_mm, ps[2 + pp], f"bk_k{pp}", kbuf, 0)
            bank = ps[4 + pp]
            evv_v = P.ve(lambda e, B=B, bank=bank: e.tensor_copy(
                B["pn"][:, :], bank[0:128, 0:512]),
                waits=[dat["mm_v"], pn_war[pp]])
            trs = None
            for sub in range(4):
                trs = P.pe(lambda e, bank=bank, B=B, sub=sub: e.transpose(
                    bank[0:128, 128 * sub:128 * (sub + 1)],
                    B["pn"][:, 128 * sub:128 * (sub + 1)], ident_f[:, :]),
                    waits=[evv_v, ev_idf] if sub == 0 else (),
                    inc="pe" if sub == 3 else None)
            vcp = P.ve(lambda e, bank=bank, c=c: e.tensor_copy(
                vbuf[:, 512 * c:512 * (c + 1)], bank[0:128, 0:512]),
                waits=[trs])
            st[f"bk_v{pp}"] = vcp
            pn_war[pp] = trs

        prev_dat = None
        for c in range(TCH):
            dat = s1(c)
            if prev_dat is not None:
                s2(prev_dat)
            prev_dat = dat
        s2(prev_dat)
        st["pn_war0"], st["pn_war1"] = pn_war
        st["qn_war0"], st["qn_war1"] = qn_war
        st["sq_war0"], st["sq_war1"] = sq_war
        return ssq_red

    def attn_pass(att, hs, ssq_red, qsrc=None, qoff=0, qrow=0):
        qs = qbuf if qsrc is None else qsrc
        ao_evs = []
        scwar = [st.get(f"sc_war{i}") for i in range(4)]
        av_war = [st.get("av_war0"), st.get("av_war1")]
        cs_rd_war = [st.get("cs_war0"), st.get("cs_war1")]
        negb_rd = st.get("negb_rd")
        rb_war = st.get("rb_war")
        sq_war = st.get("sq_war")
        last_exp = None
        for q in range(TCH):
            kw = [ssq_red["k"][c] for c in range(q + 1)]
            e1 = P.ve(lambda e, q=q: e.tensor_reduce(
                rows[0:1, 32:33], rows[0:1, 16:16 + q + 1], axis=AX.X,
                op=ALU.max), waits=kw + [st.get("rows64n_war")])
            e2 = P.ve(lambda e, q=q: e.tensor_tensor(
                rows[0:1, 33:34], rows[0:1, 32:33],
                rows[0:1, qrow + q:qrow + q + 1],
                op=ALU.add), waits=[e1, ssq_red["q"][q]])
            e3 = P.ve(lambda e: e.tensor_scalar(
                rows[0:1, 33:34], rows[0:1, 33:34], -SCALE / 2.0, None,
                op0=ALU.mult), waits=[e2])
            bcmm = P.pe(lambda e: e.matmul(ps[7][0:128, 0:1], ones_f[0:1, 0:128],
                                           rows[0:1, 33:34],
                                           start=True, stop=True),
                        waits=[e3, ev_one, st.get("ps7_war")])
            st["rows64n_war"] = bcmm
            nb = P.ve(lambda e: e.tensor_copy(cols[:, 0:1], ps[7][0:128, 0:1]),
                      waits=[bcmm, negb_rd])
            st["ps7_war"] = nb
            unit_pe = None
            for c in range(q + 1):
                exp_evs = [None] * 4
                pair_mm = None
                for ks in range(4):
                    mmw = [scwar[ks]]
                    if ks == 0 and c == 0:
                        mmw += [av_war[q % 2], cs_rd_war[q % 2], nb]
                    pair_mm = P.pe(lambda e, c=c, ks=ks, q=q: e.matmul(
                        ps[ks][0:128, 0:512],
                        kbuf[:, 512 * c + 128 * ks:512 * c + 128 * (ks + 1)],
                        qs[:, qoff + 512 * q:qoff + 512 * (q + 1)],
                        start=True, stop=True),
                        waits=mmw, inc="pe" if ks in (1, 3) else None)
                    if ks in (1, 3):
                        for k2 in (ks - 1, ks):
                            slot = (c * 4 + k2) % 8
                            ex = P.sc(lambda e, k2=k2, slot=slot: e.activation(
                                pbuf[:, 512 * slot:512 * (slot + 1)],
                                ps[k2][0:128, 0:512], AF.Exp,
                                bias=cols[:, 0:1], scale=SCALE),
                                waits=[pair_mm, nb, st["pslot_war"][slot]])
                            exp_evs[k2] = ex
                            last_exp = ex
                for ks in range(4):
                    scwar[ks] = exp_evs[ks]
                gate_evs = exp_evs
                if c == q:
                    gate_evs = []
                    for ks in range(4):
                        slot = (c * 4 + ks) % 8
                        pm = P.gp(lambda e, slot=slot, ks=ks: e.affine_select(
                            out=pbuf[:, 512 * slot:512 * (slot + 1)],
                            in_=pbuf[:, 512 * slot:512 * (slot + 1)],
                            pattern=[[1, 512]], compare_op=ALU.is_ge, fill=0.0,
                            base=-128 * ks, channel_multiplier=-1),
                            waits=[exp_evs[ks]])
                        gate_evs.append(pm)
                for ks in range(4):
                    slot = (c * 4 + ks) % 8
                    first = (c == 0 and ks == 0)
                    last = (c == q and ks == 3)
                    P.pe(lambda e, q=q, c=c, ks=ks, slot=slot, first=first,
                         last=last: e.matmul(
                        ps[4 + q % 2][0:128, 0:512],
                        vbuf[:, 128 * (4 * c + ks):128 * (4 * c + ks + 1)],
                        pbuf[:, 512 * slot:512 * (slot + 1)],
                        start=first, stop=last, skip_group_check=True),
                        waits=[gate_evs[ks]], inc=None)
                    unit_pe = P.pe(lambda e, q=q, slot=slot, first=first,
                                   last=last: e.matmul(
                        ps[6][32 * (q % 2):32 * (q % 2) + 1, 0:512],
                        ones_b[:, 0:1], pbuf[:, 512 * slot:512 * (slot + 1)],
                        start=first, stop=last, skip_group_check=True),
                        waits=[ev_oneb, st.get(f"ps6r{q % 2}")],
                        inc="pe" if ks == 3 else None)
                for ks in range(4):
                    st["pslot_war"][(c * 4 + ks) % 8] = unit_pe
            rcp = P.ve(lambda e, q=q: e.reciprocal(
                rows[0:1, 512:1024], ps[6][32 * (q % 2):32 * (q % 2) + 1, 0:512]),
                waits=[unit_pe, st.get("rows64r_war")])
            st[f"ps6r{q % 2}"] = rcp
            bb = P.pe(lambda e: e.matmul(ps[7][0:128, 0:512], ones_f[0:1, 0:128],
                                         rows[0:1, 512:1024], start=True, stop=True),
                      waits=[rcp, st.get("ps7_war")])
            st["rows64r_war"] = bb
            rbc = P.ve(lambda e: e.tensor_copy(rb[:, :], ps[7][0:128, 0:512]),
                       waits=[bb, rb_war])
            st["ps7_war"] = rbc
            negb_rd = last_exp
            ao = P.ve(lambda e, hs=hs, q=q: e.tensor_tensor(
                aobuf[:, 8192 * hs + 512 * q:8192 * hs + 512 * (q + 1)],
                ps[4 + q % 2][0:128, 0:512], rb[:, :], op=ALU.mult),
                waits=[rbc, unit_pe, st.get("ao_war")])
            rb_war = ao
            av_war[q % 2] = ao
            cs_rd_war[q % 2] = rcp
            ao_evs.append(ao)
        st["rb_war"] = rb_war
        st["negb_rd"] = negb_rd
        st["last_exp"] = last_exp
        st["sq_war"] = sq_war
        for i in range(4):
            st[f"sc_war{i}"] = scwar[i]
        st["av_war0"], st["av_war1"] = av_war
        st["cs_war0"], st["cs_war1"] = cs_rd_war
        st["bk_q0"] = st["bk_q1"] = st["bk_k0"] = st["bk_k1"] = last_exp
        st["bk_v0"] = ao_evs[14]
        st["bk_v1"] = ao_evs[15]
        st["ps6r0"] = cs_rd_war[0]
        st["ps6r1"] = cs_rd_war[1]
        st["ps2_war"] = last_exp
        return ao_evs

    def oproj_pass(rs_loc, ao_last, slab_ev):
        st_war = [st.get("stv_war0"), st.get("stv_war1"), st.get("stv_war2")]
        evac_toggle = 0
        last_store = None
        last_mm = None
        for c in range(TCH):
            for dout in range(DC):
                par = (c * DC + dout) % 2
                bank_war = st.get(f"obk{par}")
                mm = None
                for hs in range(2):
                    mm = P.pe(lambda e, par=par, hs=hs, dout=dout, c=c: e.matmul(
                        ps[par][0:128, 0:512],
                        oslab[:, 128 * (16 * hs + dout):128 * (16 * hs + dout + 1)],
                        aobuf[:, 8192 * hs + 512 * c:8192 * hs + 512 * (c + 1)],
                        start=(hs == 0), stop=(hs == 1)),
                        waits=[ao_last, slab_ev, bank_war, st.get("last_exp")]
                        if hs == 0 else (),
                        inc="pe" if hs == 1 else None)
                last_mm = mm
                sl = (c * DC + dout) % 3
                if evac_toggle == 0:
                    evc = P.ve(lambda e, par=par, sl=sl: e.tensor_copy(
                        stv[:, 512 * sl:512 * (sl + 1)], ps[par][0:128, 0:512]),
                        waits=[mm, st_war[sl]])
                else:
                    evc = P.sc(lambda e, par=par, sl=sl: e.activation(
                        stv[:, 512 * sl:512 * (sl + 1)], ps[par][0:128, 0:512],
                        AF.Copy), waits=[mm, st_war[sl]])
                evac_toggle ^= 1
                st[f"obk{par}"] = evc
                r, off = c // 2, (c % 2) * 512
                dst = rs_loc.ap()[D * r + 128 * dout:D * r + 128 * (dout + 1),
                                  off:off + 512]
                sev = P.dma("sync", lambda e, sl=sl, dst=dst: e.dma_start(
                    out=dst, in_=stv[:, 512 * sl:512 * (sl + 1)]),
                    waits=[evc], inc=f"st{sl}")
                st_war[sl] = sev
                last_store = sev
        for i in range(3):
            st[f"stv_war{i}"] = st_war[i]
        all_stores = [(f"st{i}", P.cnt[f"st{i}"]) for i in range(3)]
        return all_stores, last_mm

    if DBG == "2":
        dbg_snap([(dbg_out["d_xg1"], xg1.ap())])
    prev_done = None
    ao_first = None
    ssq_k_saved = None
    for hs in range(2):
        ssq = proj_pass(1, hs, xg1, ev_cc1, prev_done)
        if DBG == "2" and hs == 0:
            dbg_snap([(dbg_out["d_q"], qbuf[:, :]),
                      (dbg_out["d_k"], kbuf[:, :]),
                      (dbg_out["d_v"], vbuf[:, :])])
        ao_evs = attn_pass(1, hs, ssq)
        if hs == 0:
            ao_first = ao_evs[-1]
        prev_done = ao_evs[-1]
    if DBG == "2":
        dbg_snap([(dbg_out["d_ao"], aobuf[:, :])])
    ev_wo = None
    for hs in range(2):
        ev_wo = P.dma("gpsimd", lambda e, hs=hs: e.dma_start(
            out=oslab[:, 2048 * hs:2048 * (hs + 1)].rearrange("p (d f) -> p d f", d=DC),
            in_=wo_my[hs].rearrange("c p f -> p c f")), waits=[ao_first], inc="wld")
    st1_last, o1_mm = oproj_pass(rs1_loc, prev_done, ev_wo)
    ev_cc2 = P.coll(lambda e: e.collective_compute(
        "ReduceScatter", ALU.add, replica_groups=[list(range(NCORE))],
        ins=[rs1_loc.ap().opt()], outs=[rs1_out.ap().opt()]), waits=[st1_last])
    if DBG == "2":
        dbg_snap([(dbg_out["d_rs1o"], rs1_out.ap())])
        dump_banks(0, lnbc)

    sum_mm = ssq_mm = None
    r_evs = []
    xs_w = [None, None]
    rs_w = [None, None]
    sq_w = None
    attn1_done = prev_done
    for d in range(DC):
        par = d % 2
        exv = None
        for h_ in range(2):
            exv = P.dma("sync", lambda e, d=d, par=par, h_=h_: e.dma_start(
                out=xst[:, 1024 * par + 512 * h_:1024 * par + 512 * (h_ + 1)],
                in_=x_sh[:, 8192 * h_ + 512 * d:8192 * h_ + 512 * (d + 1)]),
                waits=[xs_w[par], attn1_done] if h_ == 0 else (),
                inc=f"x0{par}")
        erv = P.dma("sync", lambda e, d=d, par=par: e.dma_start(
            out=rsst[:, 1024 * par:1024 * (par + 1)],
            in_=rs1_out.ap()[128 * d:128 * (d + 1), :]),
            waits=[ev_cc2, rs_w[par], attn1_done], inc=f"rs{par}")
        rev = P.ve(lambda e, d=d, par=par: e.scalar_tensor_tensor(
            rbuf[:, 1024 * d:1024 * (d + 1)],
            rsst[:, 1024 * par:1024 * (par + 1)], bo_sb[:, d:d + 1],
            xst[:, 1024 * par:1024 * (par + 1)], op0=ALU.add, op1=ALU.add),
            waits=[exv, erv, ev_bo, ("pe", P.cnt["pe"])])
        xs_w[par] = rev
        rs_w[par] = rev
        r_evs.append(rev)
        sq = P.ve(lambda e, d=d: e.tensor_mul(
            hscr[:, 1024:2048], rbuf[:, 1024 * d:1024 * (d + 1)],
            rbuf[:, 1024 * d:1024 * (d + 1)]), waits=[rev, sq_w])
        for t2 in range(2):
            sum_mm = P.pe(lambda e, d=d, t2=t2: e.matmul(
                ps[t2][0:1, 0:512], ones_f[:, 0:1],
                rbuf[:, 1024 * d + 512 * t2:1024 * d + 512 * (t2 + 1)],
                start=(d == 0), stop=(d == DC - 1), skip_group_check=True),
                waits=[rev, ev_one, st.get("last_exp"), st.get("obk0"),
                       st.get("obk1"), st.get("av_war0"), st.get("av_war1")],
                inc=None)
            ssq_mm = P.pe(lambda e, d=d, t2=t2: e.matmul(
                ps[4 + t2][0:1, 0:512], ones_b[:, 0:1],
                hscr[:, 1024 + 512 * t2:1024 + 512 * (t2 + 1)],
                start=(d == 0), stop=(d == DC - 1), skip_group_check=True),
                waits=[sq], inc="pe" if t2 == 1 else None)
        sq_w = ssq_mm
        sum_mm = ssq_mm

    def ln_stats(sum_ev, ssq_ev, eps):
        evm = []
        for t2 in range(2):
            e1 = P.ve(lambda e, t2=t2: e.tensor_scalar(
                rows[0:1, 512 * t2:512 * (t2 + 1)],
                ps[t2][0:1, 0:512], 1.0 / D, None, op0=ALU.mult),
                waits=[sum_ev, st.get("rows0_war")])
            e2 = P.ve(lambda e, t2=t2: e.tensor_scalar(
                rows[32:33, 512 * t2:512 * (t2 + 1)],
                ps[4 + t2][0:1, 0:512], 1.0 / D, None, op0=ALU.mult),
                waits=[ssq_ev, st.get("rows32_war")])
            e3 = P.ve(lambda e, t2=t2: e.tensor_mul(
                ps[2][0:1, 0:512],
                rows[0:1, 512 * t2:512 * (t2 + 1)],
                rows[0:1, 512 * t2:512 * (t2 + 1)]),
                waits=[e1, st.get("rows64n_war"), st.get("rows64r_war"),
                       st.get("ps2_war")])
            e4 = P.ve(lambda e, t2=t2: e.tensor_tensor(
                rows[64:65, 512 * t2:512 * (t2 + 1)],
                rows[32:33, 512 * t2:512 * (t2 + 1)],
                ps[2][0:1, 0:512], op=ALU.subtract),
                waits=[e2, e3])
            e5 = P.ve(lambda e, t2=t2, eps=eps: e.tensor_scalar(
                rows[64:65, 512 * t2:512 * (t2 + 1)],
                rows[64:65, 512 * t2:512 * (t2 + 1)], 1.0, eps,
                op0=ALU.mult, op1=ALU.add), waits=[e4])
            e6 = P.sc(lambda e, t2=t2: e.activation(
                rows[64:65, 512 * t2:512 * (t2 + 1)],
                rows[64:65, 512 * t2:512 * (t2 + 1)], AF.Ln), waits=[e5])
            e7 = P.sc(lambda e, t2=t2: e.activation(
                rows[64:65, 512 * t2:512 * (t2 + 1)],
                rows[64:65, 512 * t2:512 * (t2 + 1)], AF.Exp, scale=-0.5),
                waits=[e6])
            e8a = P.ve(lambda e, t2=t2: e.tensor_copy(
                ps[2][0:1, 0:512], rows[64:65, 512 * t2:512 * (t2 + 1)]),
                waits=[e7])
            e8 = P.ve(lambda e, t2=t2: e.tensor_mul(
                rows[32:33, 512 * t2:512 * (t2 + 1)],
                rows[0:1, 512 * t2:512 * (t2 + 1)],
                ps[2][0:1, 0:512]), waits=[e8a])
            e9 = P.ve(lambda e, t2=t2: e.tensor_scalar(
                rows[32:33, 512 * t2:512 * (t2 + 1)],
                rows[32:33, 512 * t2:512 * (t2 + 1)], -1.0, None, op0=ALU.mult),
                waits=[e8])
            evm.append((e7, e9))
        abc = bbc = None
        for t2 in range(2):
            m1 = P.pe(lambda e, t2=t2: e.matmul(
                ps[2][0:128, 0:512], ones_f[64:65, 0:128],
                rows[64:65, 512 * t2:512 * (t2 + 1)], start=True, stop=True),
                waits=[evm[0][1], evm[1][1], evm[t2][0], ev_one,
                       st.get("ps2_war")])
            abc = P.ve(lambda e, t2=t2: e.tensor_copy(
                lnbc[:, 512 * t2:512 * (t2 + 1)], ps[2][0:128, 0:512]),
                waits=[m1])
            st["ps2_war"] = abc
            m2 = P.pe(lambda e, t2=t2: e.matmul(
                ps[2][0:128, 0:512], ones_f[32:33, 0:128],
                rows[32:33, 512 * t2:512 * (t2 + 1)], start=True, stop=True),
                waits=[evm[t2][1], abc])
            bbc = P.ve(lambda e, t2=t2: e.tensor_copy(
                lnbc[:, 1024 + 512 * t2:1024 + 512 * (t2 + 1)],
                ps[2][0:128, 0:512]), waits=[m2])
            st["ps2_war"] = bbc
        st["rows0_war"] = bbc
        st["rows32_war"] = bbc
        st["rows64n_war"] = bbc
        st["rows64r_war"] = bbc
        return abc, bbc

    if DBG == "2":
        dbg_snap([(dbg_out["d_m1"], rbuf[:, :])])
        dump_banks(2, lnbc)
    abc, bbc = ln_stats(sum_mm, ssq_mm, LN_EPS)
    if DBG == "2":
        dbg_snap([(dbg_out["d_lnbc"], lnbc[:, :]),
                  (dbg_out["d_rows"], rows[:, :])])
    x1_evs = []
    x1st_last = None
    hs_mm = None
    sq3_w = None
    hscr_w = [None, None]
    for d in range(DC):
        e1 = P.ve(lambda e, d=d: e.tensor_tensor(
            rbuf[:, 1024 * d:1024 * (d + 1)], rbuf[:, 1024 * d:1024 * (d + 1)],
            lnbc[:, 0:1024], op=ALU.mult), waits=[abc, bbc, r_evs[d]])
        e2 = P.ve(lambda e, d=d: e.tensor_tensor(
            rbuf[:, 1024 * d:1024 * (d + 1)], rbuf[:, 1024 * d:1024 * (d + 1)],
            lnbc[:, 1024:2048], op=ALU.add), waits=[e1])
        x1_evs.append(e2)
        x1st_last = P.dma("sync", lambda e, d=d: e.dma_start(
            out=x1_dram.ap()[128 * d:128 * (d + 1), :],
            in_=rbuf[:, 1024 * d:1024 * (d + 1)]), waits=[e2], inc="stg")
        sq = P.ve(lambda e, d=d: e.tensor_mul(
            hscr[:, 1024:2048], rbuf[:, 1024 * d:1024 * (d + 1)],
            rbuf[:, 1024 * d:1024 * (d + 1)]),
            waits=[e2, sq3_w, hscr_w[0], hscr_w[1]])
        for t2 in range(2):
            hs_mm = P.pe(lambda e, d=d, t2=t2: e.matmul(
                (ps[3] if t2 == 0 else ps[6])[0:1, 0:512], ones_b[:, 0:1],
                hscr[:, 1024 + 512 * t2:1024 + 512 * (t2 + 1)],
                start=(d == 0), stop=(d == DC - 1), skip_group_check=True),
                waits=[sq, st.get("ps6r0")],
                inc="pe" if t2 == 1 else None)
        sq3_w = hs_mm
    if DBG:
        for d in range(DC):
            P.dma("sync", lambda e, d=d: e.dma_start(
                out=dbg_out["d_x1"][128 * d:128 * (d + 1), :],
                in_=rbuf[:, 1024 * d:1024 * (d + 1)]), waits=[x1_evs[d]],
                inc="stg")
    if DBG == "2":
        dbg_snap([(dbg_out["d_x1b"], rbuf[:, :])])
    r2 = []
    for t2 in range(2):
        e1 = P.ve(lambda e, t2=t2: e.tensor_scalar(
            rows[0:1, 512 * t2:512 * (t2 + 1)],
            (ps[3] if t2 == 0 else ps[6])[0:1, 0:512],
            1.0 / D, RMS_EPS, op0=ALU.mult, op1=ALU.add),
            waits=[hs_mm, st.get("rows0_war")])
        if t2 == 1:
            st["ps6r0"] = e1
        e2 = P.sc(lambda e, t2=t2: e.activation(
            rows[0:1, 512 * t2:512 * (t2 + 1)], rows[0:1, 512 * t2:512 * (t2 + 1)],
            AF.Ln), waits=[e1])
        e3 = P.sc(lambda e, t2=t2: e.activation(
            rows[0:1, 512 * t2:512 * (t2 + 1)], rows[0:1, 512 * t2:512 * (t2 + 1)],
            AF.Exp, scale=-0.5), waits=[e2])
        m1 = P.pe(lambda e, t2=t2: e.matmul(
            ps[2][0:128, 0:512], ones_f[0:1, 0:128],
            rows[0:1, 512 * t2:512 * (t2 + 1)], start=True, stop=True),
            waits=[e3, st.get("ps2_war")])
        rc = P.ve(lambda e, t2=t2: e.tensor_copy(
            lnbc[:, 512 * t2:512 * (t2 + 1)], ps[2][0:128, 0:512]),
            waits=[m1, x1_evs[-1]])
        st["ps2_war"] = rc
        st["rows0_war"] = m1
        r2.append(rc)
    he_last = None
    for d in range(DC):
        par = d % 2
        he = P.ve(lambda e, d=d, par=par: e.scalar_tensor_tensor(
            hscr[:, 1024 * par:1024 * (par + 1)],
            rbuf[:, 1024 * d:1024 * (d + 1)], anw_sb[:, d:d + 1],
            lnbc[:, 0:1024], op0=ALU.mult, op1=ALU.mult),
            waits=[r2[0], r2[1], x1_evs[d], ev_anw, hscr_w[par], sq3_w])
        hstv = None
        for h_ in range(2):
            hstv = P.dma("sync", lambda e, d=d, par=par, h_=h_: e.dma_start(
                out=xg2_loc.ap()[:, 8192 * h_ + 512 * d:8192 * h_ + 512 * (d + 1)],
                in_=hscr[:, 1024 * par + 512 * h_:1024 * par + 512 * (h_ + 1)]),
                waits=[he] if h_ == 0 else [hstv], inc=f"hst{par}")
        hscr_w[par] = hstv
        he_last = he
    st["ln_done"] = [("dve", P.cnt["dve"]), ("act", P.cnt["act"]),
                     ("pool", P.cnt["pool"]), ("pe", P.cnt["pe"]),
                     ("stg", P.cnt["stg"])]
    for k in ("bk_q0", "bk_q1", "bk_k0", "bk_k1", "bk_v0", "bk_v1"):
        st[k] = [st.get(k), he_last]
    ev_cc3 = P.coll(lambda e: e.collective_compute(
        "AllGather", ALU.bypass, replica_groups=[list(range(NCORE))],
        ins=[xg2_loc.ap().opt()], outs=[xg2.ap().opt()]),
        waits=[hscr_w[0], hscr_w[1]])

    ev_w2 = None
    for si, (src, idx) in enumerate([(q2_my, 0), (q2_my, 1), (k2_my, 0),
                                     (v2_my, 0)]):
        ev_w2 = P.dma("gpsimd", lambda e, si=si, src=src, idx=idx: e.dma_start(
            out=aslab[:, 2048 * si:2048 * (si + 1)].rearrange("p (d f) -> p d f", d=DC),
            in_=src[idx].rearrange("(d p) f -> p d f", p=128)),
            waits=[st.get("ln_done")], inc="wld")
    st["ev_w2"] = ev_w2

    prev2 = [x1st_last, he_last]
    st["ao_war"] = [st1_last, o1_mm]
    ao2_evs_last = None
    ssq = proj_pass2(xg2, ev_cc3, prev2)
    if DBG == "2":
        dbg_snap([(dbg_out["d_q2"], qbuf[:, :]),
                  (dbg_out["d_k2"], kbuf[:, :]),
                  (dbg_out["d_v2"], vbuf[:, :])])
        e1_ = P.ve(lambda e: e.tensor_copy(tbv[:, 0:512], sinv[:, :]))
        e2_ = P.ve(lambda e: e.tensor_copy(tbv[:, 512:1024]
                                           if False else auxv[:, :],
                                           cosv[:, :]), waits=[e1_])
        e3_ = P.dma("sync", lambda e: e.dma_start(
            out=dbg_out["d_trig"][:, 0:512], in_=tbv[:, :]), waits=[e2_],
            inc="dbg")
        e3_ = P.dma("sync", lambda e: e.dma_start(
            out=dbg_out["d_trig"][:, 512:1024], in_=auxv[:, :]),
            waits=[e3_], inc="dbg")
        for eng_ in Pl.ENGS:
            P.wait(eng_, e3_)
    ao_evs = attn_pass(2, 0, ssq)
    prev2 = ao_evs[-1]
    st["ao_war"] = None
    ao_evs = attn_pass(2, 1, {"q": ssq["q1"], "k": ssq["k"]},
                       qsrc=aobuf, qoff=8192, qrow=34)
    prev2 = ao_evs[-1]
    ao2_evs_last = prev2
    if DBG == "2":
        dbg_snap([(dbg_out["d_ao2"], aobuf[:, :])])
    ev_wo2 = None
    for hs in range(2):
        ev_wo2 = P.dma("gpsimd", lambda e, hs=hs: e.dma_start(
            out=oslab[:, 2048 * hs:2048 * (hs + 1)].rearrange("p (d f) -> p d f", d=DC),
            in_=o2_my[hs].rearrange("c p f -> p c f")),
            waits=[o1_mm, st1_last], inc="wld")
    pre_gld = []
    for ic_ in range(2):
        pre_gld.append(P.dma("gpsimd", lambda e, ic_=ic_: e.dma_start(
            out=wgsl[:, 2048 * ic_:2048 * (ic_ + 1)].rearrange(
                "p (d f) -> p d f", d=DC),
            in_=wg_p[0, ic_].rearrange("(d p) f -> p d f", p=128)),
            inc=f"wg{ic_}"))
    st2_last, o2_mm = oproj_pass(rs2_loc, ao2_evs_last, ev_wo2)
    ev_cc4 = P.coll(lambda e: e.collective_compute(
        "ReduceScatter", ALU.add, replica_groups=[list(range(NCORE))],
        ins=[rs2_loc.ap().opt()], outs=[rs2_out.ap().opt()]), waits=[st2_last])
    if DBG == "2":
        dbg_snap([(dbg_out["d_rs2o"], rs2_out.ap())])

    xs_w3 = [None, None]
    rs_w3 = [None, None]
    h1_w = [None, None]
    hsq_w = None
    ssqr_mm = None
    fb_evs = []
    h1st_last = None
    for d in range(DC):
        par = d % 2
        exv = P.dma("sync", lambda e, d=d, par=par: e.dma_start(
            out=xst[:, 1024 * par:1024 * (par + 1)],
            in_=x1_dram.ap()[128 * d:128 * (d + 1), :]),
            waits=[xs_w3[par], x1st_last, ao2_evs_last], inc=f"x1{par}")
        erv = P.dma("sync", lambda e, d=d, par=par: e.dma_start(
            out=rsst[:, 1024 * par:1024 * (par + 1)],
            in_=rs2_out.ap()[128 * d:128 * (d + 1), :]),
            waits=[ev_cc4, rs_w3[par], ao2_evs_last], inc=f"r2{par}")
        h1e = P.ve(lambda e, par=par: e.tensor_tensor(
            h1ev[:, 1024 * par:1024 * (par + 1)],
            rsst[:, 1024 * par:1024 * (par + 1)],
            xst[:, 1024 * par:1024 * (par + 1)], op=ALU.add),
            waits=[exv, erv, h1_w[par]])
        h1st_last = P.dma("sync", lambda e, d=d, par=par: e.dma_start(
            out=h1_dram.ap()[128 * d:128 * (d + 1), :],
            in_=h1ev[:, 1024 * par:1024 * (par + 1)]), waits=[h1e],
            inc=f"h1s{par}")
        fb = P.ve(lambda e, d=d, par=par: e.tensor_tensor(
            rbuf[:, 1024 * d:1024 * (d + 1)],
            h1ev[:, 1024 * par:1024 * (par + 1)],
            xst[:, 1024 * par:1024 * (par + 1)], op=ALU.add),
            waits=[h1e, ("pe", P.cnt["pe"])])
        fb_evs.append(fb)
        xs_w3[par] = fb
        rs_w3[par] = h1e
        sq = P.ve(lambda e, par=par: e.tensor_mul(
            hsqt[:, :], h1ev[:, 1024 * par:1024 * (par + 1)],
            h1ev[:, 1024 * par:1024 * (par + 1)]), waits=[h1e, hsq_w])
        h1_w[par] = [fb, sq, h1st_last]
        for t2 in range(2):
            ssqr_mm = P.pe(lambda e, t2=t2, d=d: e.matmul(
                ps[t2][0:1, 0:512], ones_b[:, 0:1],
                hsqt[:, 512 * t2:512 * (t2 + 1)],
                start=(d == 0), stop=(d == DC - 1), skip_group_check=True),
                waits=[sq, ev_oneb, st.get("last_exp"), st.get("obk0"),
                       st.get("obk1"), st.get("av_war0"), st.get("av_war1")],
                inc="pe" if t2 == 1 else None)
        hsq_w = [ssqr_mm]
    if DBG:
        for d in range(DC):
            P.dma("sync", lambda e, d=d: e.dma_start(
                out=dbg_out["d_h1"][128 * d:128 * (d + 1), :],
                in_=h1_dram.ap()[128 * d:128 * (d + 1), :]),
                waits=[("stg", P.cnt["stg"])], inc="stg")
    if DBG == "2":
        dbg_snap([(dbg_out["d_fb"], rbuf[:, :])])

    r3 = []
    for t2 in range(2):
        e1 = P.ve(lambda e, t2=t2: e.tensor_scalar(
            rows[0:1, 512 * t2:512 * (t2 + 1)], ps[t2][0:1, 0:512],
            1.0 / D, RMS_EPS, op0=ALU.mult, op1=ALU.add),
            waits=[ssqr_mm, st.get("rows0_war")])
        e2 = P.sc(lambda e, t2=t2: e.activation(
            rows[0:1, 512 * t2:512 * (t2 + 1)], rows[0:1, 512 * t2:512 * (t2 + 1)],
            AF.Ln), waits=[e1])
        e3 = P.sc(lambda e, t2=t2: e.activation(
            rows[0:1, 512 * t2:512 * (t2 + 1)], rows[0:1, 512 * t2:512 * (t2 + 1)],
            AF.Exp, scale=-0.5), waits=[e2])
        m1 = P.pe(lambda e, t2=t2: e.matmul(
            ps[2][0:128, 0:512], ones_f[0:1, 0:128],
            rows[0:1, 512 * t2:512 * (t2 + 1)], start=True, stop=True),
            waits=[e3, st.get("ps2_war")])
        rc = P.ve(lambda e, t2=t2: e.tensor_copy(
            rstbc[:, 512 * t2:512 * (t2 + 1)], ps[2][0:128, 0:512]),
            waits=[m1, h1_w[0], h1_w[1]])
        st["ps2_war"] = rc
        st["rows0_war"] = m1
        r3.append(rc)
    tcol = None
    for tt in range(8):
        tcol = P.pe(lambda e, tt=tt: e.matmul(
            ps[4][0:128, tt:tt + 1], rows[0:1, 128 * tt:128 * (tt + 1)],
            ones_f[0:1, 0:1], start=(tt == 0), stop=(tt == 7),
            skip_group_check=True),
            waits=[r3[0], r3[1], ev_one], inc="pe" if tt == 7 else None)
    rcol_all = P.ve(lambda e: e.tensor_copy(cols[:, 8:16], ps[4][0:128, 0:8]),
                    waits=[tcol])
    rcol_evs = [rcol_all] * 8

    wprep = []
    wp = None
    for d in range(DC):
        w1 = P.ve(lambda e, d=d: e.tensor_scalar(
            cols[:, 32:40], gwt_sb[:, 8 * d:8 * (d + 1)], fnw_sb[:, d:d + 1],
            None, op0=ALU.mult), waits=[ev_gw, ev_fnw, wp])
        w2 = P.ve(lambda e, d=d: e.tensor_copy(
            whi_sb[:, 8 * d:8 * (d + 1)], cols[:, 32:40]), waits=[w1])
        w3 = P.ve(lambda e, d=d: e.tensor_tensor(
            cols[:, 40:48], whi_sb[:, 8 * d:8 * (d + 1)], cols[:, 32:40],
            op=ALU.subtract), waits=[w2])
        w4 = P.ve(lambda e, d=d: e.tensor_scalar(
            wlo_sb[:, 8 * d:8 * (d + 1)], cols[:, 40:48], -1.0, None,
            op0=ALU.mult), waits=[w3])
        wp = w4
        wprep.append(w4)
    h1s_w = [None, None]
    hl_w = None
    gmm = None
    h2b_evs = []
    for d in range(DC):
        par = d % 2
        ehv = P.dma("sync", lambda e, d=d, par=par: e.dma_start(
            out=xst[:, 1024 * par:1024 * (par + 1)],
            in_=h1_dram.ap()[128 * d:128 * (d + 1), :]),
            waits=[h1s_w[par], ("h1s0", P.cnt["h1s0"]),
                   ("h1s1", P.cnt["h1s1"]),
                   fb_evs[DC - 2 + par]], inc=f"h1{par}")
        hb = P.ve(lambda e, d=d, par=par: e.scalar_tensor_tensor(
            hbuf[:, 1024 * d:1024 * (d + 1)],
            xst[:, 1024 * par:1024 * (par + 1)], fnw_sb[:, d:d + 1],
            rstbc[:, 0:1024], op0=ALU.mult, op1=ALU.mult),
            waits=[ehv, r3[0], r3[1], ev_fnw, st.get("last_exp")])
        h2b_evs.append(hb)
        hi = P.ve(lambda e, par=par: e.tensor_copy(
            hilo[:, 0:1024], xst[:, 1024 * par:1024 * (par + 1)]),
            waits=[ehv, hl_w])
        lo = P.ve(lambda e, par=par: e.tensor_tensor(
            hilo[:, 1024:2048], xst[:, 1024 * par:1024 * (par + 1)],
            hilo[:, 0:1024], op=ALU.subtract), waits=[hi])
        h1s_w[par] = [hb, lo]
        for tt in range(8):
            for pi, (aoff, wbuf) in enumerate(((0, whi_sb), (0, wlo_sb),
                                               (1024, whi_sb))):
                gmm = P.pe(lambda e, d=d, tt=tt, aoff=aoff, wbuf=wbuf, pi=pi:
                           e.matmul(
                    ps[tt][0:128, 0:8],
                    hilo[:, aoff + 128 * tt:aoff + 128 * (tt + 1)],
                    wbuf[:, 8 * d:8 * (d + 1)],
                    start=(d == 0 and pi == 0), stop=(d == DC - 1 and pi == 2),
                    skip_group_check=True),
                    waits=[lo, wprep[d], st.get("last_exp"), rcol_all,
                           st.get("ps6r0"), st.get("ps6r1"),
                           st.get("ps7_war")],
                    inc="pe" if (pi == 2 and tt == 7) else None)
        hl_w = gmm
    if DBG == "2":
        dbg_snap([(dbg_out["d_hb"], hbuf[:, :])])
    top_done = []
    evac_evs = [None] * 8
    prev_ev = None
    c0 = 16
    for tt in range(8):
        e1 = P.ve(lambda e, tt=tt: e.tensor_scalar(
            comb_all[:, 8 * tt:8 * (tt + 1)], ps[tt][0:128, 0:8],
            cols[:, 8 + tt:9 + tt], None, op0=ALU.mult),
            waits=[gmm, rcol_evs[tt], prev_ev])
        evac_evs[tt] = e1
        e2 = P.ve(lambda e, tt=tt: e.tensor_reduce(
            cols[:, c0:c0 + 1], comb_all[:, 8 * tt:8 * (tt + 1)], axis=AX.X,
            op=ALU.max), waits=[e1])
        e3 = P.ve(lambda e: e.tensor_scalar(
            cols[:, c0 + 1:c0 + 2], cols[:, c0:c0 + 1], -1.0, None,
            op0=ALU.mult), waits=[e2])
        e4 = P.sc(lambda e, tt=tt: e.activation(
            comb_all[:, 8 * tt:8 * (tt + 1)], comb_all[:, 8 * tt:8 * (tt + 1)],
            AF.Exp, bias=cols[:, c0 + 1:c0 + 2], scale=1.0), waits=[e3])
        e5 = P.ve(lambda e, tt=tt: e.tensor_reduce(
            cols[:, c0 + 2:c0 + 3], comb_all[:, 8 * tt:8 * (tt + 1)], axis=AX.X,
            op=ALU.add), waits=[e4])
        e6 = P.ve(lambda e: e.reciprocal(
            cols[:, c0 + 3:c0 + 4], cols[:, c0 + 2:c0 + 3]), waits=[e5])
        e7 = P.ve(lambda e, tt=tt: e.tensor_scalar(
            comb_all[:, 8 * tt:8 * (tt + 1)], comb_all[:, 8 * tt:8 * (tt + 1)],
            cols[:, c0 + 3:c0 + 4], None, op0=ALU.mult), waits=[e6])
        t1 = P.ve(lambda e, tt=tt: e.tensor_reduce(
            cols[:, c0:c0 + 1], comb_all[:, 8 * tt:8 * (tt + 1)], axis=AX.X,
            op=ALU.max), waits=[e7])
        t2_ = P.ve(lambda e, tt=tt: e.tensor_scalar(
            cols[:, 48:56], comb_all[:, 8 * tt:8 * (tt + 1)], cols[:, c0:c0 + 1],
            None, op0=ALU.is_ge), waits=[t1])
        t3 = P.ve(lambda e, tt=tt: e.scalar_tensor_tensor(
            cols[:, 56:64], cols[:, 48:56], -1e9,
            comb_all[:, 8 * tt:8 * (tt + 1)], op0=ALU.mult, op1=ALU.add),
            waits=[t2_])
        t4 = P.ve(lambda e: e.tensor_reduce(
            cols[:, c0 + 1:c0 + 2], cols[:, 56:64], axis=AX.X, op=ALU.max),
            waits=[t3])
        t5 = P.ve(lambda e: e.tensor_tensor(
            cols[:, c0 + 2:c0 + 3], cols[:, c0:c0 + 1], cols[:, c0 + 1:c0 + 2],
            op=ALU.add), waits=[t4])
        t6 = P.ve(lambda e: e.tensor_scalar(
            cols[:, c0 + 2:c0 + 3], cols[:, c0 + 2:c0 + 3], 1.0, 1e-9,
            op0=ALU.mult, op1=ALU.add), waits=[t5])
        t7 = P.ve(lambda e: e.reciprocal(
            cols[:, c0 + 3:c0 + 4], cols[:, c0 + 2:c0 + 3]), waits=[t6])
        t8 = P.ve(lambda e, tt=tt: e.tensor_scalar(
            cols[:, 48:56], comb_all[:, 8 * tt:8 * (tt + 1)],
            cols[:, c0 + 1:c0 + 2], None, op0=ALU.is_ge), waits=[t7])
        t9 = P.ve(lambda e, tt=tt: e.tensor_tensor(
            cols[:, 48:56], cols[:, 48:56], comb_all[:, 8 * tt:8 * (tt + 1)],
            op=ALU.mult), waits=[t8])
        ta = P.ve(lambda e, tt=tt: e.tensor_scalar(
            comb_all[:, 8 * tt:8 * (tt + 1)], cols[:, 48:56],
            cols[:, c0 + 3:c0 + 4], None, op0=ALU.mult), waits=[t9])
        prev_ev = ta
        top_done.append(ta)
    if DBG:
        P.dma("sync", lambda e: e.dma_start(
            out=dbg_out["d_comb"][:, :], in_=comb_all[:, :]),
            waits=[top_done[-1]], inc="stg")
    ctc = None
    for tt in range(8):
        trc = P.pe(lambda e, tt=tt: e.transpose(
            ps[2][0:8, 0:128], comb_all[:, 8 * tt:8 * (tt + 1)], ident_f[:, :]),
            waits=[top_done[tt], ev_idf, st.get("ps2_war"), o2_mm])
        ctc = P.ve(lambda e, tt=tt: e.tensor_copy(
            combT[0:8, 128 * tt:128 * (tt + 1)], ps[2][0:8, 0:128]),
            waits=[trc])
        st["ps2_war"] = ctc

    wg_w = [None, None]
    wu_w = [None, None]
    wd_w = [None, None]
    act_w = {}
    crow_w = None
    cbc_w = [None, None]
    bank_w = [evac_evs[0], evac_evs[1], [evac_evs[2], ctc], evac_evs[3],
              [evac_evs[4], rcol_all], evac_evs[5], evac_evs[6], evac_evs[7]]
    wdbk_w = [None, None]
    sil_w = None
    euc_w = None
    f_evs = {d: fb_evs[d] for d in range(DC)}
    last_sil = None
    for eidx in range(NE):
        crow = P.dma("sync", lambda e, eidx=eidx: e.dma_start(
            out=rows[0:1, 0:1024],
            in_=combT[eidx:eidx + 1, :]),
            waits=[ctc, crow_w, st.get("rows64n_war"), st.get("rows64r_war"),
                   st.get("rows0_war"), st.get("rows32_war")],
            inc="cr")
        st["rows64n_war"] = None
        st["rows64r_war"] = None
        for t2 in range(2):
            m1 = P.pe(lambda e, t2=t2: e.matmul(
                ps[2][0:128, 0:512], ones_f[0:1, 0:128],
                rows[0:1, 512 * t2:512 * (t2 + 1)], start=True, stop=True),
                waits=[crow, st.get("ps2_war")])
            crow_w = m1
            cb2 = P.ve(lambda e, t2=t2: e.tensor_copy(
                cbc[:, 512 * t2:512 * (t2 + 1)], ps[2][0:128, 0:512]),
                waits=[m1, cbc_w[t2], r3[0], r3[1]])
            st["ps2_war"] = cb2
            cbc_w[t2] = cb2
        eg_ev = eu_ev = None
        for ic in range(NIC):
            wpar = ic % 2
            if eidx == 0 and ic < 2:
                gld = pre_gld[ic]
            else:
                gld = P.dma("gpsimd", lambda e, eidx=eidx, ic=ic, wpar=wpar:
                            e.dma_start(
                    out=wgsl[:, 2048 * wpar:2048 * (wpar + 1)].rearrange(
                        "p (d f) -> p d f", d=DC),
                    in_=wg_p[eidx, ic].rearrange("(d p) f -> p d f", p=128)),
                    waits=[wg_w[wpar]], inc=f"wg{wpar}")
            uld = P.dma("gpsimd", lambda e, eidx=eidx, ic=ic, wpar=wpar:
                        e.dma_start(
                out=wusl[:, 2048 * wpar:2048 * (wpar + 1)].rearrange(
                    "p (d f) -> p d f", d=DC),
                in_=wu_p[eidx, ic].rearrange("(d p) f -> p d f", p=128)),
                waits=[wu_w[wpar], ao2_evs_last], inc=f"wu{wpar}")
            bsel = 4 * (ic % 2)
            for t2 in range(2):
                egb = ps[bsel + t2]
                eub = ps[bsel + 2 + t2]
                for d in range(DC):
                    eg_ev = P.pe(lambda e, d=d, t2=t2, wpar=wpar, egb=egb:
                                 e.matmul(
                        egb[0:128, 0:512],
                        wgsl[:, 2048 * wpar + 128 * d:2048 * wpar + 128 * (d + 1)],
                        hbuf[:, 1024 * d + 512 * t2:1024 * d + 512 * (t2 + 1)],
                        start=(d == 0), stop=(d == DC - 1)),
                        waits=[gld, h2b_evs[-1], bank_w[bsel + t2]]
                        if d == 0 else (),
                        inc="pe" if d == DC - 1 else None)
                for d in range(DC):
                    eu_ev = P.pe(lambda e, d=d, t2=t2, wpar=wpar, eub=eub:
                                 e.matmul(
                        eub[0:128, 0:512],
                        wusl[:, 2048 * wpar + 128 * d:2048 * wpar + 128 * (d + 1)],
                        hbuf[:, 1024 * d + 512 * t2:1024 * d + 512 * (t2 + 1)],
                        start=(d == 0), stop=(d == DC - 1)),
                        waits=[uld, bank_w[bsel + 2 + t2], cbc_w[t2]]
                        if (d == 0 and ic == 0) else
                        ([uld, bank_w[bsel + 2 + t2]] if d == 0 else ()),
                        inc="pe" if d == DC - 1 else None)
                aslot = 2 * ic + t2
                sil = P.sc(lambda e, egb=egb: e.activation(
                    silscr[:, :], egb[0:128, 0:512], AF.Silu),
                    waits=[eg_ev, sil_w])
                last_sil = sil
                euc = P.ve(lambda e, eub=eub, t2=t2: e.tensor_tensor(
                    mescr[:, :], eub[0:128, 0:512], cbc[:, 512 * t2:512 * (t2 + 1)],
                    op=ALU.mult), waits=[eu_ev, cbc_w[t2], euc_w])
                act = P.ve(lambda e, aslot=aslot: e.tensor_tensor(
                    aobuf[:, 512 * aslot:512 * (aslot + 1)], mescr[:, :],
                    silscr[:, :], op=ALU.mult),
                    waits=[sil, euc, act_w.get(aslot)])
                sil_w = act
                euc_w = act
                bank_w[bsel + t2] = sil
                bank_w[bsel + 2 + t2] = euc
                act_w[aslot] = act
            wg_w[wpar] = eg_ev
            wu_w[wpar] = eu_ev
        last_f = None
        wd_mm = None
        for dout in range(DC):
            wdp = dout % 2
            dld = P.dma("gpsimd", lambda e, eidx=eidx, dout=dout, wdp=wdp:
                        e.dma_start(
                out=wdsl[:, 1408 * wdp:1408 * (wdp + 1)].rearrange(
                    "p (d f) -> p d f", d=NIC),
                in_=wd_p[eidx, dout].rearrange("(d p) f -> p d f", p=128)),
                waits=[wd_w[wdp], o2_mm], inc=f"wd{wdp}")
            for t2 in range(2):
                bpar = (dout * 2 + t2) % 2
                for ic in range(NIC):
                    wd_mm = P.pe(lambda e, ic=ic, t2=t2, wdp=wdp, bpar=bpar:
                                 e.matmul(
                        ps[6 + bpar][0:128, 0:512],
                        wdsl[:, 1408 * wdp + 128 * ic:1408 * wdp + 128 * (ic + 1)],
                        aobuf[:, 512 * (2 * ic + t2):512 * (2 * ic + t2 + 1)],
                        start=(ic == 0), stop=(ic == NIC - 1)),
                        waits=[dld, act_w.get(2 * ic + t2), wdbk_w[bpar],
                               st.get("ps6_war"), st.get("ps7_war"),
                               bank_w[6], bank_w[7]]
                        if ic == 0 else [act_w.get(2 * ic + t2)],
                        inc="pe" if ic == NIC - 1 else None)
                fadd = P.ve(lambda e, dout=dout, t2=t2, bpar=bpar: e.tensor_tensor(
                    rbuf[:, 1024 * dout + 512 * t2:1024 * dout + 512 * (t2 + 1)],
                    ps[6 + bpar][0:128, 0:512],
                    rbuf[:, 1024 * dout + 512 * t2:1024 * dout + 512 * (t2 + 1)],
                    op=ALU.add), waits=[wd_mm, f_evs[dout]])
                f_evs[dout] = fadd
                wdbk_w[bpar] = fadd
                last_f = fadd
            wd_w[wdp] = wd_mm
        st["ps6_war"] = None
        st["ps7_war"] = None
        bank_w[6] = last_f
        bank_w[7] = last_f
        for aslot in range(2 * NIC):
            act_w[aslot] = last_f

    if DBG:
        for d in range(DC):
            P.dma("sync", lambda e, d=d: e.dma_start(
                out=dbg_out["d_F"][128 * d:128 * (d + 1), :],
                in_=rbuf[:, 1024 * d:1024 * (d + 1)]), waits=[f_evs[d]],
                inc="stg")

    sum2 = ssq2 = None
    sqf_w = None
    for d in range(DC):
        sq = P.ve(lambda e, d=d: e.tensor_mul(
            hscr[:, 1024:2048], rbuf[:, 1024 * d:1024 * (d + 1)],
            rbuf[:, 1024 * d:1024 * (d + 1)]), waits=[f_evs[d], sqf_w])
        for t2 in range(2):
            sum2 = P.pe(lambda e, d=d, t2=t2: e.matmul(
                ps[t2][0:1, 0:512], ones_f[:, 0:1],
                rbuf[:, 1024 * d + 512 * t2:1024 * d + 512 * (t2 + 1)],
                start=(d == 0), stop=(d == DC - 1), skip_group_check=True),
                waits=[f_evs[d], ev_one, last_sil], inc=None)
            ssq2 = P.pe(lambda e, d=d, t2=t2: e.matmul(
                ps[4 + t2][0:1, 0:512], ones_b[:, 0:1],
                hscr[:, 1024 + 512 * t2:1024 + 512 * (t2 + 1)],
                start=(d == 0), stop=(d == DC - 1), skip_group_check=True),
                waits=[sq, last_sil], inc="pe" if t2 == 1 else None)
        sqf_w = ssq2
        sum2 = ssq2
    st["rows0_war"] = None
    st["rows32_war"] = None
    abc2, bbc2 = ln_stats(sum2, ssq2, LN_EPS)
    out_last = None
    for d in range(DC):
        e1 = P.ve(lambda e, d=d: e.tensor_tensor(
            rbuf[:, 1024 * d:1024 * (d + 1)], rbuf[:, 1024 * d:1024 * (d + 1)],
            lnbc[:, 0:1024], op=ALU.mult), waits=[abc2, bbc2, f_evs[d]])
        e2 = P.ve(lambda e, d=d: e.tensor_tensor(
            rbuf[:, 1024 * d:1024 * (d + 1)], rbuf[:, 1024 * d:1024 * (d + 1)],
            lnbc[:, 1024:2048], op=ALU.add), waits=[e1])
        out_last = P.dma("sync", lambda e, d=d: e.dma_start(
            out=out_t[128 * d:128 * (d + 1), :],
            in_=rbuf[:, 1024 * d:1024 * (d + 1)]), waits=[e2], inc="out")
    P.wait("sync", out_last)
    if DBG:
        P.wait("sync", ("stg", P.cnt["stg"]))

    P.check_deadlock()
    with nc.Block() as block:
        def make(eng_name):
            def body(eng):
                for fn in P.ops[eng_name]:
                    fn(eng)
            return body
        block.sync(make("sync"))
        block.gpsimd(make("gpsimd"))
        block.vector(make("vector"))
        block.scalar(make("scalar"))
        block.tensor(make("tensor"))
    return nc


def _prep(inputs):
    f = np.float32
    x = np.asarray(inputs["L0_attn_x"], f).reshape(S, D)
    wq = np.asarray(inputs["L0_attn_wq"], f)
    wk = np.asarray(inputs["L0_attn_wk"], f)
    wv = np.asarray(inputs["L0_attn_wv"], f)
    bq = np.asarray(inputs["L0_attn_bq"], f)
    bk = np.asarray(inputs["L0_attn_bk"], f)
    bv = np.asarray(inputs["L0_attn_bv"], f)
    wo = np.asarray(inputs["L0_attn_wo"], f)
    bo = np.asarray(inputs["L0_attn_bo"], f)
    anw = np.asarray(inputs["L0_ffn_attn_norm_w"], f)
    qw2 = np.asarray(inputs["L0_ffn_q_w"], f)
    kw2 = np.asarray(inputs["L0_ffn_k_w"], f)
    vw2 = np.asarray(inputs["L0_ffn_v_w"], f)
    ow2 = np.asarray(inputs["L0_ffn_o_w"], f)
    qnw = np.asarray(inputs["L0_ffn_q_norm_w"], f)
    knw = np.asarray(inputs["L0_ffn_k_norm_w"], f)
    fnw = np.asarray(inputs["L0_ffn_ffn_norm_w"], f)
    gw = np.asarray(inputs["L0_ffn_gate_w"], f)
    wg = np.asarray(inputs["L0_ffn_wg"], f)
    wu = np.asarray(inputs["L0_ffn_wu"], f)
    wd = np.asarray(inputs["L0_ffn_wd"], f)

    def headslabs(w, nh):
        return np.ascontiguousarray(w.T.reshape(D, nh, HD).transpose(1, 0, 2))

    wq_s = headslabs(wq, NH)
    wk_s = headslabs(wk, NH)
    wv_s = headslabs(wv, NH)
    q2_s = headslabs(qw2, NH)
    k2_s = headslabs(kw2, KVH)
    v2_s = headslabs(vw2, KVH)
    woT = np.ascontiguousarray(wo.T)
    o2T = np.ascontiguousarray(ow2.T)
    gw_t = np.ascontiguousarray(gw.T)
    wg_pp = np.ascontiguousarray(
        np.stack([wg[e].T.reshape(D, NIC, HD).transpose(1, 0, 2) for e in range(NE)]))
    wu_pp = np.ascontiguousarray(
        np.stack([wu[e].T.reshape(D, NIC, HD).transpose(1, 0, 2) for e in range(NE)]))
    wd_pp = np.ascontiguousarray(
        np.stack([wd[e].T.reshape(NI, DC, HD).transpose(1, 0, 2) for e in range(NE)]))
    bo_t = np.ascontiguousarray(bo.reshape(DC, HD).T)
    anw_t = np.ascontiguousarray(anw.reshape(DC, HD).T)
    fnw_t = np.ascontiguousarray(fnw.reshape(DC, HD).T)
    qknw = np.ascontiguousarray(np.stack([qnw, knw], axis=1))

    in_maps = []
    for i in range(NCORE):
        h0, h1 = 2 * i, 2 * i + 1
        kv = i // 2
        b3 = np.zeros((HD, 8), f)
        b3[:, 0] = bq[HD * h0:HD * h0 + HD]
        b3[:, 1] = bq[HD * h1:HD * h1 + HD]
        b3[:, 2] = bk[HD * h0:HD * h0 + HD]
        b3[:, 3] = bk[HD * h1:HD * h1 + HD]
        b3[:, 4] = bv[HD * h0:HD * h0 + HD]
        b3[:, 5] = bv[HD * h1:HD * h1 + HD]
        wo_my = np.ascontiguousarray(
            woT[256 * i:256 * (i + 1)].reshape(2, HD, DC, HD).transpose(0, 2, 1, 3))
        o2_myv = np.ascontiguousarray(
            o2T[256 * i:256 * (i + 1)].reshape(2, HD, DC, HD).transpose(0, 2, 1, 3))
        in_maps.append({
            "x_sh": np.ascontiguousarray(
                x[T * i:T * (i + 1), :].reshape(2, 512, DC, 128)
                .transpose(3, 0, 2, 1).reshape(128, 2 * DC * 512)),
            "wq_my": np.ascontiguousarray(wq_s[[h0, h1]]),
            "wk_my": np.ascontiguousarray(wk_s[[h0, h1]]),
            "wv_my": np.ascontiguousarray(wv_s[[h0, h1]]),
            "b3_my": b3,
            "wo_my": wo_my,
            "bo_t": bo_t,
            "q2_my": np.ascontiguousarray(q2_s[[h0, h1]]),
            "k2_my": np.ascontiguousarray(k2_s[[kv]]),
            "v2_my": np.ascontiguousarray(v2_s[[kv]]),
            "o2_my": o2_myv,
            "qknw": qknw,
            "anw_t": anw_t,
            "fnw_t": fnw_t,
            "gw_t": gw_t,
            "wg_p": wg_pp,
            "wu_p": wu_pp,
            "wd_p": wd_pp,
        })
    return in_maps


def kernel(**inputs):
    in_maps = _prep(inputs)
    nc = _build()
    res = run_bass_kernel_spmd(nc, in_maps, core_ids=list(range(NCORE)))
    out = np.empty((1, S, D), np.float32)
    for i in range(NCORE):
        out[0, T * i:T * (i + 1), :] = res.results[i]["out_t"].T
    if DBG:
        kernel._dbg = res.results
    return out


# version 15 (final)
# speedup vs baseline: 1.0031x; 1.0031x over previous
import os
import math
import numpy as np

import concourse.bass as bass
import concourse.mybir as mybir
from concourse.bass_utils import run_bass_kernel_spmd

F32 = mybir.dt.float32
BF16 = mybir.dt.bfloat16
AF = mybir.ActivationFunctionType
ALU = mybir.AluOpType
AX = mybir.AxisListType

NCORE = 8
S = 8192
D = 2048
T = 1024
HD = 128
NH = 16
KVH = 4
NE = 8
NI = 1408
NIC = 11
DC = 16
TCH = 16
SCALE = 1.0 / math.sqrt(HD)
LN_EPS = 1e-5
RMS_EPS = 1e-6
ROPE_THETA = 1.0e6

TWO_PI = float(2.0 * np.pi)
INV_2PI = float(1.0 / TWO_PI)
MAGIC = float(1.5 * 2.0 ** 23)
CW1 = float(np.float32(6.28125))
CW2 = float(np.float32(1.9353628e-03))
CW3 = float(TWO_PI - np.float64(np.float32(CW1)) - np.float64(np.float32(CW2)))

DBG = os.environ.get("KDBG", "")


class Pl:
    ENGS = ("sync", "gpsimd", "vector", "scalar", "tensor")

    def __init__(self, nc):
        self.nc = nc
        self.ops = {e: [] for e in self.ENGS}
        self.meta = {e: [] for e in self.ENGS}
        self.sems = {}
        self.cnt = {}
        self.lastw = {e: {} for e in self.ENGS}

    def s(self, name):
        if name not in self.sems:
            self.sems[name] = self.nc.alloc_semaphore(name)
            self.cnt[name] = 0
        return self.sems[name]

    def wait(self, eng, ev):
        if ev is None:
            return
        if isinstance(ev, (list, tuple)) and (len(ev) != 2 or not isinstance(ev[0], str)):
            for e2 in ev:
                self.wait(eng, e2)
            return
        name, val = ev
        if self.lastw[eng].get(name, -1) >= val:
            return
        self.lastw[eng][name] = val
        h = self.s(name)
        self.meta[eng].append(("w", name, val))
        self.ops[eng].append(lambda e, h=h, v=val: e.wait_ge(h, v))

    def emit(self, eng, fn, waits=(), inc=None, n=1):
        for w in waits:
            self.wait(eng, w)
        if inc is None:
            self.ops[eng].append(fn)
            self.meta[eng].append(("i", None, 0))
            return None
        h = self.s(inc)
        self.cnt[inc] += n
        self.ops[eng].append(lambda e, fn=fn, h=h, n=n: fn(e).then_inc(h, n))
        self.meta[eng].append(("i", inc, n))
        return (inc, self.cnt[inc])

    def pe(self, fn, waits=(), inc="pe"):
        return self.emit("tensor", fn, waits, inc)

    def ve(self, fn, waits=(), inc="dve"):
        return self.emit("vector", fn, waits, inc)

    def sc(self, fn, waits=(), inc="act"):
        return self.emit("scalar", fn, waits, inc)

    def gp(self, fn, waits=(), inc="pool"):
        return self.emit("gpsimd", fn, waits, inc)

    def dma(self, eng, fn, waits=(), inc="ld"):
        return self.emit(eng, fn, waits, inc, n=16)

    def check_deadlock(self):
        pos = {e: 0 for e in self.ENGS}
        val = {name: 0 for name in self.cnt}
        progress = True
        while progress:
            progress = False
            for e in self.ENGS:
                while pos[e] < len(self.meta[e]):
                    kind, name, v = self.meta[e][pos[e]]
                    if kind == "w":
                        if val.get(name, 0) < v:
                            break
                    else:
                        if name is not None:
                            val[name] = val.get(name, 0) + v
                    pos[e] += 1
                    progress = True
        stuck = {e: (pos[e], len(self.meta[e])) for e in self.ENGS
                 if pos[e] < len(self.meta[e])}
        if stuck:
            detail = {}
            for e in stuck:
                kind, name, v = self.meta[e][pos[e]]
                detail[e] = (pos[e], "waits", name, v, "cur", val.get(name, 0))
            raise AssertionError(f"DEADLOCK in plan: {detail}")

    def coll(self, fn, waits=()):
        for w in waits:
            self.wait("gpsimd", w)
        h = self.s("cc")
        self.cnt["cc"] += 1
        self.ops["gpsimd"].append(lambda e, fn=fn, h=h: fn(e).then_inc(h))
        self.meta["gpsimd"].append(("i", "cc", 1))
        return ("cc", self.cnt["cc"])


def _build():
    nc = bass.Bass("TRN2", target_bir_lowering=False)

    def din(name, shape):
        return nc.dram_tensor(name, shape, F32, kind="ExternalInput").ap()

    x_sh = din("x_sh", [128, 2 * DC * 512])
    wq_my = din("wq_my", [2, 128, DC * HD])
    wk_my = din("wk_my", [2, 128, DC * HD])
    wv_my = din("wv_my", [2, 128, DC * HD])
    b3_my = din("b3_my", [HD, 8])
    wo_my = din("wo_my", [2, DC, HD, HD])
    bo_t = din("bo_t", [HD, DC])
    q2_my = din("q2_my", [2, 128, DC * HD])
    k2_my = din("k2_my", [1, 128, DC * HD])
    v2_my = din("v2_my", [1, 128, DC * HD])
    o2_my = din("o2_my", [2, DC, HD, HD])
    qknw = din("qknw", [HD, 2])
    anw_t = din("anw_t", [HD, DC])
    fnw_t = din("fnw_t", [HD, DC])
    gw_t = din("gw_t", [D, NE])
    wg_p = din("wg_p", [NE, NIC, D, HD])
    wu_p = din("wu_p", [NE, NIC, D, HD])
    wd_p = din("wd_p", [NE, DC, NI, HD])

    out_t = nc.dram_tensor("out_t", [D, T], F32, kind="ExternalOutput").ap()
    dbg_out = {}
    if DBG:
        for nm in ("d_x1", "d_h1", "d_F"):
            dbg_out[nm] = nc.dram_tensor(nm, [D, T], F32, kind="ExternalOutput").ap()
        dbg_out["d_comb"] = nc.dram_tensor("d_comb", [128, 64], F32,
                                           kind="ExternalOutput").ap()
    if DBG == "2":
        dbg_out["d_xg1"] = nc.dram_tensor("d_xg1", [NCORE * 128, 16384],
                                          BF16, kind="ExternalOutput").ap()
        for nm in ("d_q", "d_k", "d_v"):
            dbg_out[nm] = nc.dram_tensor(nm, [128, 8192], BF16,
                                         kind="ExternalOutput").ap()
        dbg_out["d_ao"] = nc.dram_tensor("d_ao", [128, 16384], BF16,
                                         kind="ExternalOutput").ap()
        dbg_out["d_rs1o"] = nc.dram_tensor("d_rs1o", [D, T], BF16,
                                           kind="ExternalOutput").ap()
        dbg_out["d_m1"] = nc.dram_tensor("d_m1", [128, 16384], F32,
                                         kind="ExternalOutput").ap()
        dbg_out["d_x1b"] = nc.dram_tensor("d_x1b", [128, 16384], F32,
                                          kind="ExternalOutput").ap()
        for nm in ("d_q2", "d_k2", "d_v2"):
            dbg_out[nm] = nc.dram_tensor(nm, [128, 8192], BF16,
                                         kind="ExternalOutput").ap()
        dbg_out["d_ao2"] = nc.dram_tensor("d_ao2", [128, 16384], BF16,
                                          kind="ExternalOutput").ap()
        dbg_out["d_rs2o"] = nc.dram_tensor("d_rs2o", [D, T], BF16,
                                           kind="ExternalOutput").ap()
        dbg_out["d_fb"] = nc.dram_tensor("d_fb", [128, 16384], F32,
                                         kind="ExternalOutput").ap()
        dbg_out["d_hb"] = nc.dram_tensor("d_hb", [128, 16384], BF16,
                                         kind="ExternalOutput").ap()
        dbg_out["d_lnbc"] = nc.dram_tensor("d_lnbc", [128, 2048], F32,
                                           kind="ExternalOutput").ap()
        dbg_out["d_rows"] = nc.dram_tensor("d_rows", [128, 1024], F32,
                                           kind="ExternalOutput").ap()
        dbg_out["d_ps"] = nc.dram_tensor("d_ps", [4, 2048], F32,
                                         kind="ExternalOutput").ap()
        dbg_out["d_trig"] = nc.dram_tensor("d_trig", [128, 1024], F32,
                                           kind="ExternalOutput").ap()

    xg1_loc = nc.dram_tensor("xg1_loc", [128, 16384], BF16, kind="Internal")
    xg1 = nc.dram_tensor("xg1", [NCORE * 128, 16384], BF16, kind="Internal",
                         addr_space="Shared")
    xg2_loc = nc.dram_tensor("xg2_loc", [128, 16384], BF16, kind="Internal")
    xg2 = nc.dram_tensor("xg2", [NCORE * 128, 16384], BF16, kind="Internal",
                         addr_space="Shared")
    rs1_loc = nc.dram_tensor("rs1_loc", [NCORE * D, T], BF16, kind="Internal")
    rs1_out = nc.dram_tensor("rs1_out", [D, T], BF16, kind="Internal")
    rs2_loc = nc.dram_tensor("rs2_loc", [NCORE * D, T], BF16, kind="Internal")
    rs2_out = nc.dram_tensor("rs2_out", [D, T], BF16, kind="Internal")
    x1_dram = nc.dram_tensor("x1_dram", [D, T], F32, kind="Internal")
    h1_dram = nc.dram_tensor("h1_dram", [D, T], F32, kind="Internal")

    def sb(name, shape, dt):
        return nc.alloc_sbuf_tensor(name, shape, dt)

    rbuf = sb("rbuf", [128, 16 * 1024], F32)
    hbuf = sb("hbuf", [128, 16 * 1024], BF16)
    qbuf = sb("qbuf", [128, 8192], BF16)
    kbuf = sb("kbuf", [128, 8192], BF16)
    vbuf = sb("vbuf", [128, 8192], BF16)
    aobuf = sb("aobuf", [128, 16 * 1024], BF16)
    pbuf = sb("pbuf", [128, 8 * 512], BF16)
    oslab = sb("oslab", [128, 32 * 128], BF16)
    ones_f = sb("ones_f", [128, 128], F32)
    ident_f = sb("ident_f", [128, 128], F32)
    ones_b = sb("ones_b", [128, 8], BF16)
    freqc = sb("freqc", [128, 4], F32)
    cols = sb("cols", [128, 64], F32)
    bias3 = sb("bias3", [128, 8], F32)
    bo_sb = sb("bo_sb", [128, DC], F32)
    anw_sb = sb("anw_sb", [128, DC], F32)
    fnw_sb = sb("fnw_sb", [128, DC], F32)
    qknw_sb = sb("qknw_sb", [128, 2], F32)
    gwt_sb = sb("gwt_sb", [128, 128], F32)
    whi_sb = sb("whi_sb", [128, 128], BF16)
    wlo_sb = sb("wlo_sb", [128, 128], BF16)
    comb_all = sb("comb_all", [128, 64], F32)
    rows = sb("rows", [128, 1024], F32)
    wgsl = sb("wgsl", [128, 2 * 2048], BF16)

    rem = nc.sbuf_bytes_remaining
    assert rem >= 0, f"sbuf over budget by {-rem}"

    def addr(h):
        return nc.lookup_mloc(h).addr

    def at(name, base, off, shape, dt):
        return nc.alloc_sbuf_tensor_at(name, shape, dt, offset=base + off)

    hb0 = addr(hbuf)
    pn = at("pn", hb0, 0, [128, 512], F32)
    qn = at("qn", hb0, 2048, [128, 512], F32)
    rb = at("rb", hb0, 4096, [128, 512], F32)
    sqt = at("sqt", hb0, 6144, [128, 512], BF16)
    tbv = at("tbv", hb0, 7168, [128, 512], F32)
    fv = at("fv", hb0, 9216, [128, 512], F32)
    auxv = at("auxv", hb0, 11264, [128, 512], F32)
    sinv = at("sinv", hb0, 13312, [128, 512], F32)
    cosv = at("cosv", hb0, 15360, [128, 512], F32)
    stv = at("stv", hb0, 17408, [128, 3 * 512], BF16)
    swp = at("swp", hb0, 17408, [128, 512], F32)
    lnbc = at("lnbc", hb0, 20480, [128, 2 * 1024], F32)
    hscr = at("hscr", hb0, 28672, [128, 2 * 1024], BF16)
    pnB = at("pnB", hb0, 20480, [128, 512], F32)
    qnB = at("qnB", hb0, 22528, [128, 512], F32)
    auxvB = at("auxvB", hb0, 24576, [128, 512], F32)
    tbvB = at("tbvB", hb0, 26624, [128, 512], F32)
    sqtB = at("sqtB", hb0, 19456, [128, 512], BF16)

    qb0, kb0, vb0, pb0, ob0, rb0 = (addr(qbuf), addr(kbuf), addr(vbuf),
                                    addr(pbuf), addr(oslab), addr(rbuf))
    xst = at("xst", qb0, 0, [128, 2 * 1024], F32)
    rsst = at("rsst", kb0, 0, [128, 2 * 1024], BF16)
    hilo = at("hilo", kb0, 4096, [128, 2 * 1024], BF16)
    h1ev = at("h1ev", vb0, 0, [128, 2 * 1024], F32)
    hsqt = at("hsqt", vb0, 8192, [128, 1024], BF16)
    rstbc = at("rstbc", vb0, 10240, [128, 1024], F32)
    mescr = at("mescr", kb0, 0, [128, 512], F32)
    silscr = at("silscr", kb0, 2048, [128, 512], BF16)
    cbc = at("cbc", vb0, 0, [128, 1024], F32)
    wusl = at("wusl", pb0, 0, [128, 2 * 2048], BF16)
    wdsl = at("wdsl", ob0, 0, [128, 2 * 1408], BF16)
    xsb = at("xsb", rb0, 0, [128, 2 * 8192], BF16)
    combT = at("combT", addr(aobuf), 24576, [8, 1024], F32)
    aslab = at("aslab", rb0, 32768, [128, 6 * 2048], BF16)
    sinvB = at("sinvB", rb0, 57344, [128, 512], F32)
    cosvB = at("cosvB", rb0, 59392, [128, 512], F32)
    swpB = at("swpB", rb0, 61440, [128, 512], F32)

    ps = [nc.alloc_psum_tensor(f"ps{i}", [128, 512], F32) for i in range(8)]

    P = Pl(nc)
    st = {}

    def dbg_snap(pairs):
        allw = [(name, P.cnt[name]) for name in list(P.cnt)]
        ev = None
        for (o, i_) in pairs:
            ev = P.dma("sync", lambda e, o=o, i_=i_: e.dma_start(out=o, in_=i_),
                       waits=allw, inc="dbg")
        for eng in Pl.ENGS:
            P.wait(eng, ev)

    def dump_banks(row, bounce):
        allw = [(name, P.cnt[name]) for name in list(P.cnt)]
        prev = None
        for b in range(8):
            prev = P.ve(lambda e, b=b: e.tensor_copy(
                bounce[0:1, 512 * (b % 4):512 * (b % 4 + 1)],
                ps[b][0:1, 0:512]), waits=allw + [prev])
            if b % 4 == 3:
                prev = P.dma("sync", lambda e, b=b, row=row: e.dma_start(
                    out=dbg_out["d_ps"][row + b // 4:row + b // 4 + 1, :],
                    in_=bounce[0:1, 0:2048]), waits=[prev], inc="dbg")
        for eng in Pl.ENGS:
            P.wait(eng, prev)

    ev_b3 = P.dma("sync", lambda e: e.dma_start(out=bias3[:, 0:6], in_=b3_my[:, 0:6]))
    ev_bo = P.dma("sync", lambda e: e.dma_start(out=bo_sb[:, :], in_=bo_t[:, :]),
                  waits=[ev_b3])
    ev_anw = P.dma("sync", lambda e: e.dma_start(out=anw_sb[:, :], in_=anw_t[:, :]),
                   waits=[ev_bo])
    ev_fnw = P.dma("sync", lambda e: e.dma_start(out=fnw_sb[:, :], in_=fnw_t[:, :]),
                   waits=[ev_anw])
    ev_qknw = P.dma("sync", lambda e: e.dma_start(out=qknw_sb[:, :], in_=qknw[:, :]),
                    waits=[ev_fnw])
    ev_gw = P.dma("sync", lambda e: e.dma_start(
        out=gwt_sb[:, :].rearrange("p (c f) -> p c f", c=DC),
        in_=gw_t[:, :].rearrange("(c p) f -> p c f", p=128)), waits=[ev_qknw])
    ev_one = P.gp(lambda e: e.memset(ones_f[:, :], 1.0))
    ev_oneb = P.gp(lambda e: e.memset(ones_b[:, :], 1.0))
    ev_id0 = P.gp(lambda e: e.memset(ident_f[:, :], 1.0), waits=[ev_one])
    ev_idf = P.gp(lambda e: e.affine_select(
        out=ident_f[:, :], in_=ident_f[:, :], pattern=[[-1, 128]],
        compare_op=ALU.is_equal, fill=0.0, base=0, channel_multiplier=1),
        waits=[ev_id0])
    ev_j = P.gp(lambda e: e.iota(freqc[:, 1:2], pattern=[[1, 1]], base=0,
                                 channel_multiplier=1,
                                 allow_small_or_imprecise_dtypes=True))
    ev_j = P.ve(lambda e: e.tensor_scalar(
        freqc[:, 2:3], freqc[:, 1:2], 64.0, None, op0=ALU.is_ge), waits=[ev_j])
    ev_j = P.ve(lambda e: e.scalar_tensor_tensor(
        freqc[:, 1:2], freqc[:, 2:3], -64.0, freqc[:, 1:2],
        op0=ALU.mult, op1=ALU.add), waits=[ev_j])
    ev = P.ve(lambda e: e.memset(freqc[:, 0:1], 1.0), waits=[ev_j])
    for i in range(5, -1, -1):
        r_i = float(np.float32(1.0 / (ROPE_THETA ** ((2 ** i) / 64.0))))
        ev = P.ve(lambda e, i=i: e.tensor_scalar(
            freqc[:, 2:3], freqc[:, 1:2], float(2 ** i), None, op0=ALU.is_ge),
            waits=[ev])
        ev = P.ve(lambda e, i=i: e.scalar_tensor_tensor(
            freqc[:, 1:2], freqc[:, 2:3], float(-(2 ** i)), freqc[:, 1:2],
            op0=ALU.mult, op1=ALU.add), waits=[ev])
        ev = P.ve(lambda e, r_i=r_i: e.tensor_scalar(
            freqc[:, 2:3], freqc[:, 2:3], float(r_i - 1.0), 1.0,
            op0=ALU.mult, op1=ALU.add), waits=[ev])
        ev = P.ve(lambda e: e.tensor_tensor(
            freqc[:, 0:1], freqc[:, 0:1], freqc[:, 2:3], op=ALU.mult), waits=[ev])
    ev = P.ve(lambda e: e.tensor_scalar(
        freqc[0:64, 0:1], freqc[0:64, 0:1], -1.0, None, op0=ALU.mult), waits=[ev])
    ev_freq = ev

    ev_x0st = P.dma("gpsimd", lambda e: e.dma_start(
        out=xg1_loc.ap(), in_=x_sh[:, :]), inc="xld")
    ev_cc1 = P.coll(lambda e: e.collective_compute(
        "AllGather", ALU.bypass, replica_groups=[list(range(NCORE))],
        ins=[xg1_loc.ap().opt()], outs=[xg1.ap().opt()]), waits=[ev_x0st])

    slab_src1 = [wq_my, wq_my, wk_my, wk_my, wv_my, wv_my]
    ev_wa = None
    for si in range(6):
        ev_wa = P.dma("gpsimd", lambda e, si=si, src=slab_src1[si], hs=si % 2:
                      e.dma_start(
            out=aslab[:, 2048 * si:2048 * (si + 1)],
            in_=src[hs]), inc="wld")

    st["xs_war"] = [None, None]
    st["pslot_war"] = [None] * 8

    def xstream(gx, c, cc_ev):
        par = c % 2
        r, h = c // 2, c % 2
        src = gx.ap()[128 * r:128 * (r + 1), 8192 * h:8192 * (h + 1)]
        eng = "sync" if par == 0 else "scalar"
        ev = P.dma(eng, lambda e, par=par, src=src: e.dma_start(
            out=xsb[:, 8192 * par:8192 * (par + 1)], in_=src),
            waits=[st["xs_war"][par], cc_ev, st.get("ln_done")],
            inc=f"xs{par}")
        return ev, par

    def xslice(par, d):
        return xsb[:, 8192 * par + 512 * d: 8192 * par + 512 * (d + 1)]

    def rope_tables(c):
        pp = c % 2
        sv = sinv if pp == 0 else sinvB
        cv = cosv if pp == 0 else cosvB
        av_ = auxv if pp == 0 else auxvB
        tb = tbv if pp == 0 else tbvB
        ev = P.gp(lambda e, c=c, tb=tb: e.iota(
            tb[:, :], pattern=[[1, 512]], base=512 * c,
            channel_multiplier=0,
            allow_small_or_imprecise_dtypes=True),
            waits=[st.get(f"rope_war{pp}"), st.get(f"tbv_war{pp}"),
                   st.get("ln_done")])
        ev = P.ve(lambda e, tb=tb: e.tensor_scalar(
            fv[:, :], tb[:, :], freqc[:, 0:1], None, op0=ALU.mult),
            waits=[ev, ev_freq, st.get("fv_war")])
        ev = P.ve(lambda e, av_=av_: e.tensor_scalar(
            av_[:, :], fv[:, :], INV_2PI, MAGIC, op0=ALU.mult, op1=ALU.add),
            waits=[ev, st.get(f"aux_war{pp}")])
        ev = P.ve(lambda e, av_=av_: e.tensor_scalar(
            av_[:, :], av_[:, :], MAGIC, None, op0=ALU.subtract), waits=[ev])
        first = True
        for cw in (CW1, CW2, CW3):
            src = fv if first else sv
            ev = P.ve(lambda e, cw=cw, src=src, sv=sv, av_=av_:
                      e.scalar_tensor_tensor(
                sv[:, :], av_[:, :], -cw, src[:, :], op0=ALU.mult, op1=ALU.add),
                waits=[ev, st.get(f"sin_war{pp}") if first else None])
            if first:
                st["fv_war"] = ev
            first = False
        ev = P.ve(lambda e, sv=sv: e.tensor_scalar(
            sv[:, :], sv[:, :], float(np.pi), float(-np.pi),
            op0=ALU.min, op1=ALU.max), waits=[ev])
        ev_r = ev
        ev = P.ve(lambda e, sv=sv, cv=cv: e.tensor_scalar(
            cv[:, :], sv[:, :], float(np.pi / 2), float(np.pi),
            op0=ALU.add, op1=ALU.is_gt),
            waits=[ev, st.get(f"cos_war{pp}")])
        ev = P.ve(lambda e, sv=sv, cv=cv: e.scalar_tensor_tensor(
            cv[:, :], cv[:, :], -TWO_PI, sv[:, :], op0=ALU.mult, op1=ALU.add),
            waits=[ev])
        ev = P.ve(lambda e, cv=cv: e.tensor_scalar(
            cv[:, :], cv[:, :], float(np.pi / 2), None, op0=ALU.add),
            waits=[ev])
        ev = P.ve(lambda e, cv=cv: e.tensor_scalar(
            cv[:, :], cv[:, :], float(np.pi), float(-np.pi),
            op0=ALU.min, op1=ALU.max), waits=[ev])
        ev_cos = P.sc(lambda e, cv=cv: e.activation(cv[:, :], cv[:, :], AF.Sin),
                      waits=[ev])
        ev_sin = P.sc(lambda e, sv=sv: e.activation(sv[:, :], sv[:, :], AF.Sin),
                      waits=[ev_r, ev])
        return ev_sin, ev_cos

    def proj_pass(att, hs, gx, cc_ev, prev_done):
        if att == 1:
            parts = [("q", hs, qbuf), ("k", 2 + hs, kbuf), ("v", 4 + hs, vbuf)]
            slab_wait = ev_wa
        elif hs == 0:
            parts = [("q", 0, qbuf), ("k", 2, kbuf), ("v", 3, vbuf)]
            slab_wait = st["ev_w2"]
        else:
            parts = [("q", 1, qbuf)]
            slab_wait = st["ev_w2"]
        rows_off = {"q": 0, "k": 16}
        ssq_red = {"q": [None] * TCH, "k": [None] * TCH}
        pn_war = [st.get("pn_war0"), st.get("pn_war1")]
        qn_war = [st.get("qn_war0"), st.get("qn_war1")]
        sq_war = [st.get("sq_war0"), st.get("sq_war1")]
        for c in range(TCH):
            pp = c % 2
            pn_p = pn if pp == 0 else pnB
            qn_p = qn if pp == 0 else qnB
            sqt_p = sqt if pp == 0 else sqtB
            swp_p = swp if pp == 0 else swpB
            sv_p = sinv if pp == 0 else sinvB
            cv_p = cosv if pp == 0 else cosvB
            av_p = auxv if pp == 0 else auxvB
            tb_p = tbv if pp == 0 else tbvB
            rrow = rows[32:33, 0:512] if pp == 0 else rows[64:65, 0:512]
            orow = ones_f[32:33, 0:128] if pp == 0 else ones_f[64:65, 0:128]
            p6row = ps[6][0:1, 0:512] if pp == 0 else ps[6][32:33, 0:512]
            ev_x, par = xstream(gx, c, cc_ev)
            tab_evs = rope_tables(c) if att == 2 else None
            last_x_read = None
            for (pname, slab_i, buf) in parts:
                bank = ps[{"q": 0, "k": 2, "v": 4}[pname] + pp]
                bank_war = st.get(f"bk_{pname}{pp}")
                mm = None
                for d in range(DC):
                    mm = P.pe(lambda e, bank=bank, slab_i=slab_i, par=par, d=d:
                              e.matmul(bank[0:128, 0:512],
                                       aslab[:, 2048 * slab_i + 128 * d:
                                             2048 * slab_i + 128 * (d + 1)],
                                       xslice(par, d),
                                       start=(d == 0), stop=(d == DC - 1)),
                              waits=[ev_x, slab_wait, bank_war, prev_done]
                              if d == 0 else (),
                              inc="pe" if d == DC - 1 else None)
                last_x_read = mm
                if att == 1:
                    bofs = {"q": 0, "k": 2, "v": 4}[pname] + hs
                    if pname in ("q", "k"):
                        evv = P.ve(lambda e, bank=bank, buf=buf, c=c, bofs=bofs:
                                   e.tensor_scalar(buf[:, 512 * c:512 * (c + 1)],
                                                   bank[0:128, 0:512],
                                                   bias3[:, bofs:bofs + 1], None,
                                                   op0=ALU.add),
                                   waits=[mm, ev_b3])
                        st[f"bk_{pname}{pp}"] = evv
                        buf_ev = evv
                    else:
                        evv = P.ve(lambda e, bank=bank, pn_p=pn_p, bofs=bofs:
                                   e.tensor_scalar(pn_p[:, :], bank[0:128, 0:512],
                                                   bias3[:, bofs:bofs + 1], None,
                                                   op0=ALU.add),
                                   waits=[mm, ev_b3, pn_war[pp]])
                        trs = None
                        for sub in range(4):
                            trs = P.pe(lambda e, bank=bank, pn_p=pn_p, sub=sub:
                                       e.transpose(
                                bank[0:128, 128 * sub:128 * (sub + 1)],
                                pn_p[:, 128 * sub:128 * (sub + 1)], ident_f[:, :]),
                                waits=[evv, ev_idf] if sub == 0 else (),
                                inc="pe" if sub == 3 else None)
                        vcp = P.ve(lambda e, bank=bank, c=c: e.tensor_copy(
                            vbuf[:, 512 * c:512 * (c + 1)], bank[0:128, 0:512]),
                            waits=[trs])
                        st[f"bk_v{pp}"] = vcp
                        pn_war[pp] = trs
                        buf_ev = vcp
                else:
                    if pname in ("q", "k"):
                        dst = pn_p if pname == "q" else qn_p
                        dst2 = qn_p if pname == "q" else pn_p
                        dwar = pn_war[pp] if pname == "q" else qn_war[pp]
                        evv = P.ve(lambda e, bank=bank, dst=dst: e.tensor_copy(
                            dst[:, :], bank[0:128, 0:512]),
                            waits=[mm, dwar, st.get("ln_done")])
                        evs = P.ve(lambda e, dst=dst, sqt_p=sqt_p: e.tensor_mul(
                            sqt_p[:, :], dst[:, :], dst[:, :]),
                            waits=[evv, sq_war[pp], st.get("ln_done")])
                        ssq_mm = P.pe(lambda e, p6row=p6row, sqt_p=sqt_p:
                                      e.matmul(
                            p6row, ones_b[:, 0:1], sqt_p[:, :],
                            start=True, stop=True),
                            waits=[evs, ev_oneb, st.get(f"ps6r{pp}")])
                        sq_war[pp] = ssq_mm
                        evr = P.ve(lambda e, rrow=rrow, p6row=p6row:
                                   e.tensor_scalar(
                            rrow, p6row, 1.0 / HD,
                            RMS_EPS, op0=ALU.mult, op1=ALU.add),
                            waits=[ssq_mm, st.get(f"rows3264_war{pp}")])
                        st[f"ps6r{pp}"] = evr
                        evl = P.sc(lambda e, rrow=rrow: e.activation(
                            rrow, rrow, AF.Ln), waits=[evr])
                        evx = P.sc(lambda e, rrow=rrow: e.activation(
                            rrow, rrow, AF.Exp, scale=-0.5), waits=[evl])
                        evb = P.pe(lambda e, bank=bank, orow=orow, rrow=rrow:
                                   e.matmul(
                            bank[0:128, 0:512], orow, rrow,
                            start=True, stop=True),
                            waits=[evx, ev_one, evv])
                        st[f"rows3264_war{pp}"] = evb
                        evrb = P.ve(lambda e, bank=bank: e.tensor_copy(
                            rb[:, :], bank[0:128, 0:512]),
                            waits=[evb, st.get("rb_war")])
                        st[f"bk_{pname}{pp}"] = evrb
                        nwcol = qknw_sb[:, 0:1] if pname == "q" else qknw_sb[:, 1:2]
                        d2war = qn_war[pp] if pname == "q" else pn_war[pp]
                        evn = P.ve(lambda e, dst=dst, dst2=dst2, nwcol=nwcol:
                                   e.scalar_tensor_tensor(
                                       dst2[:, :], dst[:, :], nwcol, rb[:, :],
                                       op0=ALU.mult, op1=ALU.mult),
                                   waits=[evrb, ev_qknw, d2war])
                        st["rb_war"] = evn
                        evs_t, evc_t = tab_evs
                        s1 = P.gp(lambda e, dst2=dst2, swp_p=swp_p: e.tensor_copy(
                            swp_p[0:64, :], dst2[64:128, :]),
                            waits=[evn, st.get(f"swp_war{pp}"),
                                   st.get("ln_done")])
                        s2 = P.gp(lambda e, dst2=dst2, swp_p=swp_p: e.tensor_copy(
                            swp_p[64:128, :], dst2[0:64, :]), waits=[s1])
                        m1 = P.gp(lambda e, dst2=dst2, av_p=av_p, cv_p=cv_p:
                                  e.tensor_tensor(
                            av_p[:, :], dst2[:, :], cv_p[:, :], op=ALU.mult),
                            waits=[evn, evc_t, st.get(f"aux_war{pp}")])
                        m2 = P.gp(lambda e, swp_p=swp_p, tb_p=tb_p, sv_p=sv_p:
                                  e.tensor_tensor(
                            tb_p[:, :], swp_p[:, :], sv_p[:, :], op=ALU.mult),
                            waits=[s2, evs_t, st.get(f"tbv_war{pp}")])
                        o2 = P.gp(lambda e, buf=buf, c=c, av_p=av_p, tb_p=tb_p:
                                  e.tensor_tensor(
                            buf[:, 512 * c:512 * (c + 1)], av_p[:, :],
                            tb_p[:, :], op=ALU.add), waits=[m1, m2])
                        st[f"swp_war{pp}"] = m2
                        st[f"aux_war{pp}"] = o2
                        st[f"tbv_war{pp}"] = o2
                        st[f"rope_war{pp}"] = o2
                        st[f"sin_war{pp}"] = o2
                        st[f"cos_war{pp}"] = o2
                        if pname == "q":
                            qn_war[pp] = [qn_war[pp], o2]
                            pn_war[pp] = [pn_war[pp], o2]
                        else:
                            pn_war[pp] = [pn_war[pp], o2]
                            qn_war[pp] = [qn_war[pp], o2]
                        buf_ev = o2
                    else:
                        evv = P.ve(lambda e, bank=bank, pn_p=pn_p: e.tensor_copy(
                            pn_p[:, :], bank[0:128, 0:512]),
                            waits=[mm, pn_war[pp], st.get("ln_done")])
                        trs = None
                        for sub in range(4):
                            trs = P.pe(lambda e, bank=bank, pn_p=pn_p, sub=sub:
                                       e.transpose(
                                bank[0:128, 128 * sub:128 * (sub + 1)],
                                pn_p[:, 128 * sub:128 * (sub + 1)], ident_f[:, :]),
                                waits=[evv, ev_idf] if sub == 0 else (),
                                inc="pe" if sub == 3 else None)
                        vcp = P.ve(lambda e, bank=bank, c=c: e.tensor_copy(
                            vbuf[:, 512 * c:512 * (c + 1)], bank[0:128, 0:512]),
                            waits=[trs])
                        st[f"bk_v{pp}"] = vcp
                        pn_war[pp] = trs
                        buf_ev = vcp
                if pname in ("q", "k"):
                    roff = rows_off[pname]
                    evs2 = P.ve(lambda e, buf=buf, c=c, sqt_p=sqt_p: e.tensor_mul(
                        sqt_p[:, :], buf[:, 512 * c:512 * (c + 1)],
                        buf[:, 512 * c:512 * (c + 1)]), waits=[buf_ev, sq_war[pp]])
                    smm = P.pe(lambda e, p6row=p6row, sqt_p=sqt_p: e.matmul(
                        p6row, ones_b[:, 0:1], sqt_p[:, :],
                        start=True, stop=True),
                        waits=[evs2, ev_oneb, st.get(f"ps6r{pp}")])
                    sq_war[pp] = smm
                    red = P.ve(lambda e, roff=roff, c=c, p6row=p6row:
                               e.tensor_reduce(
                        rows[0:1, roff + c:roff + c + 1], p6row,
                        axis=AX.X, op=ALU.max), waits=[smm, st.get("rows0_war")])
                    st[f"ps6r{pp}"] = red
                    ssq_red[pname][c] = red
            st["xs_war"][par] = last_x_read
        st["pn_war0"], st["pn_war1"] = pn_war
        st["qn_war0"], st["qn_war1"] = qn_war
        st["sq_war0"], st["sq_war1"] = sq_war
        return ssq_red

    def proj_pass2(gx, cc_ev, prev_done):
        parts = ["q", "k", "v"]
        slabs = {"q": 0, "k": 2, "v": 3}
        slab_wait = st["ev_w2"]
        rows_off = {"q": 0, "k": 16, "q1": 34}
        ssq_red = {"q": [None] * TCH, "k": [None] * TCH, "q1": [None] * TCH}
        pn_war = [st.get("pn_war0"), st.get("pn_war1")]
        qn_war = [st.get("qn_war0"), st.get("qn_war1")]
        sq_war = [st.get("sq_war0"), st.get("sq_war1")]

        def bufs(c):
            pp = c % 2
            return {
                "pp": pp,
                "pn": pn if pp == 0 else pnB,
                "qn": qn if pp == 0 else qnB,
                "sqt": sqt if pp == 0 else sqtB,
                "swp": swp if pp == 0 else swpB,
                "sv": sinv if pp == 0 else sinvB,
                "cv": cosv if pp == 0 else cosvB,
                "av": auxv if pp == 0 else auxvB,
                "tb": tbv if pp == 0 else tbvB,
                "rrow": rows[32:33, 0:512] if pp == 0 else rows[64:65, 0:512],
                "orow": ones_f[32:33, 0:128] if pp == 0 else ones_f[64:65, 0:128],
                "p6": ps[6][0:1, 0:512] if pp == 0 else ps[6][32:33, 0:512],
            }

        def s1(c):
            B = bufs(c)
            pp = B["pp"]
            ev_x, par = xstream(gx, c, cc_ev)
            tab_evs = rope_tables(c)
            dat = {"B": B, "tab": tab_evs, "par": par, "c": c}
            last = None
            for pname in parts:
                bank = ps[{"q": 0, "k": 2, "v": 4}[pname] + pp]
                bank_war = st.get(f"bk_{pname}{pp}")
                mm = None
                for d in range(DC):
                    mm = P.pe(lambda e, bank=bank, si=slabs[pname], par=par, d=d:
                              e.matmul(bank[0:128, 0:512],
                                       aslab[:, 2048 * si + 128 * d:
                                             2048 * si + 128 * (d + 1)],
                                       xslice(par, d),
                                       start=(d == 0), stop=(d == DC - 1)),
                              waits=[ev_x, slab_wait, bank_war, prev_done]
                              if d == 0 else (),
                              inc="pe" if d == DC - 1 else None)
                dat[f"mm_{pname}"] = mm
                last = mm
            st["xs_war"][par] = last
            evv = P.ve(lambda e, B=B, pp=pp: e.tensor_copy(
                B["pn"][:, :], ps[0 + pp][0:128, 0:512]),
                waits=[dat["mm_q"], pn_war[pp], st.get("ln_done")])
            evs = P.ve(lambda e, B=B: e.tensor_mul(
                B["sqt"][:, :], B["pn"][:, :], B["pn"][:, :]),
                waits=[evv, sq_war[pp], st.get("ln_done")])
            ssq_mm = P.pe(lambda e, B=B: e.matmul(
                B["p6"], ones_b[:, 0:1], B["sqt"][:, :], start=True, stop=True),
                waits=[evs, ev_oneb, st.get(f"ps6r{pp}")])
            sq_war[pp] = ssq_mm
            dat["evv_q"] = evv
            dat["ssq_q"] = ssq_mm
            return dat

        def qk_tail(pname, dat, evv, ssq_mm, bank, relkey, outbuf, outoff):
            B = dat["B"]
            pp = B["pp"]
            c = dat["c"]
            kside = pname == "k"
            dst = B["qn"] if kside else B["pn"]
            dst2 = B["pn"] if kside else B["qn"]
            evr = P.ve(lambda e, B=B: e.tensor_scalar(
                B["rrow"], B["p6"], 1.0 / HD, RMS_EPS,
                op0=ALU.mult, op1=ALU.add),
                waits=[ssq_mm, st.get(f"rows3264_war{pp}")])
            st[f"ps6r{pp}"] = evr
            evl = P.sc(lambda e, B=B: e.activation(B["rrow"], B["rrow"], AF.Ln),
                       waits=[evr])
            evx = P.sc(lambda e, B=B: e.activation(B["rrow"], B["rrow"], AF.Exp,
                                                   scale=-0.5), waits=[evl])
            evb = P.pe(lambda e, bank=bank, B=B: e.matmul(
                bank[0:128, 0:512], B["orow"], B["rrow"],
                start=True, stop=True), waits=[evx, ev_one, evv])
            st[f"rows3264_war{pp}"] = evb
            evrb = P.ve(lambda e, bank=bank: e.tensor_copy(
                rb[:, :], bank[0:128, 0:512]), waits=[evb, st.get("rb_war")])
            st[relkey] = evrb
            nwcol = qknw_sb[:, 1:2] if kside else qknw_sb[:, 0:1]
            d2war = pn_war[pp] if kside else qn_war[pp]
            evn = P.ve(lambda e, dst=dst, dst2=dst2, nwcol=nwcol:
                       e.scalar_tensor_tensor(
                           dst2[:, :], dst[:, :], nwcol, rb[:, :],
                           op0=ALU.mult, op1=ALU.mult),
                       waits=[evrb, ev_qknw, d2war])
            st["rb_war"] = evn
            evs_t, evc_t = dat["tab"]
            s1_ = P.gp(lambda e, dst2=dst2, B=B: e.tensor_copy(
                B["swp"][0:64, :], dst2[64:128, :]),
                waits=[evn, st.get(f"swp_war{pp}"), st.get("ln_done")])
            s2_ = P.gp(lambda e, dst2=dst2, B=B: e.tensor_copy(
                B["swp"][64:128, :], dst2[0:64, :]), waits=[s1_])
            m1 = P.gp(lambda e, dst2=dst2, B=B: e.tensor_tensor(
                B["av"][:, :], dst2[:, :], B["cv"][:, :], op=ALU.mult),
                waits=[evn, evc_t, st.get(f"aux_war{pp}")])
            m2 = P.gp(lambda e, B=B: e.tensor_tensor(
                B["tb"][:, :], B["swp"][:, :], B["sv"][:, :], op=ALU.mult),
                waits=[s2_, evs_t, st.get(f"tbv_war{pp}")])
            o2 = P.gp(lambda e, c=c, B=B, outbuf=outbuf, outoff=outoff:
                      e.tensor_tensor(
                outbuf[:, outoff + 512 * c:outoff + 512 * (c + 1)],
                B["av"][:, :], B["tb"][:, :], op=ALU.add), waits=[m1, m2])
            st[f"swp_war{pp}"] = m2
            st[f"aux_war{pp}"] = o2
            st[f"tbv_war{pp}"] = o2
            st[f"rope_war{pp}"] = o2
            st[f"sin_war{pp}"] = o2
            st[f"cos_war{pp}"] = o2
            pn_war[pp] = [pn_war[pp], o2]
            qn_war[pp] = [qn_war[pp], o2]
            evs2 = P.ve(lambda e, outbuf=outbuf, outoff=outoff, c=c, B=B:
                        e.tensor_mul(
                B["sqt"][:, :], outbuf[:, outoff + 512 * c:outoff + 512 * (c + 1)],
                outbuf[:, outoff + 512 * c:outoff + 512 * (c + 1)]),
                waits=[o2, sq_war[pp]])
            smm = P.pe(lambda e, B=B: e.matmul(
                B["p6"], ones_b[:, 0:1], B["sqt"][:, :], start=True, stop=True),
                waits=[evs2, ev_oneb, st.get(f"ps6r{pp}")])
            sq_war[pp] = smm
            red = P.ve(lambda e, roff=rows_off[pname], c=c, B=B:
                       e.tensor_reduce(
                rows[0:1, roff + c:roff + c + 1], B["p6"],
                axis=AX.X, op=ALU.max), waits=[smm, st.get("rows0_war")])
            st[f"ps6r{pp}"] = red
            ssq_red[pname][c] = red

        def s2(dat):
            B = dat["B"]
            pp = B["pp"]
            c = dat["c"]
            par = dat["par"]
            qk_tail("q", dat, dat["evv_q"], dat["ssq_q"], ps[0 + pp],
                    f"bk_q{pp}", qbuf, 0)
            mmq1 = None
            for d in range(DC):
                mmq1 = P.pe(lambda e, par=par, d=d: e.matmul(
                    ps[7][0:128, 0:512],
                    aslab[:, 2048 * 1 + 128 * d:2048 * 1 + 128 * (d + 1)],
                    xslice(par, d),
                    start=(d == 0), stop=(d == DC - 1)),
                    waits=[st.get("ps7_war")] if d == 0 else (),
                    inc="pe" if d == DC - 1 else None)
            st["xs_war"][par] = mmq1
            evv1 = P.ve(lambda e, B=B: e.tensor_copy(
                B["pn"][:, :], ps[7][0:128, 0:512]),
                waits=[mmq1, pn_war[pp]])
            evs1 = P.ve(lambda e, B=B: e.tensor_mul(
                B["sqt"][:, :], B["pn"][:, :], B["pn"][:, :]),
                waits=[evv1, sq_war[pp]])
            ssq1 = P.pe(lambda e, B=B: e.matmul(
                B["p6"], ones_b[:, 0:1], B["sqt"][:, :], start=True, stop=True),
                waits=[evs1, ev_oneb, st.get(f"ps6r{pp}")])
            sq_war[pp] = ssq1
            qk_tail("q1", dat, evv1, ssq1, ps[7], "ps7_war", aobuf, 8192)
            evv = P.ve(lambda e, B=B, pp=pp: e.tensor_copy(
                B["qn"][:, :], ps[2 + pp][0:128, 0:512]),
                waits=[dat["mm_k"], qn_war[pp]])
            evs = P.ve(lambda e, B=B: e.tensor_mul(
                B["sqt"][:, :], B["qn"][:, :], B["qn"][:, :]),
                waits=[evv, sq_war[pp]])
            ssq_mm = P.pe(lambda e, B=B: e.matmul(
                B["p6"], ones_b[:, 0:1], B["sqt"][:, :],
                start=True, stop=True),
                waits=[evs, ev_oneb, st.get(f"ps6r{pp}")])
            sq_war[pp] = ssq_mm
            qk_tail("k", dat, evv, ssq_mm, ps[2 + pp], f"bk_k{pp}", kbuf, 0)
            bank = ps[4 + pp]
            evv_v = P.ve(lambda e, B=B, bank=bank: e.tensor_copy(
                B["pn"][:, :], bank[0:128, 0:512]),
                waits=[dat["mm_v"], pn_war[pp]])
            trs = None
            for sub in range(4):
                trs = P.pe(lambda e, bank=bank, B=B, sub=sub: e.transpose(
                    bank[0:128, 128 * sub:128 * (sub + 1)],
                    B["pn"][:, 128 * sub:128 * (sub + 1)], ident_f[:, :]),
                    waits=[evv_v, ev_idf] if sub == 0 else (),
                    inc="pe" if sub == 3 else None)
            vcp = P.ve(lambda e, bank=bank, c=c: e.tensor_copy(
                vbuf[:, 512 * c:512 * (c + 1)], bank[0:128, 0:512]),
                waits=[trs])
            st[f"bk_v{pp}"] = vcp
            pn_war[pp] = trs

        prev_dat = None
        for c in range(TCH):
            dat = s1(c)
            if prev_dat is not None:
                s2(prev_dat)
            prev_dat = dat
        s2(prev_dat)
        st["pn_war0"], st["pn_war1"] = pn_war
        st["qn_war0"], st["qn_war1"] = qn_war
        st["sq_war0"], st["sq_war1"] = sq_war
        return ssq_red

    def attn_pass(att, hs, ssq_red, qsrc=None, qoff=0, qrow=0):
        qs = qbuf if qsrc is None else qsrc
        ao_evs = []
        scwar = [st.get(f"sc_war{i}") for i in range(4)]
        av_war = [st.get("av_war0"), st.get("av_war1")]
        cs_rd_war = [st.get("cs_war0"), st.get("cs_war1")]
        negb_rd = st.get("negb_rd")
        rb_war = st.get("rb_war")
        sq_war = st.get("sq_war")
        last_exp = None
        for q in range(TCH):
            kw = [ssq_red["k"][c] for c in range(q + 1)]
            e1 = P.ve(lambda e, q=q: e.tensor_reduce(
                rows[0:1, 32:33], rows[0:1, 16:16 + q + 1], axis=AX.X,
                op=ALU.max), waits=kw + [st.get("rows64n_war")])
            e2 = P.ve(lambda e, q=q: e.tensor_tensor(
                rows[0:1, 33:34], rows[0:1, 32:33],
                rows[0:1, qrow + q:qrow + q + 1],
                op=ALU.add), waits=[e1, ssq_red["q"][q]])
            e3 = P.ve(lambda e: e.tensor_scalar(
                rows[0:1, 33:34], rows[0:1, 33:34], -SCALE / 2.0, None,
                op0=ALU.mult), waits=[e2])
            bcmm = P.pe(lambda e: e.matmul(ps[7][0:128, 0:1], ones_f[0:1, 0:128],
                                           rows[0:1, 33:34],
                                           start=True, stop=True),
                        waits=[e3, ev_one, st.get("ps7_war")])
            st["rows64n_war"] = bcmm
            nb = P.ve(lambda e: e.tensor_copy(cols[:, 0:1], ps[7][0:128, 0:1]),
                      waits=[bcmm, negb_rd])
            st["ps7_war"] = nb
            unit_pe = None
            for c in range(q + 1):
                exp_evs = [None] * 4
                pair_mm = None
                for ks in range(4):
                    mmw = [scwar[ks]]
                    if ks == 0 and c == 0:
                        mmw += [av_war[q % 2], cs_rd_war[q % 2], nb]
                    pair_mm = P.pe(lambda e, c=c, ks=ks, q=q: e.matmul(
                        ps[ks][0:128, 0:512],
                        kbuf[:, 512 * c + 128 * ks:512 * c + 128 * (ks + 1)],
                        qs[:, qoff + 512 * q:qoff + 512 * (q + 1)],
                        start=True, stop=True),
                        waits=mmw, inc="pe" if ks in (1, 3) else None)
                    if ks in (1, 3):
                        for k2 in (ks - 1, ks):
                            slot = (c * 4 + k2) % 8
                            ex = P.sc(lambda e, k2=k2, slot=slot: e.activation(
                                pbuf[:, 512 * slot:512 * (slot + 1)],
                                ps[k2][0:128, 0:512], AF.Exp,
                                bias=cols[:, 0:1], scale=SCALE),
                                waits=[pair_mm, nb, st["pslot_war"][slot]])
                            exp_evs[k2] = ex
                            last_exp = ex
                for ks in range(4):
                    scwar[ks] = exp_evs[ks]
                gate_evs = exp_evs
                if c == q:
                    gate_evs = []
                    for ks in range(4):
                        slot = (c * 4 + ks) % 8
                        pm = P.gp(lambda e, slot=slot, ks=ks: e.affine_select(
                            out=pbuf[:, 512 * slot:512 * (slot + 1)],
                            in_=pbuf[:, 512 * slot:512 * (slot + 1)],
                            pattern=[[1, 512]], compare_op=ALU.is_ge, fill=0.0,
                            base=-128 * ks, channel_multiplier=-1),
                            waits=[exp_evs[ks]])
                        gate_evs.append(pm)
                for ks in range(4):
                    slot = (c * 4 + ks) % 8
                    first = (c == 0 and ks == 0)
                    last = (c == q and ks == 3)
                    P.pe(lambda e, q=q, c=c, ks=ks, slot=slot, first=first,
                         last=last: e.matmul(
                        ps[4 + q % 2][0:128, 0:512],
                        vbuf[:, 128 * (4 * c + ks):128 * (4 * c + ks + 1)],
                        pbuf[:, 512 * slot:512 * (slot + 1)],
                        start=first, stop=last, skip_group_check=True),
                        waits=[gate_evs[ks]], inc=None)
                    unit_pe = P.pe(lambda e, q=q, slot=slot, first=first,
                                   last=last: e.matmul(
                        ps[6][32 * (q % 2):32 * (q % 2) + 1, 0:512],
                        ones_b[:, 0:1], pbuf[:, 512 * slot:512 * (slot + 1)],
                        start=first, stop=last, skip_group_check=True),
                        waits=[ev_oneb, st.get(f"ps6r{q % 2}")],
                        inc="pe" if ks == 3 else None)
                for ks in range(4):
                    st["pslot_war"][(c * 4 + ks) % 8] = unit_pe
            rcp = P.ve(lambda e, q=q: e.reciprocal(
                rows[0:1, 512:1024], ps[6][32 * (q % 2):32 * (q % 2) + 1, 0:512]),
                waits=[unit_pe, st.get("rows64r_war")])
            st[f"ps6r{q % 2}"] = rcp
            bb = P.pe(lambda e: e.matmul(ps[7][0:128, 0:512], ones_f[0:1, 0:128],
                                         rows[0:1, 512:1024], start=True, stop=True),
                      waits=[rcp, st.get("ps7_war")])
            st["rows64r_war"] = bb
            rbc = P.ve(lambda e: e.tensor_copy(rb[:, :], ps[7][0:128, 0:512]),
                       waits=[bb, rb_war])
            st["ps7_war"] = rbc
            negb_rd = last_exp
            ao = P.ve(lambda e, hs=hs, q=q: e.tensor_tensor(
                aobuf[:, 8192 * hs + 512 * q:8192 * hs + 512 * (q + 1)],
                ps[4 + q % 2][0:128, 0:512], rb[:, :], op=ALU.mult),
                waits=[rbc, unit_pe, st.get("ao_war")])
            rb_war = ao
            av_war[q % 2] = ao
            cs_rd_war[q % 2] = rcp
            ao_evs.append(ao)
        st["rb_war"] = rb_war
        st["negb_rd"] = negb_rd
        st["last_exp"] = last_exp
        st["sq_war"] = sq_war
        for i in range(4):
            st[f"sc_war{i}"] = scwar[i]
        st["av_war0"], st["av_war1"] = av_war
        st["cs_war0"], st["cs_war1"] = cs_rd_war
        st["bk_q0"] = st["bk_q1"] = st["bk_k0"] = st["bk_k1"] = last_exp
        st["bk_v0"] = ao_evs[14]
        st["bk_v1"] = ao_evs[15]
        st["ps6r0"] = cs_rd_war[0]
        st["ps6r1"] = cs_rd_war[1]
        st["ps2_war"] = last_exp
        return ao_evs

    def oproj_pass(rs_loc, ao_last, slab_ev):
        st_war = [st.get("stv_war0"), st.get("stv_war1"), st.get("stv_war2")]
        evac_toggle = 0
        last_store = None
        last_mm = None
        for c in range(TCH):
            for dout in range(DC):
                par = (c * DC + dout) % 2
                bank_war = st.get(f"obk{par}")
                mm = None
                for hs in range(2):
                    mm = P.pe(lambda e, par=par, hs=hs, dout=dout, c=c: e.matmul(
                        ps[par][0:128, 0:512],
                        oslab[:, 128 * (16 * hs + dout):128 * (16 * hs + dout + 1)],
                        aobuf[:, 8192 * hs + 512 * c:8192 * hs + 512 * (c + 1)],
                        start=(hs == 0), stop=(hs == 1)),
                        waits=[ao_last, slab_ev, bank_war, st.get("last_exp")]
                        if hs == 0 else (),
                        inc="pe" if hs == 1 else None)
                last_mm = mm
                sl = (c * DC + dout) % 3
                if evac_toggle == 0:
                    evc = P.ve(lambda e, par=par, sl=sl: e.tensor_copy(
                        stv[:, 512 * sl:512 * (sl + 1)], ps[par][0:128, 0:512]),
                        waits=[mm, st_war[sl]])
                else:
                    evc = P.sc(lambda e, par=par, sl=sl: e.activation(
                        stv[:, 512 * sl:512 * (sl + 1)], ps[par][0:128, 0:512],
                        AF.Copy), waits=[mm, st_war[sl]])
                evac_toggle ^= 1
                st[f"obk{par}"] = evc
                r, off = c // 2, (c % 2) * 512
                dst = rs_loc.ap()[D * r + 128 * dout:D * r + 128 * (dout + 1),
                                  off:off + 512]
                sev = P.dma("sync", lambda e, sl=sl, dst=dst: e.dma_start(
                    out=dst, in_=stv[:, 512 * sl:512 * (sl + 1)]),
                    waits=[evc], inc=f"st{sl}")
                st_war[sl] = sev
                last_store = sev
        for i in range(3):
            st[f"stv_war{i}"] = st_war[i]
        all_stores = [(f"st{i}", P.cnt[f"st{i}"]) for i in range(3)]
        return all_stores, last_mm

    if DBG == "2":
        dbg_snap([(dbg_out["d_xg1"], xg1.ap())])
    prev_done = None
    ao_first = None
    ssq_k_saved = None
    for hs in range(2):
        ssq = proj_pass(1, hs, xg1, ev_cc1, prev_done)
        if DBG == "2" and hs == 0:
            dbg_snap([(dbg_out["d_q"], qbuf[:, :]),
                      (dbg_out["d_k"], kbuf[:, :]),
                      (dbg_out["d_v"], vbuf[:, :])])
        ao_evs = attn_pass(1, hs, ssq)
        if hs == 0:
            ao_first = ao_evs[-1]
        prev_done = ao_evs[-1]
    if DBG == "2":
        dbg_snap([(dbg_out["d_ao"], aobuf[:, :])])
    ev_wo = None
    for hs in range(2):
        ev_wo = P.dma("gpsimd", lambda e, hs=hs: e.dma_start(
            out=oslab[:, 2048 * hs:2048 * (hs + 1)].rearrange("p (d f) -> p d f", d=DC),
            in_=wo_my[hs].rearrange("c p f -> p c f")), waits=[ao_first], inc="wld")
    st1_last, o1_mm = oproj_pass(rs1_loc, prev_done, ev_wo)
    ev_cc2 = P.coll(lambda e: e.collective_compute(
        "ReduceScatter", ALU.add, replica_groups=[list(range(NCORE))],
        ins=[rs1_loc.ap().opt()], outs=[rs1_out.ap().opt()]), waits=[st1_last])
    if DBG == "2":
        dbg_snap([(dbg_out["d_rs1o"], rs1_out.ap())])
        dump_banks(0, lnbc)

    sum_mm = ssq_mm = None
    r_evs = []
    xs_w = [None, None]
    rs_w = [None, None]
    sq_w = None
    attn1_done = prev_done
    for d in range(DC):
        par = d % 2
        exv = None
        for h_ in range(2):
            exv = P.dma("sync", lambda e, d=d, par=par, h_=h_: e.dma_start(
                out=xst[:, 1024 * par + 512 * h_:1024 * par + 512 * (h_ + 1)],
                in_=x_sh[:, 8192 * h_ + 512 * d:8192 * h_ + 512 * (d + 1)]),
                waits=[xs_w[par], attn1_done] if h_ == 0 else (),
                inc=f"x0{par}")
        erv = P.dma("sync", lambda e, d=d, par=par: e.dma_start(
            out=rsst[:, 1024 * par:1024 * (par + 1)],
            in_=rs1_out.ap()[128 * d:128 * (d + 1), :]),
            waits=[ev_cc2, rs_w[par], attn1_done], inc=f"rs{par}")
        rev = P.ve(lambda e, d=d, par=par: e.scalar_tensor_tensor(
            rbuf[:, 1024 * d:1024 * (d + 1)],
            rsst[:, 1024 * par:1024 * (par + 1)], bo_sb[:, d:d + 1],
            xst[:, 1024 * par:1024 * (par + 1)], op0=ALU.add, op1=ALU.add),
            waits=[exv, erv, ev_bo, ("pe", P.cnt["pe"])])
        xs_w[par] = rev
        rs_w[par] = rev
        r_evs.append(rev)
        sq = P.ve(lambda e, d=d: e.tensor_mul(
            hscr[:, 1024:2048], rbuf[:, 1024 * d:1024 * (d + 1)],
            rbuf[:, 1024 * d:1024 * (d + 1)]), waits=[rev, sq_w])
        for t2 in range(2):
            sum_mm = P.pe(lambda e, d=d, t2=t2: e.matmul(
                ps[t2][0:1, 0:512], ones_f[:, 0:1],
                rbuf[:, 1024 * d + 512 * t2:1024 * d + 512 * (t2 + 1)],
                start=(d == 0), stop=(d == DC - 1), skip_group_check=True),
                waits=[rev, ev_one, st.get("last_exp"), st.get("obk0"),
                       st.get("obk1"), st.get("av_war0"), st.get("av_war1")],
                inc=None)
            ssq_mm = P.pe(lambda e, d=d, t2=t2: e.matmul(
                ps[4 + t2][0:1, 0:512], ones_b[:, 0:1],
                hscr[:, 1024 + 512 * t2:1024 + 512 * (t2 + 1)],
                start=(d == 0), stop=(d == DC - 1), skip_group_check=True),
                waits=[sq], inc="pe" if t2 == 1 else None)
        sq_w = ssq_mm
        sum_mm = ssq_mm

    def ln_stats(sum_ev, ssq_ev, eps):
        evm = []
        for t2 in range(2):
            e1 = P.ve(lambda e, t2=t2: e.tensor_scalar(
                rows[0:1, 512 * t2:512 * (t2 + 1)],
                ps[t2][0:1, 0:512], 1.0 / D, None, op0=ALU.mult),
                waits=[sum_ev, st.get("rows0_war")])
            e2 = P.ve(lambda e, t2=t2: e.tensor_scalar(
                rows[32:33, 512 * t2:512 * (t2 + 1)],
                ps[4 + t2][0:1, 0:512], 1.0 / D, None, op0=ALU.mult),
                waits=[ssq_ev, st.get("rows32_war")])
            e3 = P.ve(lambda e, t2=t2: e.tensor_mul(
                ps[2][0:1, 0:512],
                rows[0:1, 512 * t2:512 * (t2 + 1)],
                rows[0:1, 512 * t2:512 * (t2 + 1)]),
                waits=[e1, st.get("rows64n_war"), st.get("rows64r_war"),
                       st.get("ps2_war")])
            e4 = P.ve(lambda e, t2=t2: e.tensor_tensor(
                rows[64:65, 512 * t2:512 * (t2 + 1)],
                rows[32:33, 512 * t2:512 * (t2 + 1)],
                ps[2][0:1, 0:512], op=ALU.subtract),
                waits=[e2, e3])
            e5 = P.ve(lambda e, t2=t2, eps=eps: e.tensor_scalar(
                rows[64:65, 512 * t2:512 * (t2 + 1)],
                rows[64:65, 512 * t2:512 * (t2 + 1)], 1.0, eps,
                op0=ALU.mult, op1=ALU.add), waits=[e4])
            e6 = P.sc(lambda e, t2=t2: e.activation(
                rows[64:65, 512 * t2:512 * (t2 + 1)],
                rows[64:65, 512 * t2:512 * (t2 + 1)], AF.Ln), waits=[e5])
            e7 = P.sc(lambda e, t2=t2: e.activation(
                rows[64:65, 512 * t2:512 * (t2 + 1)],
                rows[64:65, 512 * t2:512 * (t2 + 1)], AF.Exp, scale=-0.5),
                waits=[e6])
            e8a = P.ve(lambda e, t2=t2: e.tensor_copy(
                ps[2][0:1, 0:512], rows[64:65, 512 * t2:512 * (t2 + 1)]),
                waits=[e7])
            e8 = P.ve(lambda e, t2=t2: e.tensor_mul(
                rows[32:33, 512 * t2:512 * (t2 + 1)],
                rows[0:1, 512 * t2:512 * (t2 + 1)],
                ps[2][0:1, 0:512]), waits=[e8a])
            e9 = P.ve(lambda e, t2=t2: e.tensor_scalar(
                rows[32:33, 512 * t2:512 * (t2 + 1)],
                rows[32:33, 512 * t2:512 * (t2 + 1)], -1.0, None, op0=ALU.mult),
                waits=[e8])
            evm.append((e7, e9))
        abc = bbc = None
        for t2 in range(2):
            m1 = P.pe(lambda e, t2=t2: e.matmul(
                ps[2][0:128, 0:512], ones_f[64:65, 0:128],
                rows[64:65, 512 * t2:512 * (t2 + 1)], start=True, stop=True),
                waits=[evm[0][1], evm[1][1], evm[t2][0], ev_one,
                       st.get("ps2_war")])
            abc = P.ve(lambda e, t2=t2: e.tensor_copy(
                lnbc[:, 512 * t2:512 * (t2 + 1)], ps[2][0:128, 0:512]),
                waits=[m1])
            st["ps2_war"] = abc
            m2 = P.pe(lambda e, t2=t2: e.matmul(
                ps[2][0:128, 0:512], ones_f[32:33, 0:128],
                rows[32:33, 512 * t2:512 * (t2 + 1)], start=True, stop=True),
                waits=[evm[t2][1], abc])
            bbc = P.ve(lambda e, t2=t2: e.tensor_copy(
                lnbc[:, 1024 + 512 * t2:1024 + 512 * (t2 + 1)],
                ps[2][0:128, 0:512]), waits=[m2])
            st["ps2_war"] = bbc
        st["rows0_war"] = bbc
        st["rows32_war"] = bbc
        st["rows64n_war"] = bbc
        st["rows64r_war"] = bbc
        return abc, bbc

    if DBG == "2":
        dbg_snap([(dbg_out["d_m1"], rbuf[:, :])])
        dump_banks(2, lnbc)
    abc, bbc = ln_stats(sum_mm, ssq_mm, LN_EPS)
    if DBG == "2":
        dbg_snap([(dbg_out["d_lnbc"], lnbc[:, :]),
                  (dbg_out["d_rows"], rows[:, :])])
    x1_evs = []
    x1st_last = None
    hs_mm = None
    sq3_w = None
    hscr_w = [None, None]
    for d in range(DC):
        e1 = P.ve(lambda e, d=d: e.tensor_tensor(
            rbuf[:, 1024 * d:1024 * (d + 1)], rbuf[:, 1024 * d:1024 * (d + 1)],
            lnbc[:, 0:1024], op=ALU.mult), waits=[abc, bbc, r_evs[d]])
        e2 = P.ve(lambda e, d=d: e.tensor_tensor(
            rbuf[:, 1024 * d:1024 * (d + 1)], rbuf[:, 1024 * d:1024 * (d + 1)],
            lnbc[:, 1024:2048], op=ALU.add), waits=[e1])
        x1_evs.append(e2)
        x1st_last = P.dma("sync", lambda e, d=d: e.dma_start(
            out=x1_dram.ap()[128 * d:128 * (d + 1), :],
            in_=rbuf[:, 1024 * d:1024 * (d + 1)]), waits=[e2], inc="stg")
        sq = P.ve(lambda e, d=d: e.tensor_mul(
            hscr[:, 1024:2048], rbuf[:, 1024 * d:1024 * (d + 1)],
            rbuf[:, 1024 * d:1024 * (d + 1)]),
            waits=[e2, sq3_w, hscr_w[0], hscr_w[1]])
        for t2 in range(2):
            hs_mm = P.pe(lambda e, d=d, t2=t2: e.matmul(
                (ps[3] if t2 == 0 else ps[6])[0:1, 0:512], ones_b[:, 0:1],
                hscr[:, 1024 + 512 * t2:1024 + 512 * (t2 + 1)],
                start=(d == 0), stop=(d == DC - 1), skip_group_check=True),
                waits=[sq, st.get("ps6r0")],
                inc="pe" if t2 == 1 else None)
        sq3_w = hs_mm
    if DBG:
        for d in range(DC):
            P.dma("sync", lambda e, d=d: e.dma_start(
                out=dbg_out["d_x1"][128 * d:128 * (d + 1), :],
                in_=rbuf[:, 1024 * d:1024 * (d + 1)]), waits=[x1_evs[d]],
                inc="stg")
    if DBG == "2":
        dbg_snap([(dbg_out["d_x1b"], rbuf[:, :])])
    r2 = []
    for t2 in range(2):
        e1 = P.ve(lambda e, t2=t2: e.tensor_scalar(
            rows[0:1, 512 * t2:512 * (t2 + 1)],
            (ps[3] if t2 == 0 else ps[6])[0:1, 0:512],
            1.0 / D, RMS_EPS, op0=ALU.mult, op1=ALU.add),
            waits=[hs_mm, st.get("rows0_war")])
        if t2 == 1:
            st["ps6r0"] = e1
        e2 = P.sc(lambda e, t2=t2: e.activation(
            rows[0:1, 512 * t2:512 * (t2 + 1)], rows[0:1, 512 * t2:512 * (t2 + 1)],
            AF.Ln), waits=[e1])
        e3 = P.sc(lambda e, t2=t2: e.activation(
            rows[0:1, 512 * t2:512 * (t2 + 1)], rows[0:1, 512 * t2:512 * (t2 + 1)],
            AF.Exp, scale=-0.5), waits=[e2])
        m1 = P.pe(lambda e, t2=t2: e.matmul(
            ps[2][0:128, 0:512], ones_f[0:1, 0:128],
            rows[0:1, 512 * t2:512 * (t2 + 1)], start=True, stop=True),
            waits=[e3, st.get("ps2_war")])
        rc = P.ve(lambda e, t2=t2: e.tensor_copy(
            lnbc[:, 512 * t2:512 * (t2 + 1)], ps[2][0:128, 0:512]),
            waits=[m1, x1_evs[-1]])
        st["ps2_war"] = rc
        st["rows0_war"] = m1
        r2.append(rc)
    he_last = None
    for d in range(DC):
        par = d % 2
        he = P.ve(lambda e, d=d, par=par: e.scalar_tensor_tensor(
            hscr[:, 1024 * par:1024 * (par + 1)],
            rbuf[:, 1024 * d:1024 * (d + 1)], anw_sb[:, d:d + 1],
            lnbc[:, 0:1024], op0=ALU.mult, op1=ALU.mult),
            waits=[r2[0], r2[1], x1_evs[d], ev_anw, hscr_w[par], sq3_w])
        hstv = None
        for h_ in range(2):
            hstv = P.dma("sync", lambda e, d=d, par=par, h_=h_: e.dma_start(
                out=xg2_loc.ap()[:, 8192 * h_ + 512 * d:8192 * h_ + 512 * (d + 1)],
                in_=hscr[:, 1024 * par + 512 * h_:1024 * par + 512 * (h_ + 1)]),
                waits=[he] if h_ == 0 else [hstv], inc=f"hst{par}")
        hscr_w[par] = hstv
        he_last = he
    st["ln_done"] = [("dve", P.cnt["dve"]), ("act", P.cnt["act"]),
                     ("pool", P.cnt["pool"]), ("pe", P.cnt["pe"]),
                     ("stg", P.cnt["stg"])]
    for k in ("bk_q0", "bk_q1", "bk_k0", "bk_k1", "bk_v0", "bk_v1"):
        st[k] = [st.get(k), he_last]
    ev_cc3 = P.coll(lambda e: e.collective_compute(
        "AllGather", ALU.bypass, replica_groups=[list(range(NCORE))],
        ins=[xg2_loc.ap().opt()], outs=[xg2.ap().opt()]),
        waits=[hscr_w[0], hscr_w[1]])

    ev_w2 = None
    for si, (src, idx) in enumerate([(q2_my, 0), (q2_my, 1), (k2_my, 0),
                                     (v2_my, 0)]):
        ev_w2 = P.dma("gpsimd", lambda e, si=si, src=src, idx=idx: e.dma_start(
            out=aslab[:, 2048 * si:2048 * (si + 1)],
            in_=src[idx]),
            waits=[st.get("ln_done")], inc="wld")
    st["ev_w2"] = ev_w2

    prev2 = [x1st_last, he_last]
    st["ao_war"] = [st1_last, o1_mm]
    ao2_evs_last = None
    ssq = proj_pass2(xg2, ev_cc3, prev2)
    if DBG == "2":
        dbg_snap([(dbg_out["d_q2"], qbuf[:, :]),
                  (dbg_out["d_k2"], kbuf[:, :]),
                  (dbg_out["d_v2"], vbuf[:, :])])
        e1_ = P.ve(lambda e: e.tensor_copy(tbv[:, 0:512], sinv[:, :]))
        e2_ = P.ve(lambda e: e.tensor_copy(tbv[:, 512:1024]
                                           if False else auxv[:, :],
                                           cosv[:, :]), waits=[e1_])
        e3_ = P.dma("sync", lambda e: e.dma_start(
            out=dbg_out["d_trig"][:, 0:512], in_=tbv[:, :]), waits=[e2_],
            inc="dbg")
        e3_ = P.dma("sync", lambda e: e.dma_start(
            out=dbg_out["d_trig"][:, 512:1024], in_=auxv[:, :]),
            waits=[e3_], inc="dbg")
        for eng_ in Pl.ENGS:
            P.wait(eng_, e3_)
    ao_evs = attn_pass(2, 0, ssq)
    prev2 = ao_evs[-1]
    st["ao_war"] = None
    ao_evs = attn_pass(2, 1, {"q": ssq["q1"], "k": ssq["k"]},
                       qsrc=aobuf, qoff=8192, qrow=34)
    prev2 = ao_evs[-1]
    ao2_evs_last = prev2
    if DBG == "2":
        dbg_snap([(dbg_out["d_ao2"], aobuf[:, :])])
    ev_wo2 = None
    for hs in range(2):
        ev_wo2 = P.dma("gpsimd", lambda e, hs=hs: e.dma_start(
            out=oslab[:, 2048 * hs:2048 * (hs + 1)].rearrange("p (d f) -> p d f", d=DC),
            in_=o2_my[hs].rearrange("c p f -> p c f")),
            waits=[o1_mm, st1_last], inc="wld")
    pre_gld = []
    for ic_ in range(2):
        pre_gld.append(P.dma("gpsimd", lambda e, ic_=ic_: e.dma_start(
            out=wgsl[:, 2048 * ic_:2048 * (ic_ + 1)].rearrange(
                "p (d f) -> p d f", d=DC),
            in_=wg_p[0, ic_].rearrange("(d p) f -> p d f", p=128)),
            inc=f"wg{ic_}"))
    st2_last, o2_mm = oproj_pass(rs2_loc, ao2_evs_last, ev_wo2)
    ev_cc4 = P.coll(lambda e: e.collective_compute(
        "ReduceScatter", ALU.add, replica_groups=[list(range(NCORE))],
        ins=[rs2_loc.ap().opt()], outs=[rs2_out.ap().opt()]), waits=[st2_last])
    if DBG == "2":
        dbg_snap([(dbg_out["d_rs2o"], rs2_out.ap())])

    xs_w3 = [None, None]
    rs_w3 = [None, None]
    h1_w = [None, None]
    hsq_w = None
    ssqr_mm = None
    fb_evs = []
    h1st_last = None
    for d in range(DC):
        par = d % 2
        exv = P.dma("sync", lambda e, d=d, par=par: e.dma_start(
            out=xst[:, 1024 * par:1024 * (par + 1)],
            in_=x1_dram.ap()[128 * d:128 * (d + 1), :]),
            waits=[xs_w3[par], x1st_last, ao2_evs_last], inc=f"x1{par}")
        erv = P.dma("sync", lambda e, d=d, par=par: e.dma_start(
            out=rsst[:, 1024 * par:1024 * (par + 1)],
            in_=rs2_out.ap()[128 * d:128 * (d + 1), :]),
            waits=[ev_cc4, rs_w3[par], ao2_evs_last], inc=f"r2{par}")
        h1e = P.ve(lambda e, par=par: e.tensor_tensor(
            h1ev[:, 1024 * par:1024 * (par + 1)],
            rsst[:, 1024 * par:1024 * (par + 1)],
            xst[:, 1024 * par:1024 * (par + 1)], op=ALU.add),
            waits=[exv, erv, h1_w[par]])
        h1st_last = P.dma("sync", lambda e, d=d, par=par: e.dma_start(
            out=h1_dram.ap()[128 * d:128 * (d + 1), :],
            in_=h1ev[:, 1024 * par:1024 * (par + 1)]), waits=[h1e],
            inc=f"h1s{par}")
        fb = P.ve(lambda e, d=d, par=par: e.tensor_tensor(
            rbuf[:, 1024 * d:1024 * (d + 1)],
            h1ev[:, 1024 * par:1024 * (par + 1)],
            xst[:, 1024 * par:1024 * (par + 1)], op=ALU.add),
            waits=[h1e, ("pe", P.cnt["pe"])])
        fb_evs.append(fb)
        xs_w3[par] = fb
        rs_w3[par] = h1e
        sq = P.ve(lambda e, par=par: e.tensor_mul(
            hsqt[:, :], h1ev[:, 1024 * par:1024 * (par + 1)],
            h1ev[:, 1024 * par:1024 * (par + 1)]), waits=[h1e, hsq_w])
        h1_w[par] = [fb, sq, h1st_last]
        for t2 in range(2):
            ssqr_mm = P.pe(lambda e, t2=t2, d=d: e.matmul(
                ps[t2][0:1, 0:512], ones_b[:, 0:1],
                hsqt[:, 512 * t2:512 * (t2 + 1)],
                start=(d == 0), stop=(d == DC - 1), skip_group_check=True),
                waits=[sq, ev_oneb, st.get("last_exp"), st.get("obk0"),
                       st.get("obk1"), st.get("av_war0"), st.get("av_war1")],
                inc="pe" if t2 == 1 else None)
        hsq_w = [ssqr_mm]
    if DBG:
        for d in range(DC):
            P.dma("sync", lambda e, d=d: e.dma_start(
                out=dbg_out["d_h1"][128 * d:128 * (d + 1), :],
                in_=h1_dram.ap()[128 * d:128 * (d + 1), :]),
                waits=[("stg", P.cnt["stg"])], inc="stg")
    if DBG == "2":
        dbg_snap([(dbg_out["d_fb"], rbuf[:, :])])

    r3 = []
    for t2 in range(2):
        e1 = P.ve(lambda e, t2=t2: e.tensor_scalar(
            rows[0:1, 512 * t2:512 * (t2 + 1)], ps[t2][0:1, 0:512],
            1.0 / D, RMS_EPS, op0=ALU.mult, op1=ALU.add),
            waits=[ssqr_mm, st.get("rows0_war")])
        e2 = P.sc(lambda e, t2=t2: e.activation(
            rows[0:1, 512 * t2:512 * (t2 + 1)], rows[0:1, 512 * t2:512 * (t2 + 1)],
            AF.Ln), waits=[e1])
        e3 = P.sc(lambda e, t2=t2: e.activation(
            rows[0:1, 512 * t2:512 * (t2 + 1)], rows[0:1, 512 * t2:512 * (t2 + 1)],
            AF.Exp, scale=-0.5), waits=[e2])
        m1 = P.pe(lambda e, t2=t2: e.matmul(
            ps[2][0:128, 0:512], ones_f[0:1, 0:128],
            rows[0:1, 512 * t2:512 * (t2 + 1)], start=True, stop=True),
            waits=[e3, st.get("ps2_war")])
        rc = P.ve(lambda e, t2=t2: e.tensor_copy(
            rstbc[:, 512 * t2:512 * (t2 + 1)], ps[2][0:128, 0:512]),
            waits=[m1, h1_w[0], h1_w[1]])
        st["ps2_war"] = rc
        st["rows0_war"] = m1
        r3.append(rc)
    tcol = None
    for tt in range(8):
        tcol = P.pe(lambda e, tt=tt: e.matmul(
            ps[4][0:128, tt:tt + 1], rows[0:1, 128 * tt:128 * (tt + 1)],
            ones_f[0:1, 0:1], start=(tt == 0), stop=(tt == 7),
            skip_group_check=True),
            waits=[r3[0], r3[1], ev_one], inc="pe" if tt == 7 else None)
    rcol_all = P.ve(lambda e: e.tensor_copy(cols[:, 8:16], ps[4][0:128, 0:8]),
                    waits=[tcol])
    rcol_evs = [rcol_all] * 8

    wprep = []
    wp = None
    for d in range(DC):
        w1 = P.ve(lambda e, d=d: e.tensor_scalar(
            cols[:, 32:40], gwt_sb[:, 8 * d:8 * (d + 1)], fnw_sb[:, d:d + 1],
            None, op0=ALU.mult), waits=[ev_gw, ev_fnw, wp])
        w2 = P.ve(lambda e, d=d: e.tensor_copy(
            whi_sb[:, 8 * d:8 * (d + 1)], cols[:, 32:40]), waits=[w1])
        w3 = P.ve(lambda e, d=d: e.tensor_tensor(
            cols[:, 40:48], whi_sb[:, 8 * d:8 * (d + 1)], cols[:, 32:40],
            op=ALU.subtract), waits=[w2])
        w4 = P.ve(lambda e, d=d: e.tensor_scalar(
            wlo_sb[:, 8 * d:8 * (d + 1)], cols[:, 40:48], -1.0, None,
            op0=ALU.mult), waits=[w3])
        wp = w4
        wprep.append(w4)
    h1s_w = [None, None]
    hl_w = None
    gmm = None
    h2b_evs = []
    for d in range(DC):
        par = d % 2
        ehv = P.dma("sync", lambda e, d=d, par=par: e.dma_start(
            out=xst[:, 1024 * par:1024 * (par + 1)],
            in_=h1_dram.ap()[128 * d:128 * (d + 1), :]),
            waits=[h1s_w[par], ("h1s0", P.cnt["h1s0"]),
                   ("h1s1", P.cnt["h1s1"]),
                   fb_evs[DC - 2 + par]], inc=f"h1{par}")
        hb = P.ve(lambda e, d=d, par=par: e.scalar_tensor_tensor(
            hbuf[:, 1024 * d:1024 * (d + 1)],
            xst[:, 1024 * par:1024 * (par + 1)], fnw_sb[:, d:d + 1],
            rstbc[:, 0:1024], op0=ALU.mult, op1=ALU.mult),
            waits=[ehv, r3[0], r3[1], ev_fnw, st.get("last_exp")])
        h2b_evs.append(hb)
        hi = P.ve(lambda e, par=par: e.tensor_copy(
            hilo[:, 0:1024], xst[:, 1024 * par:1024 * (par + 1)]),
            waits=[ehv, hl_w])
        lo = P.ve(lambda e, par=par: e.tensor_tensor(
            hilo[:, 1024:2048], xst[:, 1024 * par:1024 * (par + 1)],
            hilo[:, 0:1024], op=ALU.subtract), waits=[hi])
        h1s_w[par] = [hb, lo]
        for tt in range(8):
            for pi, (aoff, wbuf) in enumerate(((0, whi_sb), (0, wlo_sb),
                                               (1024, whi_sb))):
                gmm = P.pe(lambda e, d=d, tt=tt, aoff=aoff, wbuf=wbuf, pi=pi:
                           e.matmul(
                    ps[tt][0:128, 0:8],
                    hilo[:, aoff + 128 * tt:aoff + 128 * (tt + 1)],
                    wbuf[:, 8 * d:8 * (d + 1)],
                    start=(d == 0 and pi == 0), stop=(d == DC - 1 and pi == 2),
                    skip_group_check=True),
                    waits=[lo, wprep[d], st.get("last_exp"), rcol_all,
                           st.get("ps6r0"), st.get("ps6r1"),
                           st.get("ps7_war")],
                    inc="pe" if (pi == 2 and tt == 7) else None)
        hl_w = gmm
    if DBG == "2":
        dbg_snap([(dbg_out["d_hb"], hbuf[:, :])])
    top_done = []
    evac_evs = [None] * 8
    prev_ev = None
    c0 = 16
    for tt in range(8):
        e1 = P.ve(lambda e, tt=tt: e.tensor_scalar(
            comb_all[:, 8 * tt:8 * (tt + 1)], ps[tt][0:128, 0:8],
            cols[:, 8 + tt:9 + tt], None, op0=ALU.mult),
            waits=[gmm, rcol_evs[tt], prev_ev])
        evac_evs[tt] = e1
        e2 = P.ve(lambda e, tt=tt: e.tensor_reduce(
            cols[:, c0:c0 + 1], comb_all[:, 8 * tt:8 * (tt + 1)], axis=AX.X,
            op=ALU.max), waits=[e1])
        e3 = P.ve(lambda e: e.tensor_scalar(
            cols[:, c0 + 1:c0 + 2], cols[:, c0:c0 + 1], -1.0, None,
            op0=ALU.mult), waits=[e2])
        e4 = P.sc(lambda e, tt=tt: e.activation(
            comb_all[:, 8 * tt:8 * (tt + 1)], comb_all[:, 8 * tt:8 * (tt + 1)],
            AF.Exp, bias=cols[:, c0 + 1:c0 + 2], scale=1.0), waits=[e3])
        e5 = P.ve(lambda e, tt=tt: e.tensor_reduce(
            cols[:, c0 + 2:c0 + 3], comb_all[:, 8 * tt:8 * (tt + 1)], axis=AX.X,
            op=ALU.add), waits=[e4])
        e6 = P.ve(lambda e: e.reciprocal(
            cols[:, c0 + 3:c0 + 4], cols[:, c0 + 2:c0 + 3]), waits=[e5])
        e7 = P.ve(lambda e, tt=tt: e.tensor_scalar(
            comb_all[:, 8 * tt:8 * (tt + 1)], comb_all[:, 8 * tt:8 * (tt + 1)],
            cols[:, c0 + 3:c0 + 4], None, op0=ALU.mult), waits=[e6])
        t1 = P.ve(lambda e, tt=tt: e.tensor_reduce(
            cols[:, c0:c0 + 1], comb_all[:, 8 * tt:8 * (tt + 1)], axis=AX.X,
            op=ALU.max), waits=[e7])
        t2_ = P.ve(lambda e, tt=tt: e.tensor_scalar(
            cols[:, 48:56], comb_all[:, 8 * tt:8 * (tt + 1)], cols[:, c0:c0 + 1],
            None, op0=ALU.is_ge), waits=[t1])
        t3 = P.ve(lambda e, tt=tt: e.scalar_tensor_tensor(
            cols[:, 56:64], cols[:, 48:56], -1e9,
            comb_all[:, 8 * tt:8 * (tt + 1)], op0=ALU.mult, op1=ALU.add),
            waits=[t2_])
        t4 = P.ve(lambda e: e.tensor_reduce(
            cols[:, c0 + 1:c0 + 2], cols[:, 56:64], axis=AX.X, op=ALU.max),
            waits=[t3])
        t5 = P.ve(lambda e: e.tensor_tensor(
            cols[:, c0 + 2:c0 + 3], cols[:, c0:c0 + 1], cols[:, c0 + 1:c0 + 2],
            op=ALU.add), waits=[t4])
        t6 = P.ve(lambda e: e.tensor_scalar(
            cols[:, c0 + 2:c0 + 3], cols[:, c0 + 2:c0 + 3], 1.0, 1e-9,
            op0=ALU.mult, op1=ALU.add), waits=[t5])
        t7 = P.ve(lambda e: e.reciprocal(
            cols[:, c0 + 3:c0 + 4], cols[:, c0 + 2:c0 + 3]), waits=[t6])
        t8 = P.ve(lambda e, tt=tt: e.tensor_scalar(
            cols[:, 48:56], comb_all[:, 8 * tt:8 * (tt + 1)],
            cols[:, c0 + 1:c0 + 2], None, op0=ALU.is_ge), waits=[t7])
        t9 = P.ve(lambda e, tt=tt: e.tensor_tensor(
            cols[:, 48:56], cols[:, 48:56], comb_all[:, 8 * tt:8 * (tt + 1)],
            op=ALU.mult), waits=[t8])
        ta = P.ve(lambda e, tt=tt: e.tensor_scalar(
            comb_all[:, 8 * tt:8 * (tt + 1)], cols[:, 48:56],
            cols[:, c0 + 3:c0 + 4], None, op0=ALU.mult), waits=[t9])
        prev_ev = ta
        top_done.append(ta)
    if DBG:
        P.dma("sync", lambda e: e.dma_start(
            out=dbg_out["d_comb"][:, :], in_=comb_all[:, :]),
            waits=[top_done[-1]], inc="stg")
    ctc = None
    for tt in range(8):
        trc = P.pe(lambda e, tt=tt: e.transpose(
            ps[2][0:8, 0:128], comb_all[:, 8 * tt:8 * (tt + 1)], ident_f[:, :]),
            waits=[top_done[tt], ev_idf, st.get("ps2_war"), o2_mm])
        ctc = P.ve(lambda e, tt=tt: e.tensor_copy(
            combT[0:8, 128 * tt:128 * (tt + 1)], ps[2][0:8, 0:128]),
            waits=[trc])
        st["ps2_war"] = ctc

    wg_w = [None, None]
    wu_w = [None, None]
    wd_w = [None, None]
    act_w = {}
    crow_w = None
    cbc_w = [None, None]
    bank_w = [evac_evs[0], evac_evs[1], [evac_evs[2], ctc], evac_evs[3],
              [evac_evs[4], rcol_all], evac_evs[5], evac_evs[6], evac_evs[7]]
    wdbk_w = [None, None]
    sil_w = None
    euc_w = None
    f_evs = {d: fb_evs[d] for d in range(DC)}
    last_sil = None
    for eidx in range(NE):
        crow = P.dma("sync", lambda e, eidx=eidx: e.dma_start(
            out=rows[0:1, 0:1024],
            in_=combT[eidx:eidx + 1, :]),
            waits=[ctc, crow_w, st.get("rows64n_war"), st.get("rows64r_war"),
                   st.get("rows0_war"), st.get("rows32_war")],
            inc="cr")
        st["rows64n_war"] = None
        st["rows64r_war"] = None
        for t2 in range(2):
            m1 = P.pe(lambda e, t2=t2: e.matmul(
                ps[2][0:128, 0:512], ones_f[0:1, 0:128],
                rows[0:1, 512 * t2:512 * (t2 + 1)], start=True, stop=True),
                waits=[crow, st.get("ps2_war")])
            crow_w = m1
            cb2 = P.ve(lambda e, t2=t2: e.tensor_copy(
                cbc[:, 512 * t2:512 * (t2 + 1)], ps[2][0:128, 0:512]),
                waits=[m1, cbc_w[t2], r3[0], r3[1]])
            st["ps2_war"] = cb2
            cbc_w[t2] = cb2
        eg_ev = eu_ev = None
        for ic in range(NIC):
            wpar = ic % 2
            if eidx == 0 and ic < 2:
                gld = pre_gld[ic]
            else:
                gld = P.dma("gpsimd", lambda e, eidx=eidx, ic=ic, wpar=wpar:
                            e.dma_start(
                    out=wgsl[:, 2048 * wpar:2048 * (wpar + 1)].rearrange(
                        "p (d f) -> p d f", d=DC),
                    in_=wg_p[eidx, ic].rearrange("(d p) f -> p d f", p=128)),
                    waits=[wg_w[wpar]], inc=f"wg{wpar}")
            uld = P.dma("gpsimd", lambda e, eidx=eidx, ic=ic, wpar=wpar:
                        e.dma_start(
                out=wusl[:, 2048 * wpar:2048 * (wpar + 1)].rearrange(
                    "p (d f) -> p d f", d=DC),
                in_=wu_p[eidx, ic].rearrange("(d p) f -> p d f", p=128)),
                waits=[wu_w[wpar], ao2_evs_last], inc=f"wu{wpar}")
            bsel = 4 * (ic % 2)
            for t2 in range(2):
                egb = ps[bsel + t2]
                eub = ps[bsel + 2 + t2]
                for d in range(DC):
                    eg_ev = P.pe(lambda e, d=d, t2=t2, wpar=wpar, egb=egb:
                                 e.matmul(
                        egb[0:128, 0:512],
                        wgsl[:, 2048 * wpar + 128 * d:2048 * wpar + 128 * (d + 1)],
                        hbuf[:, 1024 * d + 512 * t2:1024 * d + 512 * (t2 + 1)],
                        start=(d == 0), stop=(d == DC - 1)),
                        waits=[gld, h2b_evs[-1], bank_w[bsel + t2]]
                        if d == 0 else (),
                        inc="pe" if d == DC - 1 else None)
                for d in range(DC):
                    eu_ev = P.pe(lambda e, d=d, t2=t2, wpar=wpar, eub=eub:
                                 e.matmul(
                        eub[0:128, 0:512],
                        wusl[:, 2048 * wpar + 128 * d:2048 * wpar + 128 * (d + 1)],
                        hbuf[:, 1024 * d + 512 * t2:1024 * d + 512 * (t2 + 1)],
                        start=(d == 0), stop=(d == DC - 1)),
                        waits=[uld, bank_w[bsel + 2 + t2], cbc_w[t2]]
                        if (d == 0 and ic == 0) else
                        ([uld, bank_w[bsel + 2 + t2]] if d == 0 else ()),
                        inc="pe" if d == DC - 1 else None)
                aslot = 2 * ic + t2
                sil = P.sc(lambda e, egb=egb: e.activation(
                    silscr[:, :], egb[0:128, 0:512], AF.Silu),
                    waits=[eg_ev, sil_w])
                last_sil = sil
                euc = P.ve(lambda e, eub=eub, t2=t2: e.tensor_tensor(
                    mescr[:, :], eub[0:128, 0:512], cbc[:, 512 * t2:512 * (t2 + 1)],
                    op=ALU.mult), waits=[eu_ev, cbc_w[t2], euc_w])
                act = P.ve(lambda e, aslot=aslot: e.tensor_tensor(
                    aobuf[:, 512 * aslot:512 * (aslot + 1)], mescr[:, :],
                    silscr[:, :], op=ALU.mult),
                    waits=[sil, euc, act_w.get(aslot)])
                sil_w = act
                euc_w = act
                bank_w[bsel + t2] = sil
                bank_w[bsel + 2 + t2] = euc
                act_w[aslot] = act
            wg_w[wpar] = eg_ev
            wu_w[wpar] = eu_ev
        last_f = None
        wd_mm = None
        for dout in range(DC):
            wdp = dout % 2
            dld = P.dma("gpsimd", lambda e, eidx=eidx, dout=dout, wdp=wdp:
                        e.dma_start(
                out=wdsl[:, 1408 * wdp:1408 * (wdp + 1)].rearrange(
                    "p (d f) -> p d f", d=NIC),
                in_=wd_p[eidx, dout].rearrange("(d p) f -> p d f", p=128)),
                waits=[wd_w[wdp], o2_mm], inc=f"wd{wdp}")
            for t2 in range(2):
                bpar = (dout * 2 + t2) % 2
                for ic in range(NIC):
                    wd_mm = P.pe(lambda e, ic=ic, t2=t2, wdp=wdp, bpar=bpar:
                                 e.matmul(
                        ps[6 + bpar][0:128, 0:512],
                        wdsl[:, 1408 * wdp + 128 * ic:1408 * wdp + 128 * (ic + 1)],
                        aobuf[:, 512 * (2 * ic + t2):512 * (2 * ic + t2 + 1)],
                        start=(ic == 0), stop=(ic == NIC - 1)),
                        waits=[dld, act_w.get(2 * ic + t2), wdbk_w[bpar],
                               st.get("ps6_war"), st.get("ps7_war"),
                               bank_w[6], bank_w[7]]
                        if ic == 0 else [act_w.get(2 * ic + t2)],
                        inc="pe" if ic == NIC - 1 else None)
                fadd = P.ve(lambda e, dout=dout, t2=t2, bpar=bpar: e.tensor_tensor(
                    rbuf[:, 1024 * dout + 512 * t2:1024 * dout + 512 * (t2 + 1)],
                    ps[6 + bpar][0:128, 0:512],
                    rbuf[:, 1024 * dout + 512 * t2:1024 * dout + 512 * (t2 + 1)],
                    op=ALU.add), waits=[wd_mm, f_evs[dout]])
                f_evs[dout] = fadd
                wdbk_w[bpar] = fadd
                last_f = fadd
            wd_w[wdp] = wd_mm
        st["ps6_war"] = None
        st["ps7_war"] = None
        bank_w[6] = last_f
        bank_w[7] = last_f
        for aslot in range(2 * NIC):
            act_w[aslot] = last_f

    if DBG:
        for d in range(DC):
            P.dma("sync", lambda e, d=d: e.dma_start(
                out=dbg_out["d_F"][128 * d:128 * (d + 1), :],
                in_=rbuf[:, 1024 * d:1024 * (d + 1)]), waits=[f_evs[d]],
                inc="stg")

    sum2 = ssq2 = None
    sqf_w = None
    for d in range(DC):
        sq = P.ve(lambda e, d=d: e.tensor_mul(
            hscr[:, 1024:2048], rbuf[:, 1024 * d:1024 * (d + 1)],
            rbuf[:, 1024 * d:1024 * (d + 1)]), waits=[f_evs[d], sqf_w])
        for t2 in range(2):
            sum2 = P.pe(lambda e, d=d, t2=t2: e.matmul(
                ps[t2][0:1, 0:512], ones_f[:, 0:1],
                rbuf[:, 1024 * d + 512 * t2:1024 * d + 512 * (t2 + 1)],
                start=(d == 0), stop=(d == DC - 1), skip_group_check=True),
                waits=[f_evs[d], ev_one, last_sil], inc=None)
            ssq2 = P.pe(lambda e, d=d, t2=t2: e.matmul(
                ps[4 + t2][0:1, 0:512], ones_b[:, 0:1],
                hscr[:, 1024 + 512 * t2:1024 + 512 * (t2 + 1)],
                start=(d == 0), stop=(d == DC - 1), skip_group_check=True),
                waits=[sq, last_sil], inc="pe" if t2 == 1 else None)
        sqf_w = ssq2
        sum2 = ssq2
    st["rows0_war"] = None
    st["rows32_war"] = None
    abc2, bbc2 = ln_stats(sum2, ssq2, LN_EPS)
    out_last = None
    for d in range(DC):
        e1 = P.ve(lambda e, d=d: e.tensor_tensor(
            rbuf[:, 1024 * d:1024 * (d + 1)], rbuf[:, 1024 * d:1024 * (d + 1)],
            lnbc[:, 0:1024], op=ALU.mult), waits=[abc2, bbc2, f_evs[d]])
        e2 = P.ve(lambda e, d=d: e.tensor_tensor(
            rbuf[:, 1024 * d:1024 * (d + 1)], rbuf[:, 1024 * d:1024 * (d + 1)],
            lnbc[:, 1024:2048], op=ALU.add), waits=[e1])
        out_last = P.dma("sync", lambda e, d=d: e.dma_start(
            out=out_t[128 * d:128 * (d + 1), :],
            in_=rbuf[:, 1024 * d:1024 * (d + 1)]), waits=[e2], inc="out")
    P.wait("sync", out_last)
    if DBG:
        P.wait("sync", ("stg", P.cnt["stg"]))

    P.check_deadlock()
    with nc.Block() as block:
        def make(eng_name):
            def body(eng):
                for fn in P.ops[eng_name]:
                    fn(eng)
            return body
        block.sync(make("sync"))
        block.gpsimd(make("gpsimd"))
        block.vector(make("vector"))
        block.scalar(make("scalar"))
        block.tensor(make("tensor"))
    return nc


def _prep(inputs):
    f = np.float32
    x = np.asarray(inputs["L0_attn_x"], f).reshape(S, D)
    wq = np.asarray(inputs["L0_attn_wq"], f)
    wk = np.asarray(inputs["L0_attn_wk"], f)
    wv = np.asarray(inputs["L0_attn_wv"], f)
    bq = np.asarray(inputs["L0_attn_bq"], f)
    bk = np.asarray(inputs["L0_attn_bk"], f)
    bv = np.asarray(inputs["L0_attn_bv"], f)
    wo = np.asarray(inputs["L0_attn_wo"], f)
    bo = np.asarray(inputs["L0_attn_bo"], f)
    anw = np.asarray(inputs["L0_ffn_attn_norm_w"], f)
    qw2 = np.asarray(inputs["L0_ffn_q_w"], f)
    kw2 = np.asarray(inputs["L0_ffn_k_w"], f)
    vw2 = np.asarray(inputs["L0_ffn_v_w"], f)
    ow2 = np.asarray(inputs["L0_ffn_o_w"], f)
    qnw = np.asarray(inputs["L0_ffn_q_norm_w"], f)
    knw = np.asarray(inputs["L0_ffn_k_norm_w"], f)
    fnw = np.asarray(inputs["L0_ffn_ffn_norm_w"], f)
    gw = np.asarray(inputs["L0_ffn_gate_w"], f)
    wg = np.asarray(inputs["L0_ffn_wg"], f)
    wu = np.asarray(inputs["L0_ffn_wu"], f)
    wd = np.asarray(inputs["L0_ffn_wd"], f)

    def headslabs(w, nh):
        return np.ascontiguousarray(
            w.T.reshape(DC, 128, nh, HD).transpose(2, 1, 0, 3)
            .reshape(nh, 128, DC * HD))

    wq_s = headslabs(wq, NH)
    wk_s = headslabs(wk, NH)
    wv_s = headslabs(wv, NH)
    q2_s = headslabs(qw2, NH)
    k2_s = headslabs(kw2, KVH)
    v2_s = headslabs(vw2, KVH)
    woT = np.ascontiguousarray(wo.T)
    o2T = np.ascontiguousarray(ow2.T)
    gw_t = np.ascontiguousarray(gw.T)
    wg_pp = np.ascontiguousarray(
        np.stack([wg[e].T.reshape(D, NIC, HD).transpose(1, 0, 2) for e in range(NE)]))
    wu_pp = np.ascontiguousarray(
        np.stack([wu[e].T.reshape(D, NIC, HD).transpose(1, 0, 2) for e in range(NE)]))
    wd_pp = np.ascontiguousarray(
        np.stack([wd[e].T.reshape(NI, DC, HD).transpose(1, 0, 2) for e in range(NE)]))
    bo_t = np.ascontiguousarray(bo.reshape(DC, HD).T)
    anw_t = np.ascontiguousarray(anw.reshape(DC, HD).T)
    fnw_t = np.ascontiguousarray(fnw.reshape(DC, HD).T)
    qknw = np.ascontiguousarray(np.stack([qnw, knw], axis=1))

    in_maps = []
    for i in range(NCORE):
        h0, h1 = 2 * i, 2 * i + 1
        kv = i // 2
        b3 = np.zeros((HD, 8), f)
        b3[:, 0] = bq[HD * h0:HD * h0 + HD]
        b3[:, 1] = bq[HD * h1:HD * h1 + HD]
        b3[:, 2] = bk[HD * h0:HD * h0 + HD]
        b3[:, 3] = bk[HD * h1:HD * h1 + HD]
        b3[:, 4] = bv[HD * h0:HD * h0 + HD]
        b3[:, 5] = bv[HD * h1:HD * h1 + HD]
        wo_my = np.ascontiguousarray(
            woT[256 * i:256 * (i + 1)].reshape(2, HD, DC, HD).transpose(0, 2, 1, 3))
        o2_myv = np.ascontiguousarray(
            o2T[256 * i:256 * (i + 1)].reshape(2, HD, DC, HD).transpose(0, 2, 1, 3))
        in_maps.append({
            "x_sh": np.ascontiguousarray(
                x[T * i:T * (i + 1), :].reshape(2, 512, DC, 128)
                .transpose(3, 0, 2, 1).reshape(128, 2 * DC * 512)),
            "wq_my": np.ascontiguousarray(wq_s[[h0, h1]]),
            "wk_my": np.ascontiguousarray(wk_s[[h0, h1]]),
            "wv_my": np.ascontiguousarray(wv_s[[h0, h1]]),
            "b3_my": b3,
            "wo_my": wo_my,
            "bo_t": bo_t,
            "q2_my": np.ascontiguousarray(q2_s[[h0, h1]]),
            "k2_my": np.ascontiguousarray(k2_s[[kv]]),
            "v2_my": np.ascontiguousarray(v2_s[[kv]]),
            "o2_my": o2_myv,
            "qknw": qknw,
            "anw_t": anw_t,
            "fnw_t": fnw_t,
            "gw_t": gw_t,
            "wg_p": wg_pp,
            "wu_p": wu_pp,
            "wd_p": wd_pp,
        })
    return in_maps


def kernel(**inputs):
    in_maps = _prep(inputs)
    nc = _build()
    res = run_bass_kernel_spmd(nc, in_maps, core_ids=list(range(NCORE)))
    out = np.empty((1, S, D), np.float32)
    for i in range(NCORE):
        out[0, T * i:T * (i + 1), :] = res.results[i]["out_t"].T
    if DBG:
        kernel._dbg = res.results
    return out
